# version 03
# baseline (speedup 1.0000x reference)
import numpy as np
import ml_dtypes
from contextlib import ExitStack

import concourse.bass as bass
import concourse.mybir as mybir
from concourse.bass_utils import run_bass_kernel_spmd

F32 = mybir.dt.float32
BF16 = mybir.dt.bfloat16
U8 = mybir.dt.uint8
AF = mybir.ActivationFunctionType
ALU = mybir.AluOpType
AX = mybir.AxisListType

S = 4096
D = 1024
NT = S // 128
EPS = 1e-5

BF = ml_dtypes.bfloat16



class Ctr:

    def __init__(self, nc, ctx, name):
        self.sem = ctx.enter_context(nc.semaphore(name))
        self.val = 0

    def inc(self, inst, n=1):
        inst.then_inc(self.sem, n)
        self.val += n
        return self.val


class B:

    def __init__(self, nc, ctx, block):
        self.nc = nc
        self.ctx = ctx
        self.block = block
        self.c = {}
        self._last_wait = {}

    def ctr(self, name):
        if name not in self.c:
            self.c[name] = Ctr(self.nc, self.ctx, name)
        return self.c[name]

    def on(self, engine, fn):
        getattr(self.block, engine)(lambda eng: fn(W(self, engine, eng)))

    def barrier(self):
        cs = list(self.c.values())
        for eng in ("sync", "gpsimd", "tensor", "vector", "scalar"):
            def f(w, cs=cs):
                for c in cs:
                    w.wait(c)
            self.on(eng, f)


class W:

    def __init__(self, b, name, eng):
        self.b = b
        self.name = name
        self.eng = eng

    def wait(self, ctr, val=None):
        v = ctr.val if val is None else val
        if v <= 0:
            return
        key = (self.name, id(ctr))
        if self.b._last_wait.get(key, -1) >= v:
            return
        self.eng.wait_ge(ctr.sem, v)
        self.b._last_wait[key] = v

    def __getattr__(self, item):
        return getattr(self.eng, item)



def _dram(nc, names, key, shape, dt, kind):
    v = names[key]
    if isinstance(v, str):
        return nc.dram_tensor(v, shape, dt, kind=kind).ap()
    return v


def build_ffn1(nc, names, env=None, px='f1_'):
    x3 = _dram(nc, names, "x", [S, D], F32, "ExternalInput")
    k0 = _dram(nc, names, "k0", [2 * D, 4096], U8, "ExternalInput")
    k1 = _dram(nc, names, "k1", [4096, 4096], U8, "ExternalInput")
    k2 = _dram(nc, names, "k2", [4096, 1024], U8, "ExternalInput")
    s0 = _dram(nc, names, "s0", [1, 1], F32, "ExternalInput")
    s1 = _dram(nc, names, "s1", [1, 1], F32, "ExternalInput")
    s2 = _dram(nc, names, "s2", [1, 1], F32, "ExternalInput")
    out = _dram(nc, names, "out", [S, D], F32, "ExternalOutput")
    w0p = nc.dram_tensor("w0p_scratch", [8, 128, 4096], BF16, kind="Internal")
    dbg = None
    if names.get("dbg"):
        dbg = {k: nc.dram_tensor("dbg_" + k, shp, dt, kind="ExternalOutput").ap()
               for k, shp, dt in [("ln3T", [128, 8 * 512], BF16), ("r1", [128, 32 * 512], BF16),
                                  ("r2", [128, 32 * 512], BF16), ("cs1", [1, 512], F32),
                                  ("cs2", [1, 512], F32), ("w0p", [8, 128, 4096], BF16)]}

    ident_np = np.eye(128, dtype=BF)

    with ExitStack() as ctx:
        e = ctx.enter_context
        ident = e(nc.sbuf_tensor(f"{px}ident", [128, 128], BF16))
        onesf = e(nc.sbuf_tensor(f"{px}onesf", [1, 128], F32))
        onecol_bf = e(nc.sbuf_tensor(f"{px}onecol_bf", [128, 1], BF16))
        nhcol_bf = e(nc.sbuf_tensor(f"{px}nhcol_bf", [1, 128], F32))
        nhrow_bf = e(nc.sbuf_tensor(f"{px}nhrow_bf", [1, 1024], F32))
        svals = e(nc.sbuf_tensor(f"{px}svals", [1, 3], F32))
        vec2s0 = e(nc.sbuf_tensor(f"{px}vec2s0", [128, 1], F32))
        vec4s1 = e(nc.sbuf_tensor(f"{px}vec4s1", [128, 1], F32))
        vec4s2 = e(nc.sbuf_tensor(f"{px}vec4s2", [128, 1], F32))
        w0a = e(nc.sbuf_tensor(f"{px}w0a", [128, 4096], BF16))
        w0b = e(nc.sbuf_tensor(f"{px}w0b", [128, 4096], BF16))
        w0d = e(nc.sbuf_tensor(f"{px}w0d", [128, 4096], BF16))
        x3c = e(nc.sbuf_tensor(f"{px}x3c", [128, 4 * 1024], F32))
        stat = e(nc.sbuf_tensor(f"{px}stat", [128, 12], F32))
        mv = e(nc.sbuf_tensor(f"{px}mv", [128, 2], F32))
        rstd = e(nc.sbuf_tensor(f"{px}rstd", [128, 1], F32))
        lnt = e(nc.sbuf_tensor(f"{px}lnt", [128, 1024], BF16))
        ln3T = e(nc.sbuf_tensor(f"{px}ln3T", [128, 8 * 512], BF16))
        r1 = e(nc.sbuf_tensor(f"{px}r1", [128, 32 * 512], BF16))
        r2 = e(nc.sbuf_tensor(f"{px}r2", [128, 32 * 512], BF16))
        wblk = [e(nc.sbuf_tensor(f"{px}wblk{i}", [128, 1024], BF16)) for i in range(2)]
        csum1 = e(nc.sbuf_tensor(f"{px}csum1", [1, 512], F32))
        csum2 = e(nc.sbuf_tensor(f"{px}csum2", [1, 512], F32))
        outb = [e(nc.sbuf_tensor(f"{px}outb{i}", [128, 1024], F32)) for i in range(2)]

        if env is None:
            block = e(nc.Block())
            b = B(nc, ctx, block)
        else:
            block = env["block"]
            b = env["b"]
        sl = b.ctr("sl")
        sg = b.ctr("sg")
        so = b.ctr("so")
        pe = b.ctr("pe")
        dv = b.ctr("dv")
        sc = b.ctr("sc")

        identh = nc.inline_tensor(ident_np, f"{px}identnp")

        def p0_sync(s):
            sl.inc(s.dma_start(out=ident[:, :], in_=identh.ap()), 16)
            sl.inc(s.dma_start(out=svals[0:1, 0:1], in_=s0), 16)
            sl.inc(s.dma_start(out=svals[0:1, 1:2], in_=s1), 16)
            sl.inc(s.dma_start(out=svals[0:1, 2:3], in_=s2), 16)
        b.on("sync", p0_sync)

        def p0_vec(v):
            v.memset(onesf[:, :], 1.0)
            v.memset(onecol_bf[:, :], 1.0)
            v.memset(nhcol_bf[:, :], -0.5)
            dv.inc(v.memset(nhrow_bf[:, :], -0.5))
        b.on("vector", p0_vec)

        with nc.psum_tensor(f"{px}pb0", [128, 512], F32) as pb0:
            for i, (src, dst, m) in enumerate([(0, vec2s0, 2.0), (1, vec4s1, 4.0), (2, vec4s2, 4.0)]):
                def t_body(t, i=i, src=src):
                    t.wait(sl)
                    t.wait(dv)
                    pe.inc(t.matmul(pb0[:, 0:1], onesf[0:1, :], svals[0:1, src:src + 1],
                                    start=True, stop=True))
                b.on("tensor", t_body)

                def v_body(v, dst=dst, m=m):
                    v.wait(pe)
                    dv.inc(v.tensor_scalar_mul(dst[:, 0:1], pb0[:, 0:1], float(m)))
                b.on("vector", v_body)

        for it in range(8):
            def g_body(g, it=it):
                g.wait(so)
                sg.inc(g.dma_start(out=w0a[:, :], in_=k0[it * 128:(it + 1) * 128, :]), 16)
                sg.inc(g.dma_start(out=w0b[:, :], in_=k0[1024 + it * 128:1024 + (it + 1) * 128, :]), 16)
            b.on("gpsimd", g_body)

            def v_body(v):
                v.wait(sg)
                v.wait(so)
                dv.inc(v.tensor_sub(w0d[:, :], w0a[:, :], w0b[:, :]))
            b.on("vector", v_body)

            def s_body(s, it=it):
                s.wait(dv)
                so.inc(s.dma_start(out=w0p.ap()[it, :, :], in_=w0d[:, :]), 16)
            b.on("sync", s_body)

        for c in range(8):
            for st in range(4):
                gt = c * 4 + st

                def s_body(s, st=st, gt=gt):
                    s.wait(dv)
                    sl.inc(s.dma_start(out=x3c[:, st * 1024:(st + 1) * 1024],
                                       in_=x3[gt * 128:(gt + 1) * 128, :]), 16)
                b.on("sync", s_body)

                def v_body(v, st=st):
                    v.wait(sl)
                    v.wait(dv)
                    v.bn_stats(stat[:, 0:6], x3c[:, st * 1024:st * 1024 + 512])
                    dv.inc(v.bn_stats(stat[:, 6:12], x3c[:, st * 1024 + 512:st * 1024 + 1024]))
                    v.wait(dv)
                    dv.inc(v.bn_aggr(mv[:, :], stat[:, :]))
                    v.wait(dv)
                    dv.inc(v.tensor_scalar_add(rstd[:, 0:1], mv[:, 1:2], EPS))
                b.on("vector", v_body)

                def s2_body(s):
                    s.wait(dv)
                    sc.inc(s.activation(rstd[:, 0:1], rstd[:, 0:1], AF.Sqrt))
                b.on("scalar", s2_body)

                def v2_body(v, st=st):
                    v.wait(sc)
                    dv.inc(v.reciprocal(rstd[:, 0:1], rstd[:, 0:1]))
                    v.wait(dv)
                    dv.inc(v.tensor_scalar(lnt[:, :], x3c[:, st * 1024:(st + 1) * 1024],
                                           mv[:, 0:1], rstd[:, 0:1], ALU.subtract, ALU.mult))
                b.on("vector", v2_body)

                with nc.psum_tensor(f"{px}ptr_{c}_{st}", [128, 1024], BF16) as ptr:
                    def t_body(t, st=st):
                        t.wait(dv)
                        t.wait(sl)
                        for it in range(8):
                            i = pe.inc(t.transpose(ptr[:, it * 128:(it + 1) * 128],
                                                   lnt[:, it * 128:(it + 1) * 128], ident[:, :]))
                    b.on("tensor", t_body)

                    def v3_body(v, st=st):
                        v.wait(pe)
                        for it in range(8):
                            i = v.tensor_copy(ln3T[:, it * 512 + st * 128:it * 512 + (st + 1) * 128],
                                              ptr[:, it * 128:(it + 1) * 128])
                        dv.inc(i)
                    b.on("vector", v3_body)

            with ExitStack() as sctx:
                pg = [sctx.enter_context(nc.psum_tensor(f"{px}pg_{c}_{j}", [128, 512], F32))
                      for j in range(8)]
                for mg in range(4):
                    for it in range(8):
                        buf = wblk[it % 2]

                        def g_body(g, mg=mg, it=it, buf=buf):
                            if mg + it > 0:
                                g.wait(pe)
                            sg.inc(g.dma_start(out=buf[:, :],
                                               in_=w0p.ap()[it, :, mg * 1024:(mg + 1) * 1024]), 16)
                        b.on("gpsimd", g_body)

                        def t_body(t, mg=mg, it=it, buf=buf):
                            t.wait(sg)
                            if it == 0:
                                t.wait(sc)
                            if mg == 0 and it == 0:
                                t.wait(dv)
                            for j in range(8):
                                pe.inc(t.matmul(pg[j][:, :], buf[:, j * 128:(j + 1) * 128],
                                                ln3T[:, it * 512:(it + 1) * 512],
                                                start=(it == 0), stop=(it == 7)))
                        b.on("tensor", t_body)

                    def s_body(s, mg=mg):
                        s.wait(pe)
                        s.wait(dv)
                        for j in range(8):
                            sc.inc(s.activation(r1[:, (mg * 8 + j) * 512:(mg * 8 + j + 1) * 512],
                                                pg[j][:, :], AF.Relu, scale=vec2s0[:, 0:1]))
                    b.on("scalar", s_body)

            if dbg is not None and c == 0:
                def dmp1(s):
                    s.wait(sc)
                    s.wait(dv)
                    so.inc(s.dma_start(out=dbg["ln3T"], in_=ln3T[:, :]), 16)
                    so.inc(s.dma_start(out=dbg["r1"], in_=r1[:, :]), 16)
                b.on("sync", dmp1)

            with ExitStack() as sctx:
                with nc.psum_tensor(f"{px}pcs_{c}", [1, 512], F32) as pcs:
                    def t_body(t, pcs=pcs):
                        t.wait(sc)
                        for kb in range(32):
                            pe.inc(t.matmul(pcs[:, :], onecol_bf[:, 0:1], r1[:, kb * 512:(kb + 1) * 512],
                                            start=(kb == 0), stop=(kb == 31)))
                    b.on("tensor", t_body)

                    def v_body(v, pcs=pcs):
                        v.wait(pe)
                        dv.inc(v.tensor_copy(csum1[0:1, :], pcs[0:1, :]))
                    b.on("vector", v_body)

                pg = [sctx.enter_context(nc.psum_tensor(f"{px}ph_{c}_{j}", [128, 512], F32))
                      for j in range(8)]
                for mg in range(4):
                    for kb in range(32):
                        buf = wblk[kb % 2]

                        def g_body(g, mg=mg, kb=kb, buf=buf):
                            g.wait(pe)
                            sg.inc(g.dma_start(out=buf[:, :],
                                               in_=k1[kb * 128:(kb + 1) * 128, mg * 1024:(mg + 1) * 1024]), 16)
                        b.on("gpsimd", g_body)

                        def t_body(t, mg=mg, kb=kb, buf=buf):
                            t.wait(sg)
                            if kb == 0:
                                t.wait(sc)
                                t.wait(dv)
                            for j in range(8):
                                pe.inc(t.matmul(pg[j][:, :], buf[:, j * 128:(j + 1) * 128],
                                                r1[:, kb * 512:(kb + 1) * 512],
                                                start=(kb == 0), stop=False))
                        b.on("tensor", t_body)

                    def t2_body(t, mg=mg):
                        for j in range(8):
                            pe.inc(t.matmul(pg[j][:, :], nhcol_bf[0:1, :], csum1[0:1, :],
                                            start=False, stop=True))
                    b.on("tensor", t2_body)

                    def s_body(s, mg=mg):
                        s.wait(pe)
                        for j in range(8):
                            sc.inc(s.activation(r2[:, (mg * 8 + j) * 512:(mg * 8 + j + 1) * 512],
                                                pg[j][:, :], AF.Relu, scale=vec4s1[:, 0:1]))
                    b.on("scalar", s_body)

            if dbg is not None and c == 0:
                def dmp2(s):
                    s.wait(sc)
                    s.wait(dv)
                    so.inc(s.dma_start(out=dbg["r2"], in_=r2[:, :]), 16)
                    so.inc(s.dma_start(out=dbg["cs1"], in_=csum1[0:1, :]), 16)
                b.on("sync", dmp2)

            with ExitStack() as sctx:
                with nc.psum_tensor(f"{px}pcs2_{c}", [1, 512], F32) as pcs2:
                    def t_body(t, pcs2=pcs2):
                        t.wait(sc)
                        for tt in range(32):
                            pe.inc(t.matmul(pcs2[:, :], onecol_bf[:, 0:1], r2[:, tt * 512:(tt + 1) * 512],
                                            start=(tt == 0), stop=(tt == 31)))
                    b.on("tensor", t_body)

                    def v_body(v, pcs2=pcs2):
                        v.wait(pe)
                        dv.inc(v.tensor_copy(csum2[0:1, :], pcs2[0:1, :]))
                    b.on("vector", v_body)

                if dbg is not None and c == 0:
                    def dmp3(s):
                        s.wait(dv)
                        so.inc(s.dma_start(out=dbg["cs2"], in_=csum2[0:1, :]), 16)
                    b.on("sync", dmp3)

                ps2 = [sctx.enter_context(nc.psum_tensor(f"{px}ps2_{c}_{st}", [128, 1024], F32))
                       for st in range(4)]
                for tt in range(32):
                    buf = wblk[tt % 2]

                    def g_body(g, tt=tt, buf=buf):
                        g.wait(pe)
                        sg.inc(g.dma_start(out=buf[:, :], in_=k2[tt * 128:(tt + 1) * 128, :]), 16)
                    b.on("gpsimd", g_body)

                    def t2_body(t, tt=tt, buf=buf):
                        t.wait(sg)
                        if tt == 0:
                            t.wait(dv)
                        for st in range(4):
                            for n in range(2):
                                pe.inc(t.matmul(ps2[st][:, n * 512:(n + 1) * 512],
                                                r2[:, tt * 512 + st * 128:tt * 512 + (st + 1) * 128],
                                                buf[:, n * 512:(n + 1) * 512],
                                                start=(tt == 0), stop=False))
                    b.on("tensor", t2_body)

                def t3_body(t):
                    for st in range(4):
                        for n in range(2):
                            pe.inc(t.matmul(ps2[st][:, n * 512:(n + 1) * 512],
                                            csum2[0:1, st * 128:(st + 1) * 128],
                                            nhrow_bf[0:1, n * 512:(n + 1) * 512],
                                            start=False, stop=True))
                b.on("tensor", t3_body)

                for st in range(4):
                    ob = outb[st % 2]
                    sob = b.ctr(f"sob{st % 2}")

                    def v2_body(v, st=st, ob=ob, sob=sob):
                        v.wait(pe)
                        v.wait(sob)
                        for n in range(2):
                            dv.inc(v.scalar_tensor_tensor(ob[:, n * 512:(n + 1) * 512],
                                                          ps2[st][:, n * 512:(n + 1) * 512],
                                                          vec4s2[:, 0:1],
                                                          x3c[:, st * 1024 + n * 512:st * 1024 + (n + 1) * 512],
                                                          ALU.mult, ALU.add))
                    b.on("vector", v2_body)

                    def s_body(s, st=st, ob=ob, c=c, sob=sob):
                        s.wait(dv)
                        gt = c * 4 + st
                        sob.inc(s.dma_start(out=out[gt * 128:(gt + 1) * 128, :], in_=ob[:, :]), 16)
                    b.on("sync", s_body)

        if dbg is not None:
            def dump(s):
                s.wait(so)
                so.inc(s.dma_start(out=dbg["w0p"], in_=w0p.ap()), 16)
            b.on("sync", dump)

        def fin(s):
            s.wait(so)
            for name in ("sob0", "sob1"):
                if name in b.c:
                    s.wait(b.c[name])
        b.on("sync", fin)
    return nc



def build_attn0(nc, names, env=None, px='a0_'):
    x0 = _dram(nc, names, "x", [S, D], F32, "ExternalInput")
    Wd = {w: _dram(nc, names, w, [D, D], F32, "ExternalInput")
          for w in ("wq", "wk", "wv", "wo")}
    out = _dram(nc, names, "out", [S, D], F32, "ExternalOutput")
    ln0h = nc.dram_tensor("ln0T_scratch", [8, 128, S], BF16, kind="Internal")
    yph = nc.dram_tensor("yp_scratch", [8, 128, S], F32, kind="Internal")
    dbg = None
    if names.get("dbg"):
        dbg = {k: nc.dram_tensor("dbg_" + k, shp, dt, kind="ExternalOutput").ap()
               for k, shp, dt in [("KTg", [128, 4 * 4096], BF16), ("Vg", [128, 32 * 520], BF16),
                                  ("qcb", [128, 4 * 512], BF16), ("attnoT", [128, 4 * 512], BF16),
                                  ("ab0", [128, 512], BF16), ("pv0", [65, 512], F32),
                                  ("WTwk", [128, 8192], BF16), ("ln0", [8, 128, 4096], BF16)]}

    masks_np = np.zeros((128, 4 * 512), dtype=BF)
    for p in range(4):
        for kl in range(128):
            q0 = kl + 128 * p
            if q0 < 512:
                masks_np[kl, p * 512 + q0:(p + 1) * 512] = 1.0

    with ExitStack() as ctx:
        e = ctx.enter_context
        ident = e(nc.sbuf_tensor(f"{px}ident", [128, 128], BF16))
        identf = e(nc.sbuf_tensor(f"{px}identf", [128, 128], F32))
        onesf = e(nc.sbuf_tensor(f"{px}onesf", [128, 64], F32))
        masks = e(nc.sbuf_tensor(f"{px}masks", [128, 4 * 512], BF16))
        WT = {w: e(nc.sbuf_tensor(f"{px}WT{w}", [128, 8192], BF16)) for w in ("wq", "wk", "wv", "wo")}
        x0t = [e(nc.sbuf_tensor(f"{px}x0t{i}", [128, 1024], F32)) for i in range(2)]
        stat = e(nc.sbuf_tensor(f"{px}stat", [128, 12], F32))
        mv = e(nc.sbuf_tensor(f"{px}mv", [128, 2], F32))
        rstd = e(nc.sbuf_tensor(f"{px}rstd", [128, 1], F32))
        lnt = e(nc.sbuf_tensor(f"{px}lnt", [128, 1024], BF16))

        if env is None:
            block = e(nc.Block())
            b = B(nc, ctx, block)
        else:
            block = env["block"]
            b = env["b"]
        sl = b.ctr("sl"); sg = b.ctr("sg"); pe = b.ctr("pe")
        dv = b.ctr("dv"); sc = b.ctr("sc"); sp = b.ctr("sp"); sy = b.ctr("sy")
        sob = [b.ctr("sob0"), b.ctr("sob1")]

        identh = nc.inline_tensor(np.eye(128, dtype=BF), f"{px}identnp")
        identfh = nc.inline_tensor(np.eye(128, dtype=np.float32), f"{px}identfnp")
        masksh = nc.inline_tensor(masks_np, f"{px}masksnp")

        def a0_sync(s):
            sl.inc(s.dma_start(out=ident[:, :], in_=identh.ap()), 16)
            sl.inc(s.dma_start(out=identf[:, :], in_=identfh.ap()), 16)
            sl.inc(s.dma_start(out=masks[:, :], in_=masksh.ap()), 16)
        b.on("sync", a0_sync)

        def a0_vec(v):
            dv.inc(v.memset(onesf[:, :], 1.0))
        b.on("vector", a0_vec)

        with ExitStack() as actx:
            wstage = actx.enter_context(nc.sbuf_tensor(f"{px}wstage", [128, 8192], BF16))
            ln0T = actx.enter_context(nc.sbuf_tensor(f"{px}ln0T", [128, 8 * 4096], BF16))
            for wi, w in enumerate(("wq", "wk", "wv", "wo")):
                def g_body(g, w=w, wi=wi):
                    if wi > 0:
                        g.wait(pe)
                    sg.inc(g.dma_start(out=wstage[:, :].rearrange("p (t i) -> p t i", t=8),
                                       in_=Wd[w].rearrange("(t p) i -> p t i", p=128)), 16)
                b.on("gpsimd", g_body)

                with nc.psum_tensor(f"{px}ptw_{w}", [128, 1024], BF16) as ptw:
                    for u in range(8):
                        def t_body(t, w=w, u=u):
                            t.wait(sg)
                            t.wait(sl)
                            if u > 0:
                                t.wait(dv)
                            for tt in range(8):
                                pe.inc(t.transpose(ptw[:, tt * 128:(tt + 1) * 128],
                                                   wstage[:, tt * 1024 + u * 128:tt * 1024 + (u + 1) * 128],
                                                   ident[:, :]))
                        b.on("tensor", t_body)

                        def v_body(v, w=w, u=u):
                            v.wait(pe)
                            dv.inc(v.tensor_copy(WT[w][:, u * 1024:(u + 1) * 1024], ptw[:, :]))
                        b.on("vector", v_body)

            for st in range(NT):
                xb = x0t[st % 2]

                def s_body(s, st=st, xb=xb):
                    s.wait(dv)
                    sl.inc(s.dma_start(out=xb[:, :], in_=x0[st * 128:(st + 1) * 128, :]), 16)
                b.on("sync", s_body)

                def v_body(v, st=st, xb=xb):
                    v.wait(sl)
                    v.wait(dv)
                    v.bn_stats(stat[:, 0:6], xb[:, 0:512])
                    dv.inc(v.bn_stats(stat[:, 6:12], xb[:, 512:1024]))
                    v.wait(dv)
                    dv.inc(v.bn_aggr(mv[:, :], stat[:, :]))
                    v.wait(dv)
                    dv.inc(v.tensor_scalar_add(rstd[:, 0:1], mv[:, 1:2], EPS))
                b.on("vector", v_body)

                def s2_body(s):
                    s.wait(dv)
                    sc.inc(s.activation(rstd[:, 0:1], rstd[:, 0:1], AF.Sqrt))
                b.on("scalar", s2_body)

                def v2_body(v, st=st, xb=xb):
                    v.wait(sc)
                    dv.inc(v.reciprocal(rstd[:, 0:1], rstd[:, 0:1]))
                    v.wait(dv)
                    dv.inc(v.tensor_scalar(lnt[:, :], xb[:, :], mv[:, 0:1], rstd[:, 0:1],
                                           ALU.subtract, ALU.mult))
                b.on("vector", v2_body)

                with nc.psum_tensor(f"{px}plt_{st}", [128, 1024], BF16) as plt:
                    def t_body(t, st=st):
                        t.wait(dv)
                        for it in range(8):
                            pe.inc(t.transpose(plt[:, it * 128:(it + 1) * 128],
                                               lnt[:, it * 128:(it + 1) * 128], ident[:, :]))
                    b.on("tensor", t_body)

                    def v3_body(v, st=st):
                        v.wait(pe)
                        for it in range(8):
                            i = v.tensor_copy(ln0T[:, it * 4096 + st * 128:it * 4096 + (st + 1) * 128],
                                              plt[:, it * 128:(it + 1) * 128])
                        dv.inc(i)
                    b.on("vector", v3_body)

            def spill(s):
                s.wait(dv)
                for it in range(8):
                    sp.inc(s.dma_start(out=ln0h.ap()[it, :, :], in_=ln0T[:, it * 4096:(it + 1) * 4096]), 16)
            b.on("sync", spill)

        KTg = e(nc.sbuf_tensor(f"{px}KTg", [128, 4 * 4096], BF16))
        Vg = e(nc.sbuf_tensor(f"{px}Vg", [128, 32 * 520], BF16))
        lncb = [e(nc.sbuf_tensor(f"{px}lncb{i}", [128, 8 * 512], BF16)) for i in range(2)]
        qcb = e(nc.sbuf_tensor(f"{px}qcb", [128, 4 * 512], BF16))
        ab = [e(nc.sbuf_tensor(f"{px}ab{i}", [128, 512], BF16)) for i in range(4)]
        brs = e(nc.sbuf_tensor(f"{px}brs", [128, 512], F32))
        brd = e(nc.sbuf_tensor(f"{px}brd", [64, 512], F32))
        attnoT = e(nc.sbuf_tensor(f"{px}attnoT", [128, 4 * 512], BF16))
        ybuf = [e(nc.sbuf_tensor(f"{px}ybuf{i}", [128, 512], F32)) for i in range(2)]
        y0b = [e(nc.sbuf_tensor(f"{px}y0b{i}", [128, 512], F32)) for i in range(2)]

        def ln_chunk_load(s, buf, n, extra=None):
            sl.inc(s.dma_start(out=buf[:, :].rearrange("p (it w) -> p it w", it=8),
                               in_=ln0h.ap()[:, :, n * 512:(n + 1) * 512].rearrange("it p w -> p it w")), 16)

        for gp in range(2):
            def v_init(v, gp=gp):
                v.wait(pe)
                v.wait(sp)
                dv.inc(v.memset(Vg[:, :], 1.0))
            b.on("vector", v_init)

            with ExitStack() as sctx:
                pkv = [sctx.enter_context(nc.psum_tensor(f"{px}pkv_{gp}_{i}", [128, 512], F32)) for i in range(2)]
                for n in range(8):
                    buf = lncb[n % 2]

                    def s_body(s, gp=gp, n=n, buf=buf):
                        s.wait(pe)
                        if gp == 0 and n == 0:
                            s.wait(sp)
                        ln_chunk_load(s, buf, n)
                    b.on("sync", s_body)

                    for j in range(4):
                        def t_body(t, gp=gp, n=n, j=j, buf=buf):
                            t.wait(sl)
                            t.wait(sc)
                            if n == 0 and j == 0:
                                t.wait(dv)
                            for it in range(8):
                                i = t.matmul(pkv[j % 2][:, :],
                                             WT["wk"][:, it * 1024 + gp * 512 + j * 128:it * 1024 + gp * 512 + (j + 1) * 128],
                                             buf[:, it * 512:(it + 1) * 512],
                                             start=(it == 0), stop=(it == 7))
                            pe.inc(i)
                        b.on("tensor", t_body)

                        def sc_body(s, gp=gp, n=n, j=j):
                            s.wait(pe)
                            sc.inc(s.activation(KTg[:, j * 4096 + n * 512:j * 4096 + (n + 1) * 512],
                                                pkv[j % 2][:, :], AF.Identity))
                        b.on("scalar", sc_body)

                    for m4 in range(4):
                        stile = n * 4 + m4

                        def t_body(t, gp=gp, n=n, m4=m4, buf=buf, stile=stile):
                            t.wait(sc)
                            for it in range(8):
                                i = t.matmul(pkv[m4 % 2][:, :],
                                             buf[:, it * 512 + m4 * 128:it * 512 + (m4 + 1) * 128],
                                             WT["wv"][:, it * 1024 + gp * 512:it * 1024 + (gp + 1) * 512],
                                             start=(it == 0), stop=(it == 7))
                            pe.inc(i)
                        b.on("tensor", t_body)

                        def v_body(v, gp=gp, stile=stile, m4=m4):
                            v.wait(pe)
                            dst = bass.AP(Vg, stile * 520, [[32 * 520, 128], [65, 8], [1, 64]])
                            src = bass.AP(pkv[m4 % 2], 0, [[512, 128], [64, 8], [1, 64]])
                            dv.inc(v.tensor_copy(dst, src))
                        b.on("vector", v_body)

            if dbg is not None and gp == 0:
                sdbg = b.ctr("sdbg")

                def d_kv(s):
                    s.wait(dv)
                    s.wait(sc)
                    sdbg.inc(s.dma_start(out=dbg["KTg"], in_=KTg[:, :]), 16)
                    sdbg.inc(s.dma_start(out=dbg["Vg"], in_=Vg[:, :]), 16)
                    sdbg.inc(s.dma_start(out=dbg["WTwk"], in_=WT["wk"][:, :]), 16)
                    sdbg.inc(s.dma_start(out=dbg["ln0"], in_=ln0h.ap()), 16)
                b.on("sync", d_kv)

                def d_bar(v):
                    v.wait(b.c["sdbg"])
                    dv.inc(v.memset(brs[0:1, 0:1], 0.0))
                b.on("vector", d_bar)

            for qc in range(8):
                buf = lncb[qc % 2]

                def s_body(s, gp=gp, qc=qc, buf=buf):
                    s.wait(pe)
                    ln_chunk_load(s, buf, qc)
                b.on("sync", s_body)

                with ExitStack() as sctx:
                    pq = sctx.enter_context(nc.psum_tensor(f"{px}pq_{gp}_{qc}", [128, 512], F32))
                    psab = [sctx.enter_context(nc.psum_tensor(f"{px}ps_{gp}_{qc}_{i}", [128, 512], F32))
                            for i in range(4)]
                    pv = [sctx.enter_context(nc.psum_tensor(f"{px}pv_{gp}_{qc}_{i}", [65, 512], F32))
                          for i in range(2)]
                    pbr = sctx.enter_context(nc.psum_tensor(f"{px}pbr_{gp}_{qc}", [64, 512], F32))

                    for j in range(4):
                        def t_q(t, gp=gp, qc=qc, j=j, buf=buf):
                            t.wait(sl)
                            t.wait(sc)
                            if j == 0:
                                t.wait(dv)
                            for it in range(8):
                                i = t.matmul(pq[:, :],
                                             WT["wq"][:, it * 1024 + gp * 512 + j * 128:it * 1024 + gp * 512 + (j + 1) * 128],
                                             buf[:, it * 512:(it + 1) * 512],
                                             start=(it == 0), stop=(it == 7))
                            pe.inc(i)
                        b.on("tensor", t_q)

                        def sc_q(s, gp=gp, qc=qc, j=j):
                            s.wait(pe)
                            sc.inc(s.activation(qcb[:, j * 512:(j + 1) * 512], pq[:, :], AF.Identity))
                        b.on("scalar", sc_q)

                    nkb = 4 * (qc + 1)
                    for j in range(4):
                        for kb in range(nkb):
                            pA = psab[2 * (kb % 2)]
                            pB = psab[2 * (kb % 2) + 1]
                            aA = ab[2 * (kb % 2)]
                            aB = ab[2 * (kb % 2) + 1]

                            def t_s(t, gp=gp, qc=qc, j=j, kb=kb, pA=pA, pB=pB):
                                t.wait(sc)
                                t.wait(dv)
                                t.matmul(pA[:, :], KTg[0:64, j * 4096 + kb * 128:j * 4096 + (kb + 1) * 128],
                                         qcb[0:64, j * 512:(j + 1) * 512], start=True, stop=True,
                                         tile_position=(0, 0))
                                pe.inc(t.matmul(pB[:, :], KTg[64:128, j * 4096 + kb * 128:j * 4096 + (kb + 1) * 128],
                                                qcb[64:128, j * 512:(j + 1) * 512], start=True, stop=True,
                                                tile_position=(64, 0)))
                            b.on("tensor", t_s)

                            def sc_s(s, gp=gp, qc=qc, j=j, kb=kb, pA=pA, pB=pB, aA=aA, aB=aB):
                                s.wait(pe)
                                s.wait(dv)
                                s.activation(aA[:, :], pA[:, :], AF.Exp, scale=0.125)
                                sc.inc(s.activation(aB[:, :], pB[:, :], AF.Exp, scale=0.125))
                            b.on("scalar", sc_s)

                            diag = (kb >= 4 * qc)
                            if diag:
                                p_ = kb - 4 * qc

                                def v_m(v, aA=aA, aB=aB, p_=p_):
                                    v.wait(sc)
                                    dv.inc(v.tensor_mul(aA[:, :], aA[:, :], masks[:, p_ * 512:(p_ + 1) * 512]))
                                    dv.inc(v.tensor_mul(aB[:, :], aB[:, :], masks[:, p_ * 512:(p_ + 1) * 512]))
                                b.on("vector", v_m)

                            def t_av(t, gp=gp, qc=qc, j=j, kb=kb, aA=aA, aB=aB, diag=diag, nkb=nkb):
                                if diag:
                                    t.wait(dv)
                                else:
                                    t.wait(sc)
                                t.matmul(pv[0][:, :], Vg[:, kb * 520 + 2 * j * 65:kb * 520 + 2 * j * 65 + 65],
                                         aA[:, :], start=(kb == 0), stop=(kb == nkb - 1))
                                pe.inc(t.matmul(pv[1][:, :], Vg[:, kb * 520 + (2 * j + 1) * 65:kb * 520 + (2 * j + 1) * 65 + 65],
                                                aB[:, :], start=(kb == 0), stop=(kb == nkb - 1)))
                            b.on("tensor", t_av)

                        for hh in range(2):
                            def v_n(v, gp=gp, qc=qc, j=j, hh=hh):
                                v.wait(pe)
                                dv.inc(v.reciprocal(brs[64:65, :], pv[hh][64:65, :]))
                            b.on("vector", v_n)

                            def t_n(t, gp=gp, qc=qc, j=j, hh=hh):
                                t.wait(dv)
                                pe.inc(t.matmul(pbr[:, :], onesf[64:65, 0:64], brs[64:65, :], start=True, stop=True))
                            b.on("tensor", t_n)

                            def v_n2(v, gp=gp, qc=qc, j=j, hh=hh):
                                v.wait(pe)
                                dv.inc(v.tensor_copy(brd[:, :], pbr[:, :]))
                                v.wait(dv)
                                base = j * 512
                                if hh == 0:
                                    dv.inc(v.tensor_mul(attnoT[0:64, base:base + 512], pv[0][0:64, :], brd[0:64, :]))
                                else:
                                    dv.inc(v.tensor_mul(attnoT[64:128, base:base + 512], pv[1][0:64, :], brd[0:64, :]))
                            b.on("vector", v_n2)

                    if dbg is not None and gp == 0 and qc == 0:
                        def d_at(s):
                            s.wait(dv)
                            sdbg = b.ctr("sdbg")
                            sdbg.inc(s.dma_start(out=dbg["qcb"], in_=qcb[:, :]), 16)
                            sdbg.inc(s.dma_start(out=dbg["attnoT"], in_=attnoT[:, :]), 16)
                        b.on("sync", d_at)

                        def d_bar2(v):
                            v.wait(b.c["sdbg"])
                            dv.inc(v.memset(brs[0:1, 0:1], 0.0))
                        b.on("vector", d_bar2)

                    for m in range(8):
                        yb = ybuf[m % 2]

                        def t_o(t, gp=gp, qc=qc, m=m):
                            t.wait(dv)
                            t.wait(sc)
                            for itv in range(4):
                                i = t.matmul(pq[:, :],
                                             WT["wo"][:, (4 * gp + itv) * 1024 + m * 128:(4 * gp + itv) * 1024 + (m + 1) * 128],
                                             attnoT[:, itv * 512:(itv + 1) * 512],
                                             start=(itv == 0), stop=(itv == 3))
                            pe.inc(i)
                        b.on("tensor", t_o)

                        if gp == 0:
                            def v_o(v, qc=qc, m=m, yb=yb):
                                v.wait(pe)
                                v.wait(sy)
                                dv.inc(v.tensor_copy(yb[:, :], pq[:, :]))
                            b.on("vector", v_o)

                            def s_o(s, qc=qc, m=m, yb=yb):
                                s.wait(dv)
                                sy.inc(s.dma_start(out=yph.ap()[m, :, qc * 512:(qc + 1) * 512], in_=yb[:, :]), 16)
                            b.on("sync", s_o)
                        else:
                            y0 = y0b[m % 2]

                            def s_o0(s, qc=qc, m=m, y0=y0):
                                s.wait(dv)
                                sy.inc(s.dma_start(out=y0[:, :], in_=yph.ap()[m, :, qc * 512:(qc + 1) * 512]), 16)
                            b.on("sync", s_o0)

                            def v_o(v, qc=qc, m=m, yb=yb, y0=y0):
                                v.wait(pe)
                                v.wait(sy)
                                v.wait(sob[m % 2])
                                dv.inc(v.tensor_add(yb[:, :], pq[:, :], y0[:, :]))
                            b.on("vector", v_o)

                            def s_o(s, qc=qc, m=m, yb=yb):
                                s.wait(dv)
                                sob[m % 2].inc(s.dma_start(out=yph.ap()[m, :, qc * 512:(qc + 1) * 512], in_=yb[:, :]), 16)
                            b.on("sync", s_o)

        ytb = [e(nc.sbuf_tensor(f"{px}ytb{i}", [128, 1024], F32)) for i in range(2)]
        outt = [e(nc.sbuf_tensor(f"{px}outt{i}", [128, 1024], F32)) for i in range(2)]
        soo = [b.ctr("soo0"), b.ctr("soo1")]
        with ExitStack() as sctx:
            pxt = [sctx.enter_context(nc.psum_tensor(f"{px}pxt_{i}", [128, 1024], F32)) for i in range(2)]
            for st in range(NT):
                xb = x0t[st % 2]
                yt = ytb[st % 2]
                ob = outt[st % 2]

                def s_body(s, st=st, xb=xb, yt=yt):
                    s.wait(dv)
                    if st < 2:
                        s.wait(sob[0]); s.wait(sob[1])
                    sl.inc(s.dma_start(out=xb[:, :], in_=x0[st * 128:(st + 1) * 128, :]), 16)
                    sl.inc(s.dma_start(out=yt[:, :].rearrange("p (m w) -> p m w", m=8),
                                       in_=yph.ap()[:, :, st * 128:(st + 1) * 128].rearrange("m p w -> p m w")), 16)
                b.on("sync", s_body)

                def t_body(t, st=st, yt=yt):
                    t.wait(sl)
                    t.wait(dv)
                    for m in range(8):
                        pe.inc(t.transpose(pxt[st % 2][:, m * 128:(m + 1) * 128],
                                           yt[:, m * 128:(m + 1) * 128], identf[:, :]))
                b.on("tensor", t_body)

                def v_body(v, st=st, xb=xb, ob=ob):
                    v.wait(pe)
                    v.wait(soo[st % 2])
                    dv.inc(v.tensor_add(ob[:, :], pxt[st % 2][:, :], xb[:, :]))
                b.on("vector", v_body)

                def s2_body(s, st=st, ob=ob):
                    s.wait(dv)
                    soo[st % 2].inc(s.dma_start(out=out[st * 128:(st + 1) * 128, :], in_=ob[:, :]), 16)
                b.on("sync", s2_body)

        def fin(s):
            s.wait(soo[0]); s.wait(soo[1])
        b.on("sync", fin)
    return nc



def build_moe(nc, names, mode, env=None, px='mo_'):
    x1 = _dram(nc, names, "x", [S, D], F32, "ExternalInput")
    gw = _dram(nc, names, "gate_w", [8, D], F32, "ExternalInput")
    lb = _dram(nc, names, "lb_bias", [1, 8], F32, "ExternalInput")
    if mode == "full":
        egw = _dram(nc, names, "egw", [8, 512, D], F32, "ExternalInput")
        euw = _dram(nc, names, "euw", [8, 512, D], F32, "ExternalInput")
        edw = _dram(nc, names, "edw", [8, D, 512], F32, "ExternalInput")
        sgw = _dram(nc, names, "sgw", [D, D], F32, "ExternalInput")
        suw = _dram(nc, names, "suw", [D, D], F32, "ExternalInput")
        sdw = _dram(nc, names, "sdw", [D, D], F32, "ExternalInput")
        Sst = _dram(nc, names, "Sstack", [8, 8], F32, "ExternalInput")
        Cst = _dram(nc, names, "Cstack", [8, 8], F32, "ExternalInput")
        out = _dram(nc, names, "out", [S, D], F32, "ExternalOutput")
        gwT_h = nc.dram_tensor("gwT_scratch", [8, 128, 4096], BF16, kind="Internal")
        uwT_h = nc.dram_tensor("uwT_scratch", [8, 128, 4096], BF16, kind="Internal")
        dwT_h = nc.dram_tensor("dwT_scratch", [8, 128, 4096], BF16, kind="Internal")
        sgwT_h = nc.dram_tensor("sgwT_scratch", [128, 8192], BF16, kind="Internal")
        suwT_h = nc.dram_tensor("suwT_scratch", [128, 8192], BF16, kind="Internal")
        sdwT_h = nc.dram_tensor("sdwT_scratch", [128, 8192], BF16, kind="Internal")
    else:
        part = _dram(nc, names, "out", [16, 1], F32, "ExternalOutput")

    iota_np = np.broadcast_to(np.arange(8, dtype=np.float32), (128, 8)).copy()

    with ExitStack() as ctx:
        e = ctx.enter_context
        ident = e(nc.sbuf_tensor(f"{px}ident", [128, 128], BF16))
        identf = e(nc.sbuf_tensor(f"{px}identf", [128, 128], F32))
        onesf = e(nc.sbuf_tensor(f"{px}onesf", [1, 128], F32))
        onescol = e(nc.sbuf_tensor(f"{px}onescol", [128, 1], F32))
        iot = e(nc.sbuf_tensor(f"{px}iot", [128, 8], F32))
        iop8 = e(nc.sbuf_tensor(f"{px}iop8", [128, 8], F32))
        gws = e(nc.sbuf_tensor(f"{px}gws", [128, 1024], F32))
        gwsum = e(nc.sbuf_tensor(f"{px}gwsum", [128, 1], F32))
        gwTs = e(nc.sbuf_tensor(f"{px}gwTs", [128, 64], F32))
        lbB = e(nc.sbuf_tensor(f"{px}lbB", [128, 8], F32))
        gwsumB = e(nc.sbuf_tensor(f"{px}gwsumB", [128, 8], F32))
        x1t = [e(nc.sbuf_tensor(f"{px}x1t{i}", [128, 1024], F32)) for i in range(2)]
        x1Tt = e(nc.sbuf_tensor(f"{px}x1Tt", [128, 1024], F32))
        stat = e(nc.sbuf_tensor(f"{px}stat", [128, 12], F32))
        meanA = e(nc.sbuf_tensor(f"{px}meanA", [128, NT], F32))
        rstdA = e(nc.sbuf_tensor(f"{px}rstdA", [128, NT], F32))
        mvt = e(nc.sbuf_tensor(f"{px}mvt", [128, 2], F32))
        rws = e(nc.sbuf_tensor(f"{px}rws", [8, 128], F32))
        t8a = e(nc.sbuf_tensor(f"{px}t8a", [128, 8], F32))
        t8b = e(nc.sbuf_tensor(f"{px}t8b", [128, 8], F32))
        scoresb = e(nc.sbuf_tensor(f"{px}scoresb", [128, 8], F32))
        w1 = e(nc.sbuf_tensor(f"{px}w1", [128, 1], F32))
        w2 = e(nc.sbuf_tensor(f"{px}w2", [128, 1], F32))
        i1 = e(nc.sbuf_tensor(f"{px}i1", [128, 1], F32))
        i2 = e(nc.sbuf_tensor(f"{px}i2", [128, 1], F32))
        m1 = e(nc.sbuf_tensor(f"{px}m1", [128, 8], F32))
        m2 = e(nc.sbuf_tensor(f"{px}m2", [128, 8], F32))
        s2t = e(nc.sbuf_tensor(f"{px}s2t", [128, 8], F32))
        winv = e(nc.sbuf_tensor(f"{px}winv", [128, 1], F32))
        tc = e(nc.sbuf_tensor(f"{px}tc", [128, 16], F32))
        coefA = e(nc.sbuf_tensor(f"{px}coefA", [128, NT * 8], F32))
        partsb = e(nc.sbuf_tensor(f"{px}partsb", [16, 1], F32))

        if env is None:
            block = e(nc.Block())
            b = B(nc, ctx, block)
        else:
            block = env["block"]
            b = env["b"]
        sl = b.ctr("sl"); sg = b.ctr("sg"); pe = b.ctr("pe")
        dv = b.ctr("dv"); sc = b.ctr("sc"); so = b.ctr("so"); sw = b.ctr("sw")

        identh = nc.inline_tensor(np.eye(128, dtype=BF), f"{px}identnp")
        identfh = nc.inline_tensor(np.eye(128, dtype=np.float32), f"{px}identfnp")
        ioth = nc.inline_tensor(iota_np, f"{px}iotanp")
        iop8h = nc.inline_tensor(iota_np + 8.0, f"{px}iop8np")

        def g0_vec0(v):
            v.memset(onesf[:, :], 1.0)
            v.memset(onescol[:, :], 1.0)
            dv.inc(v.memset(gws[:, :], 0.0))
        b.on("vector", g0_vec0)

        def g0_sync(s):
            sl.inc(s.dma_start(out=ident[:, :], in_=identh.ap()), 16)
            sl.inc(s.dma_start(out=identf[:, :], in_=identfh.ap()), 16)
            sl.inc(s.dma_start(out=iot[:, :], in_=ioth.ap()), 16)
            sl.inc(s.dma_start(out=iop8[:, :], in_=iop8h.ap()), 16)
            s.wait(dv)
            sl.inc(s.dma_start(out=gws[0:8, :], in_=gw), 16)
        b.on("sync", g0_sync)

        def g0_vec(v):
            v.wait(sl)
            dv.inc(v.reduce_sum(gwsum[0:8, :], gws[0:8, :], axis=AX.X))
        b.on("vector", g0_vec)

        with ExitStack() as g0ctx:
            pgt = g0ctx.enter_context(nc.psum_tensor(f"{px}pgt", [128, 128], F32))
            pr8 = g0ctx.enter_context(nc.psum_tensor(f"{px}pr8", [128, 8], F32))

            for it in range(8):
                def t_body(t, it=it):
                    t.wait(sl)
                    t.wait(dv)
                    if it > 0:
                        t.wait(dv)
                    pe.inc(t.transpose(pgt[:, :], gws[:, it * 128:(it + 1) * 128], identf[:, :]))
                b.on("tensor", t_body)

                def v_body(v, it=it):
                    v.wait(pe)
                    dv.inc(v.tensor_copy(gwTs[:, it * 8:(it + 1) * 8], pgt[:, 0:8]))
                b.on("vector", v_body)

            def t2_body(t):
                t.wait(dv)
                pe.inc(t.matmul(pr8[0:1, :], gwsum[0:8, 0:1], identf[0:8, 0:8], start=True, stop=True))
            b.on("tensor", t2_body)

            def v2_body(v):
                v.wait(pe)
                dv.inc(v.tensor_copy(t8a[0:1, :], pr8[0:1, :]))
            b.on("vector", v2_body)

            def s_body(s):
                sl.inc(s.dma_start(out=t8b[0:1, :], in_=lb), 16)
            b.on("sync", s_body)

            def t3_body(t):
                t.wait(dv)
                t.wait(sl)
                pe.inc(t.matmul(pr8[:, :], onesf[0:1, :], t8a[0:1, :], start=True, stop=True))
            b.on("tensor", t3_body)

            def v3_body(v):
                v.wait(pe)
                dv.inc(v.tensor_copy(gwsumB[:, :], pr8[:, :]))
            b.on("vector", v3_body)

            def t4_body(t):
                t.wait(dv)
                t.wait(sl)
                pe.inc(t.matmul(pr8[:, :], onesf[0:1, :], t8b[0:1, :], start=True, stop=True))
            b.on("tensor", t4_body)

            def v4_body(v):
                v.wait(pe)
                dv.inc(v.tensor_copy(lbB[:, :], pr8[:, :]))
            b.on("vector", v4_body)

            ppart = g0ctx.enter_context(nc.psum_tensor(f"{px}ppart", [16, 1], F32))
            pxt = g0ctx.enter_context(nc.psum_tensor(f"{px}pxt", [128, 1024], F32))
            for st in range(NT):
                xb = x1t[st % 2]

                def s_body(s, st=st, xb=xb):
                    s.wait(pe)
                    sl.inc(s.dma_start(out=xb[:, :], in_=x1[st * 128:(st + 1) * 128, :]), 16)
                b.on("sync", s_body)

                def v_body(v, st=st, xb=xb):
                    v.wait(sl)
                    v.wait(dv)
                    v.bn_stats(stat[:, 0:6], xb[:, 0:512])
                    dv.inc(v.bn_stats(stat[:, 6:12], xb[:, 512:1024]))
                    v.wait(dv)
                    dv.inc(v.bn_aggr(mvt[:, :], stat[:, :]))
                    v.wait(dv)
                    v.tensor_copy(meanA[:, st:st + 1], mvt[:, 0:1])
                    dv.inc(v.tensor_scalar_add(rstdA[:, st:st + 1], mvt[:, 1:2], EPS))
                b.on("vector", v_body)

                def sc_body(s, st=st):
                    s.wait(dv)
                    sc.inc(s.activation(rstdA[:, st:st + 1], rstdA[:, st:st + 1], AF.Sqrt))
                b.on("scalar", sc_body)

                def v2b(v, st=st):
                    v.wait(sc)
                    dv.inc(v.reciprocal(rstdA[:, st:st + 1], rstdA[:, st:st + 1]))
                b.on("vector", v2b)

                def t_body(t, st=st, xb=xb):
                    t.wait(sl)
                    t.wait(dv)
                    for it in range(8):
                        pe.inc(t.transpose(pxt[:, it * 128:(it + 1) * 128],
                                           xb[:, it * 128:(it + 1) * 128], identf[:, :]))
                b.on("tensor", t_body)

                def v3b(v, st=st):
                    v.wait(pe)
                    dv.inc(v.tensor_copy(x1Tt[:, :], pxt[:, :]))
                b.on("vector", v3b)

                def t2b(t, st=st):
                    t.wait(dv)
                    t.wait(sc)
                    for it in range(8):
                        i = t.matmul(pgt[0:8, 0:128], gwTs[:, it * 8:(it + 1) * 8],
                                     x1Tt[:, it * 128:(it + 1) * 128], start=(it == 0), stop=(it == 7))
                    pe.inc(i)
                b.on("tensor", t2b)

                def sc2b(s, st=st):
                    s.wait(pe)
                    sc.inc(s.copy(rws[:, :], pgt[0:8, 0:128]))
                b.on("scalar", sc2b)

                def t3b(t, st=st):
                    t.wait(sc)
                    t.wait(dv)
                    pe.inc(t.matmul(pr8[:, :], rws[:, :], identf[0:8, 0:8], start=True, stop=True))
                b.on("tensor", t3b)

                def v4b(v, st=st):
                    v.wait(pe)
                    dv.inc(v.tensor_scalar_mul(t8a[:, :], gwsumB[:, :], meanA[:, st:st + 1]))
                    v.wait(dv)
                    dv.inc(v.tensor_sub(t8b[:, :], pr8[:, :], t8a[:, :]))
                    v.wait(dv)
                    dv.inc(v.tensor_scalar_mul(t8b[:, :], t8b[:, :], rstdA[:, st:st + 1]))
                    v.wait(dv)
                    dv.inc(v.tensor_add(t8b[:, :], t8b[:, :], lbB[:, :]))
                b.on("vector", v4b)

                def sc3b(s, st=st):
                    s.wait(dv)
                    sc.inc(s.activation(scoresb[:, :], t8b[:, :], AF.Sigmoid))
                b.on("scalar", sc3b)

                def v5b(v, st=st):
                    v.wait(sc)
                    dv.inc(v.reduce_max(w1[:, :], scoresb[:, :], axis=AX.X))
                    v.wait(dv)
                    dv.inc(v.tensor_scalar(m1[:, :], scoresb[:, :], w1[:, 0:1], None, ALU.is_ge))
                    v.wait(dv)
                    dv.inc(v.scalar_tensor_tensor(t8a[:, :], m1[:, :], -8.0, iop8[:, :], ALU.mult, ALU.add))
                    v.wait(dv)
                    dv.inc(v.tensor_reduce(i1[:, :], t8a[:, :], axis=AX.X, op=ALU.min))
                    v.wait(dv)
                    dv.inc(v.tensor_scalar(m1[:, :], iot[:, :], i1[:, 0:1], None, ALU.is_equal))
                    v.wait(dv)
                    dv.inc(v.scalar_tensor_tensor(s2t[:, :], m1[:, :], -1e30, scoresb[:, :], ALU.mult, ALU.add))
                    v.wait(dv)
                    dv.inc(v.reduce_max(w2[:, :], s2t[:, :], axis=AX.X))
                    v.wait(dv)
                    dv.inc(v.tensor_scalar(m2[:, :], s2t[:, :], w2[:, 0:1], None, ALU.is_ge))
                    v.wait(dv)
                    dv.inc(v.scalar_tensor_tensor(t8a[:, :], m2[:, :], -8.0, iop8[:, :], ALU.mult, ALU.add))
                    v.wait(dv)
                    dv.inc(v.tensor_reduce(i2[:, :], t8a[:, :], axis=AX.X, op=ALU.min))
                    v.wait(dv)
                    dv.inc(v.tensor_scalar(m2[:, :], iot[:, :], i2[:, 0:1], None, ALU.is_equal))
                    v.wait(dv)
                    dv.inc(v.tensor_add(winv[:, :], w1[:, :], w2[:, :]))
                    v.wait(dv)
                    dv.inc(v.tensor_scalar_add(winv[:, :], winv[:, :], 1e-8))
                    v.wait(dv)
                    dv.inc(v.reciprocal(winv[:, :], winv[:, :]))
                    v.wait(dv)
                    dv.inc(v.tensor_scalar_mul(t8a[:, :], m1[:, :], w1[:, 0:1]))
                    v.wait(dv)
                    dv.inc(v.tensor_scalar_mul(t8b[:, :], m2[:, :], w2[:, 0:1]))
                    v.wait(dv)
                    dv.inc(v.tensor_add(t8a[:, :], t8a[:, :], t8b[:, :]))
                    v.wait(dv)
                    dv.inc(v.tensor_scalar_mul(tc[:, 0:8], t8a[:, :], winv[:, 0:1]))
                    v.wait(dv)
                    dv.inc(v.tensor_add(tc[:, 8:16], m1[:, :], m2[:, :]))
                    v.wait(dv)
                    dv.inc(v.tensor_copy(coefA[:, st * 8:(st + 1) * 8], tc[:, 8:16]))
                b.on("vector", v5b)

                def t4b(t, st=st):
                    t.wait(dv)
                    pe.inc(t.matmul(ppart[:, :], tc[:, :], onescol[:, :],
                                    start=(st == 0), stop=(st == NT - 1)))
                b.on("tensor", t4b)

            def v6b(v):
                v.wait(pe)
                dv.inc(v.tensor_copy(partsb[:, :], ppart[:, :]))
            b.on("vector", v6b)

            if mode == "partials":
                def s_out(s):
                    s.wait(dv)
                    so.inc(s.dma_start(out=part, in_=partsb[:, :]), 16)
                    s.wait(so)
                b.on("sync", s_out)
                return nc

            wmB = e(nc.sbuf_tensor(f"{px}wmB", [128, 8], F32))
            Ssb = e(nc.sbuf_tensor(f"{px}Ssb", [8, 8], F32))
            Csb = e(nc.sbuf_tensor(f"{px}Csb", [8, 8], F32))
            Sv = e(nc.sbuf_tensor(f"{px}Sv", [8, 1], F32))
            Cv = e(nc.sbuf_tensor(f"{px}Cv", [8, 1], F32))

            def s2_body(s):
                sl.inc(s.dma_start(out=Ssb[:, :], in_=Sst), 16)
                sl.inc(s.dma_start(out=Csb[:, :], in_=Cst), 16)
            b.on("sync", s2_body)

            def v7b(v):
                v.wait(sl)
                v.wait(dv)
                dv.inc(v.reduce_sum(Sv[:, :], Ssb[:, :], axis=AX.X))
                dv.inc(v.reduce_sum(Cv[:, :], Csb[:, :], axis=AX.X))
                v.wait(dv)
                dv.inc(v.tensor_scalar_max(Cv[:, :], Cv[:, :], 1.0))
                v.wait(dv)
                dv.inc(v.reciprocal(Cv[:, :], Cv[:, :]))
                v.wait(dv)
                dv.inc(v.tensor_mul(Sv[:, :], Sv[:, :], Cv[:, :]))
            b.on("vector", v7b)

            def t5b(t):
                t.wait(dv)
                pe.inc(t.matmul(pr8[0:1, :], Sv[:, 0:1], identf[0:8, 0:8], start=True, stop=True))
            b.on("tensor", t5b)

            def v8b(v):
                v.wait(pe)
                dv.inc(v.tensor_copy(t8a[0:1, :], pr8[0:1, :]))
            b.on("vector", v8b)

            def t6b(t):
                t.wait(dv)
                pe.inc(t.matmul(pr8[:, :], onesf[0:1, :], t8a[0:1, :], start=True, stop=True))
            b.on("tensor", t6b)

            def v9b(v):
                v.wait(pe)
                dv.inc(v.tensor_copy(wmB[:, :], pr8[:, :]))
                v.wait(dv)
                for st in range(NT):
                    i = v.tensor_mul(coefA[:, st * 8:(st + 1) * 8], coefA[:, st * 8:(st + 1) * 8], wmB[:, :])
                dv.inc(i)
            b.on("vector", v9b)

        with ExitStack() as wctx:
            wstage = wctx.enter_context(nc.sbuf_tensor(f"{px}wstage", [128, 8192], BF16))
            wTsb = wctx.enter_context(nc.sbuf_tensor(f"{px}wTsb", [128, 8192], BF16))
            ptw = wctx.enter_context(nc.psum_tensor(f"{px}ptw", [128, 1024], BF16))
            jobs = []
            for ee in range(8):
                jobs += [(egw[ee], gwT_h.ap()[ee], 4), (euw[ee], uwT_h.ap()[ee], 4),
                         (edw[ee], dwT_h.ap()[ee], 8)]
            jobs += [(sgw, sgwT_h.ap(), 8), (suw, suwT_h.ap(), 8), (sdw, sdwT_h.ap(), 8)]
            for ji, (src, dsth, nt) in enumerate(jobs):
                ncols = 1024 if nt == 8 else 1024
                ncols = src.shape[-1]

                def g_body(g, ji=ji, src=src, nt=nt, ncols=ncols):
                    g.wait(pe)
                    sg.inc(g.dma_start(
                        out=wstage[:, 0:nt * ncols].rearrange("p (t i) -> p t i", t=nt),
                        in_=src.rearrange("(t p) i -> p t i", p=128)), 16)
                b.on("gpsimd", g_body)

                nu = ncols // 128
                for u in range(nu):
                    def t_body(t, ji=ji, u=u, nt=nt, ncols=ncols):
                        t.wait(sg)
                        t.wait(sl)
                        t.wait(dv)
                        for tt in range(nt):
                            pe.inc(t.transpose(ptw[:, tt * 128:(tt + 1) * 128],
                                               wstage[:, tt * ncols + u * 128:tt * ncols + (u + 1) * 128],
                                               ident[:, :]))
                    b.on("tensor", t_body)

                    def v_body(v, ji=ji, u=u, nt=nt):
                        v.wait(pe)
                        v.wait(sw)
                        dv.inc(v.tensor_copy(wTsb[:, u * nt * 128:(u + 1) * nt * 128], ptw[:, 0:nt * 128]))
                    b.on("vector", v_body)

                def s_body(s, ji=ji, dsth=dsth, nt=nt, ncols=ncols):
                    s.wait(dv)
                    nu_ = ncols // 128
                    sw.inc(s.dma_start(out=dsth, in_=wTsb[:, 0:nu_ * nt * 128]), 16)
                b.on("sync", s_body)

        acc = e(nc.sbuf_tensor(f"{px}acc", [128, 16 * 1024], F32))
        ln1T = e(nc.sbuf_tensor(f"{px}ln1T", [128, 8 * 2048], BF16))
        lnt = e(nc.sbuf_tensor(f"{px}lnt", [128, 1024], BF16))
        wgs = e(nc.sbuf_tensor(f"{px}wgs", [128, 8192], BF16))
        wus = e(nc.sbuf_tensor(f"{px}wus", [128, 8192], BF16))
        wds = e(nc.sbuf_tensor(f"{px}wds", [128, 8192], BF16))
        silb = e(nc.sbuf_tensor(f"{px}silb", [128, 512], BF16))
        actb = e(nc.sbuf_tensor(f"{px}actb", [128, 8 * 512], BF16))

        for h in range(2):
            def s_body(s, h=h):
                s.wait(dv)
                s.wait(so)
                s.wait(sw)
                sl.inc(s.dma_start(out=acc[:, :].rearrange("p (t d) -> p t d", t=16),
                                   in_=x1[h * 2048:(h + 1) * 2048, :].rearrange("(t p) d -> p t d", p=128)), 16)
            b.on("sync", s_body)

            with nc.psum_tensor(f"{px}plt_{h}", [128, 1024], BF16) as plt:
                for stt in range(16):
                    gst = h * 16 + stt

                    def v_body(v, h=h, stt=stt, gst=gst):
                        v.wait(sl)
                        v.wait(pe)
                        dv.inc(v.tensor_scalar(lnt[:, :], acc[:, stt * 1024:(stt + 1) * 1024],
                                               meanA[:, gst:gst + 1], rstdA[:, gst:gst + 1],
                                               ALU.subtract, ALU.mult))
                    b.on("vector", v_body)

                    def t_body(t, h=h, stt=stt):
                        t.wait(dv)
                        for it in range(8):
                            pe.inc(t.transpose(plt[:, it * 128:(it + 1) * 128],
                                               lnt[:, it * 128:(it + 1) * 128], ident[:, :]))
                    b.on("tensor", t_body)

                    def v2_body(v, h=h, stt=stt):
                        v.wait(pe)
                        for it in range(8):
                            i = v.tensor_copy(ln1T[:, it * 2048 + stt * 128:it * 2048 + (stt + 1) * 128],
                                              plt[:, it * 128:(it + 1) * 128])
                        dv.inc(i)
                    b.on("vector", v2_body)

            for ee in range(9):
                shared = (ee == 8)
                if shared:
                    def g_body(g, h=h):
                        g.wait(pe)
                        g.wait(sw)
                        sg.inc(g.dma_start(out=wgs[:, :], in_=sgwT_h.ap()), 16)
                        sg.inc(g.dma_start(out=wus[:, :], in_=suwT_h.ap()), 16)
                        sg.inc(g.dma_start(out=wds[:, :], in_=sdwT_h.ap()), 16)
                    b.on("gpsimd", g_body)
                else:
                    def g_body(g, h=h, ee=ee):
                        g.wait(pe)
                        g.wait(sw)
                        sg.inc(g.dma_start(out=wgs[:, 0:4096], in_=gwT_h.ap()[ee]), 16)
                        sg.inc(g.dma_start(out=wus[:, 0:4096], in_=uwT_h.ap()[ee]), 16)
                        sg.inc(g.dma_start(out=wds[:, 0:4096], in_=dwT_h.ap()[ee]), 16)
                    b.on("gpsimd", g_body)

                nI = 8 if shared else 4
                wgc, wuc, wdc = wgs, wus, wds
                for sc_ in range(4):
                    for Ih in range(nI // 4):
                        with ExitStack() as pctx:
                            pgu = [pctx.enter_context(nc.psum_tensor(f"{px}pgu_{h}_{ee}_{sc_}_{Ih}_{i}", [128, 512], F32))
                                   for i in range(8)]

                            def t_body(t, h=h, ee=ee, sc_=sc_, Ih=Ih, wgc=wgc, wuc=wuc, nI=nI):
                                t.wait(sg)
                                t.wait(dv)
                                t.wait(sc)
                                for It4 in range(4):
                                    It = Ih * 4 + It4
                                    for it in range(8):
                                        t.matmul(pgu[It4][:, :],
                                                 wgc[:, it * (nI * 128) + It * 128:it * (nI * 128) + (It + 1) * 128],
                                                 ln1T[:, it * 2048 + sc_ * 512:it * 2048 + (sc_ + 1) * 512],
                                                 start=(it == 0), stop=(it == 7))
                                    for it in range(8):
                                        i = t.matmul(pgu[4 + It4][:, :],
                                                     wuc[:, it * (nI * 128) + It * 128:it * (nI * 128) + (It + 1) * 128],
                                                     ln1T[:, it * 2048 + sc_ * 512:it * 2048 + (sc_ + 1) * 512],
                                                     start=(it == 0), stop=(it == 7))
                                pe.inc(i)
                            b.on("tensor", t_body)

                            for It4 in range(4):
                                def sc_body(s, It4=It4):
                                    s.wait(pe)
                                    s.wait(dv)
                                    sc.inc(s.activation(silb[:, :], pgu[It4][:, :], AF.Silu))
                                b.on("scalar", sc_body)

                                def v_body(v, It4=It4, Ih=Ih):
                                    v.wait(sc)
                                    dv.inc(v.tensor_mul(actb[:, (Ih * 4 + It4) * 512:(Ih * 4 + It4 + 1) * 512],
                                                        silb[:, :], pgu[4 + It4][:, :]))
                                b.on("vector", v_body)

                    with ExitStack() as pctx:
                        pd = [pctx.enter_context(nc.psum_tensor(f"{px}pd_{h}_{ee}_{sc_}_{i}", [128, 1024], F32))
                              for i in range(2)]
                        for stt4 in range(4):
                            gst = h * 16 + sc_ * 4 + stt4

                            def t_body(t, h=h, ee=ee, sc_=sc_, stt4=stt4, wdc=wdc, nI=nI):
                                t.wait(dv)
                                for It in range(nI):
                                    for n in range(2):
                                        i = t.matmul(pd[stt4 % 2][:, n * 512:(n + 1) * 512],
                                                     actb[:, It * 512 + stt4 * 128:It * 512 + (stt4 + 1) * 128],
                                                     wdc[:, It * 1024 + n * 512:It * 1024 + (n + 1) * 512],
                                                     start=(It == 0), stop=(It == nI - 1))
                                pe.inc(i)
                            b.on("tensor", t_body)

                            def v_body(v, h=h, ee=ee, sc_=sc_, stt4=stt4, gst=gst, shared=shared):
                                v.wait(pe)
                                stt = sc_ * 4 + stt4
                                coef = 1.0 if shared else coefA[:, gst * 8 + ee:gst * 8 + ee + 1]
                                dv.inc(v.scalar_tensor_tensor(acc[:, stt * 1024:(stt + 1) * 1024],
                                                              pd[stt4 % 2][:, :], coef,
                                                              acc[:, stt * 1024:(stt + 1) * 1024],
                                                              ALU.mult, ALU.add))
                            b.on("vector", v_body)

            def s2_body(s, h=h):
                s.wait(dv)
                so.inc(s.dma_start(out=out[h * 2048:(h + 1) * 2048, :].rearrange("(t p) d -> p t d", p=128),
                                   in_=acc[:, :].rearrange("p (t d) -> p t d", t=16)), 16)
            b.on("sync", s2_body)

        def fin(s):
            s.wait(so)
        b.on("sync", fin)
    return nc



def build_attn1(nc, names, env=None, px='a1_'):
    x2 = _dram(nc, names, "x", [S, D], F32, "ExternalInput")
    wnames = ["p", "q", "k", "v", "o"]
    Wd = {w: _dram(nc, names, "W" + w, [D, D], F32, "ExternalInput") for w in wnames}
    Bd = {w: _dram(nc, names, "b" + w, [1, D], F32, "ExternalInput") for w in wnames}
    out = _dram(nc, names, "out", [S, D], F32, "ExternalOutput")
    ln2h = nc.dram_tensor("ln2T_scratch", [8, 128, S + 2], BF16, kind="Internal")

    bsum_np = np.zeros((128, 8 * 16), dtype=BF)
    bbc_np = np.zeros((16, 8 * 128), dtype=BF)
    for t in range(8):
        bsum_np[0:64, t * 16 + 2 * t] = 1
        bsum_np[64:128, t * 16 + 2 * t + 1] = 1
        bbc_np[2 * t, t * 128:t * 128 + 64] = 1
        bbc_np[2 * t + 1, t * 128 + 64:t * 128 + 128] = 1

    with ExitStack() as ctx:
        e = ctx.enter_context
        ident = e(nc.sbuf_tensor(f"{px}ident", [128, 128], BF16))
        identf = e(nc.sbuf_tensor(f"{px}identf", [128, 128], F32))
        bsum = e(nc.sbuf_tensor(f"{px}bsum", [128, 8 * 16], BF16))
        bbc = e(nc.sbuf_tensor(f"{px}bbc", [16, 8 * 128], BF16))
        WT = {w: e(nc.sbuf_tensor(f"{px}WT{w}", [128, 8192], BF16)) for w in wnames}
        bcol = {w: e(nc.sbuf_tensor(f"{px}bcol{w}", [128, 8], F32)) for w in wnames}
        x2t = [e(nc.sbuf_tensor(f"{px}x2t{i}", [128, 1024], F32)) for i in range(2)]
        stat = e(nc.sbuf_tensor(f"{px}stat", [128, 12], F32))
        mv = e(nc.sbuf_tensor(f"{px}mv", [128, 2], F32))
        rstd = e(nc.sbuf_tensor(f"{px}rstd", [128, 1], F32))
        lnt = e(nc.sbuf_tensor(f"{px}lnt", [128, 1024], BF16))
        d1ctx = ExitStack()
        wstage = d1ctx.enter_context(nc.sbuf_tensor(f"{px}wstage", [128, 8192], BF16))
        ln2T = d1ctx.enter_context(nc.sbuf_tensor(f"{px}ln2T", [128, 8 * 4096], BF16))
        if env is None:
            block = e(nc.Block())
            b = B(nc, ctx, block)
        else:
            block = env["block"]
            b = env["b"]
        sl = b.ctr("sl"); sg = b.ctr("sg"); pe = b.ctr("pe")
        dv = b.ctr("dv"); sc = b.ctr("sc"); sp = b.ctr("sp")
        sob = [b.ctr("sob0"), b.ctr("sob1")]

        identh = nc.inline_tensor(np.eye(128, dtype=BF), f"{px}identnp")
        identfh = nc.inline_tensor(np.eye(128, dtype=np.float32), f"{px}identfnp")
        bsumh = nc.inline_tensor(bsum_np, f"{px}bsumnp")
        bbch = nc.inline_tensor(bbc_np, f"{px}bbcnp")

        def d0_sync(s):
            sl.inc(s.dma_start(out=ident[:, :], in_=identh.ap()), 16)
            sl.inc(s.dma_start(out=identf[:, :], in_=identfh.ap()), 16)
            sl.inc(s.dma_start(out=bsum[:, :], in_=bsumh.ap()), 16)
            sl.inc(s.dma_start(out=bbc[:, :], in_=bbch.ap()), 16)
            for w in wnames:
                for m in range(8):
                    sl.inc(s.dma_start(out=bcol[w][:, m:m + 1],
                                       in_=Bd[w][0:1, m * 128:(m + 1) * 128].rearrange("o p -> p o")), 16)
        b.on("sync", d0_sync)

        def d0_vec(v):
            v.wait(sl)
            dv.inc(v.tensor_scalar_mul(bcol["q"][:, :], bcol["q"][:, :], 0.125))
        b.on("vector", d0_vec)

        for wi, w in enumerate(wnames):
            def g_body(g, w=w, wi=wi):
                if wi > 0:
                    g.wait(pe)
                sg.inc(g.dma_start(out=wstage[:, :].rearrange("p (t i) -> p t i", t=8),
                                   in_=Wd[w].rearrange("(t p) i -> p t i", p=128)), 16)
            b.on("gpsimd", g_body)

            with nc.psum_tensor(f"{px}ptw_{w}", [128, 1024], BF16) as ptw:
                for u in range(8):
                    def t_body(t, w=w, u=u):
                        t.wait(sg)
                        t.wait(sl)
                        if u > 0:
                            t.wait(dv)
                        for tt in range(8):
                            pe.inc(t.transpose(ptw[:, tt * 128:(tt + 1) * 128],
                                               wstage[:, tt * 1024 + u * 128:tt * 1024 + (u + 1) * 128],
                                               ident[:, :]))
                    b.on("tensor", t_body)

                    def v_body(v, w=w, u=u):
                        v.wait(pe)
                        dv.inc(v.tensor_copy(WT[w][:, u * 1024:(u + 1) * 1024], ptw[:, :]))
                    b.on("vector", v_body)

        for st in range(NT):
            xb = x2t[st % 2]

            def s_body(s, st=st, xb=xb):
                s.wait(dv)
                sl.inc(s.dma_start(out=xb[:, :], in_=x2[st * 128:(st + 1) * 128, :]), 16)
            b.on("sync", s_body)

            def v_body(v, st=st, xb=xb):
                v.wait(sl)
                v.wait(dv)
                v.bn_stats(stat[:, 0:6], xb[:, 0:512])
                dv.inc(v.bn_stats(stat[:, 6:12], xb[:, 512:1024]))
                v.wait(dv)
                dv.inc(v.bn_aggr(mv[:, :], stat[:, :]))
                v.wait(dv)
                dv.inc(v.tensor_scalar_add(rstd[:, 0:1], mv[:, 1:2], EPS))
            b.on("vector", v_body)

            def s2_body(s):
                s.wait(dv)
                sc.inc(s.activation(rstd[:, 0:1], rstd[:, 0:1], AF.Sqrt))
            b.on("scalar", s2_body)

            def v2_body(v, st=st, xb=xb):
                v.wait(sc)
                dv.inc(v.reciprocal(rstd[:, 0:1], rstd[:, 0:1]))
                v.wait(dv)
                dv.inc(v.tensor_scalar(lnt[:, :], xb[:, :], mv[:, 0:1], rstd[:, 0:1],
                                       ALU.subtract, ALU.mult))
            b.on("vector", v2_body)

            with nc.psum_tensor(f"{px}plt_{st}", [128, 1024], BF16) as plt:
                def t_body(t, st=st):
                    t.wait(dv)
                    for it in range(8):
                        pe.inc(t.transpose(plt[:, it * 128:(it + 1) * 128],
                                           lnt[:, it * 128:(it + 1) * 128], ident[:, :]))
                b.on("tensor", t_body)

                def v3_body(v, st=st):
                    v.wait(pe)
                    for it in range(8):
                        i = v.tensor_copy(ln2T[:, it * 4096 + st * 128:it * 4096 + (st + 1) * 128],
                                          plt[:, it * 128:(it + 1) * 128])
                    dv.inc(i)
                b.on("vector", v3_body)

        def spill(s):
            s.wait(dv)
            for it in range(8):
                sp.inc(s.dma_start(out=ln2h.ap()[it, :, 1:S + 1], in_=ln2T[:, it * 4096:(it + 1) * 4096]), 16)
                sp.inc(s.dma_start(out=ln2h.ap()[it, :, 0:1], in_=ln2T[:, it * 4096 + 4095:it * 4096 + 4096],
                                   allow_slow_non_contiguous=True), 16)
                sp.inc(s.dma_start(out=ln2h.ap()[it, :, S + 1:S + 2], in_=ln2T[:, it * 4096:it * 4096 + 1],
                                   allow_slow_non_contiguous=True), 16)
        b.on("sync", spill)
        d1ctx.close()

        lnc = e(nc.sbuf_tensor(f"{px}lnc", [128, 8 * 514], BF16))
        xpc = e(nc.sbuf_tensor(f"{px}xpc", [128, 8 * 514], BF16))
        qc = e(nc.sbuf_tensor(f"{px}qc", [128, 8 * 512], BF16))
        kc = e(nc.sbuf_tensor(f"{px}kc", [128, 8 * 514], BF16))
        vc = e(nc.sbuf_tensor(f"{px}vc", [128, 8 * 514], BF16))
        prodb = [e(nc.sbuf_tensor(f"{px}prodb{i}", [128, 512], BF16)) for i in range(2)]
        expb = e(nc.sbuf_tensor(f"{px}expb", [16, 3 * 512], F32))
        dnm = e(nc.sbuf_tensor(f"{px}dnm", [16, 512], F32))
        attnb = e(nc.sbuf_tensor(f"{px}attnb", [16, 3 * 512], BF16))
        acc = e(nc.sbuf_tensor(f"{px}acc", [128, 8 * 512], F32))
        tmpb = e(nc.sbuf_tensor(f"{px}tmpb", [128, 512], F32))
        attnoT = e(nc.sbuf_tensor(f"{px}attnoT", [128, 8 * 512], BF16))
        yT = e(nc.sbuf_tensor(f"{px}yT", [128, 8 * 512], F32))
        outt = [e(nc.sbuf_tensor(f"{px}outt{i}", [128, 1024], F32)) for i in range(2)]

        def mm_proj(t, w, m, dst_main, dst_halo, src, width):
            for it in range(8):
                i = t.matmul(dst_main[:, :], WT[w][:, it * 1024 + m * 128:it * 1024 + (m + 1) * 128],
                             src[:, it * 514 + 1:it * 514 + 513],
                             start=(it == 0), stop=(it == 7))
            if width == 514:
                for it in range(8):
                    i = t.matmul(dst_halo[:, 0:2], WT[w][:, it * 1024 + m * 128:it * 1024 + (m + 1) * 128],
                                 src[:, it * 514:it * 514 + 514:513],
                                 start=(it == 0), stop=(it == 7))
            return i

        for c in range(8):
            def s_body(s, c=c):
                s.wait(pe)
                if c == 0:
                    s.wait(sp)
                sl.inc(s.dma_start(out=lnc[:, :].rearrange("p (it w) -> p it w", it=8),
                                   in_=ln2h.ap()[:, :, c * 512:c * 512 + 514].rearrange("it p w -> p it w")), 16)
            b.on("sync", s_body)

            with ExitStack() as sctx:
                pmain = [sctx.enter_context(nc.psum_tensor(f"{px}pm_{c}_{i}", [128, 512], F32)) for i in range(2)]
                phalo = [sctx.enter_context(nc.psum_tensor(f"{px}phh_{c}_{i}", [128, 2], F32)) for i in range(2)]
                for m in range(8):
                    def t_body(t, c=c, m=m):
                        t.wait(sl)
                        t.wait(dv)
                        t.wait(sc)
                        pe.inc(mm_proj(t, "p", m, pmain[m % 2], phalo[m % 2], lnc, 514))
                    b.on("tensor", t_body)

                    def sc_body(s, c=c, m=m):
                        s.wait(pe)
                        s.wait(dv)
                        s.activation(xpc[:, m * 514 + 1:m * 514 + 513], pmain[m % 2][:, :], AF.Identity,
                                     bias=bcol["p"][:, m:m + 1])
                        i = s.activation(xpc[:, m * 514:m * 514 + 514:513], phalo[m % 2][:, :], AF.Identity,
                                         bias=bcol["p"][:, m:m + 1])
                        sc.inc(i)
                    b.on("scalar", sc_body)

                for m in range(8):
                    def t_body(t, c=c, m=m):
                        t.wait(sc)
                        pe.inc(mm_proj(t, "q", m, pmain[m % 2], None, xpc, 512))
                    b.on("tensor", t_body)

                    def sc_body(s, c=c, m=m):
                        s.wait(pe)
                        sc.inc(s.activation(qc[:, m * 512:(m + 1) * 512], pmain[m % 2][:, :], AF.Identity,
                                            bias=bcol["q"][:, m:m + 1], scale=0.125))
                    b.on("scalar", sc_body)

                for dst, w in ((kc, "k"), (vc, "v")):
                    for m in range(8):
                        def t_body(t, c=c, m=m, w=w):
                            t.wait(sc)
                            pe.inc(mm_proj(t, w, m, pmain[m % 2], phalo[m % 2], xpc, 514))
                        b.on("tensor", t_body)

                        def sc_body(s, c=c, m=m, w=w, dst=dst):
                            s.wait(pe)
                            s.activation(dst[:, m * 514 + 1:m * 514 + 513], pmain[m % 2][:, :], AF.Identity,
                                         bias=bcol[w][:, m:m + 1])
                            sc.inc(s.activation(dst[:, m * 514:m * 514 + 514:513], phalo[m % 2][:, :],
                                                AF.Identity, bias=bcol[w][:, m:m + 1]))
                        b.on("scalar", sc_body)

            with ExitStack() as sctx:
                psc = sctx.enter_context(nc.psum_tensor(f"{px}psc_{c}", [16, 3 * 512], F32))
                for n in range(3):
                    for t8 in range(8):
                        pb_ = prodb[(n * 8 + t8) % 2]

                        def v_body(v, c=c, n=n, t8=t8, pb_=pb_):
                            v.wait(sc)
                            v.wait(pe)
                            dv.inc(v.tensor_mul(pb_[:, :], qc[:, t8 * 512:(t8 + 1) * 512],
                                                kc[:, t8 * 514 + n:t8 * 514 + n + 512]))
                        b.on("vector", v_body)

                        def t_body(t, c=c, n=n, t8=t8, pb_=pb_):
                            t.wait(dv)
                            pe.inc(t.matmul(psc[:, n * 512:(n + 1) * 512], bsum[:, t8 * 16:(t8 + 1) * 16],
                                            pb_[:, :], start=(t8 == 0), stop=(t8 == 7)))
                        b.on("tensor", t_body)

                def sc_body(s, c=c):
                    s.wait(pe)
                    sc.inc(s.activation(expb[:, :], psc[:, :], AF.Exp))
                b.on("scalar", sc_body)

                def v_body(v, c=c):
                    v.wait(sc)
                    dv.inc(v.tensor_add(dnm[:, :], expb[:, 0:512], expb[:, 512:1024]))
                    v.wait(dv)
                    dv.inc(v.tensor_add(dnm[:, :], dnm[:, :], expb[:, 1024:1536]))
                    v.wait(dv)
                    dv.inc(v.reciprocal(dnm[:, :], dnm[:, :]))
                    v.wait(dv)
                    for n in range(3):
                        i = v.tensor_mul(attnb[:, n * 512:(n + 1) * 512],
                                         expb[:, n * 512:(n + 1) * 512], dnm[:, :])
                    dv.inc(i)
                b.on("vector", v_body)

            with ExitStack() as sctx:
                pb2 = [sctx.enter_context(nc.psum_tensor(f"{px}pb2_{c}_{i}", [128, 512], F32)) for i in range(2)]
                for n in range(3):
                    for t8 in range(8):
                        pp = pb2[(n * 8 + t8) % 2]

                        def t_body(t, c=c, n=n, t8=t8, pp=pp):
                            t.wait(dv)
                            pe.inc(t.matmul(pp[:, :], bbc[:, t8 * 128:(t8 + 1) * 128],
                                            attnb[:, n * 512:(n + 1) * 512], start=True, stop=True))
                        b.on("tensor", t_body)

                        def v_body(v, c=c, n=n, t8=t8, pp=pp):
                            v.wait(pe)
                            vsl = vc[:, t8 * 514 + n:t8 * 514 + n + 512]
                            if n == 0:
                                dv.inc(v.tensor_mul(acc[:, t8 * 512:(t8 + 1) * 512], pp[:, :], vsl))
                            else:
                                dv.inc(v.tensor_mul(tmpb[:, :], pp[:, :], vsl))
                                v.wait(dv)
                                dst = acc if n == 1 else attnoT
                                dv.inc(v.tensor_add(dst[:, t8 * 512:(t8 + 1) * 512],
                                                    acc[:, t8 * 512:(t8 + 1) * 512], tmpb[:, :]))
                        b.on("vector", v_body)

            with ExitStack() as sctx:
                pyo = [sctx.enter_context(nc.psum_tensor(f"{px}pyo_{c}_{i}", [128, 512], F32)) for i in range(2)]
                for m in range(8):
                    def t_body(t, c=c, m=m):
                        t.wait(dv)
                        t.wait(sc)
                        for it in range(8):
                            pe.inc(t.matmul(pyo[m % 2][:, :], WT["o"][:, it * 1024 + m * 128:it * 1024 + (m + 1) * 128],
                                            attnoT[:, it * 512:(it + 1) * 512],
                                            start=(it == 0), stop=(it == 7)))
                    b.on("tensor", t_body)

                    def sc_body(s, c=c, m=m):
                        s.wait(pe)
                        sc.inc(s.activation(yT[:, m * 512:(m + 1) * 512], pyo[m % 2][:, :], AF.Identity,
                                            bias=bcol["o"][:, m:m + 1]))
                    b.on("scalar", sc_body)

            with ExitStack() as sctx:
                pxt = [sctx.enter_context(nc.psum_tensor(f"{px}pxt_{c}_{i}", [128, 1024], F32)) for i in range(2)]
                for st in range(4):
                    gt = c * 4 + st
                    xb = x2t[st % 2]
                    ob = outt[st % 2]

                    def s_body(s, c=c, st=st, xb=xb):
                        s.wait(dv)
                        sl.inc(s.dma_start(out=xb[:, :], in_=x2[(c * 4 + st) * 128:(c * 4 + st + 1) * 128, :]), 16)
                    b.on("sync", s_body)

                    def t_body(t, c=c, st=st):
                        t.wait(sc)
                        t.wait(dv)
                        for m in range(8):
                            pe.inc(t.transpose(pxt[st % 2][:, m * 128:(m + 1) * 128],
                                               yT[:, m * 512 + st * 128:m * 512 + (st + 1) * 128],
                                               identf[:, :]))
                    b.on("tensor", t_body)

                    def v_body(v, c=c, st=st, xb=xb, ob=ob, sob_=sob[st % 2]):
                        v.wait(pe)
                        v.wait(sl)
                        v.wait(sob_)
                        dv.inc(v.tensor_add(ob[:, :], pxt[st % 2][:, :], xb[:, :]))
                    b.on("vector", v_body)

                    def s2_body(s, c=c, st=st, ob=ob, sob_=sob[st % 2], gt=gt):
                        s.wait(dv)
                        sob_.inc(s.dma_start(out=out[gt * 128:(gt + 1) * 128, :], in_=ob[:, :]), 16)
                    b.on("sync", s2_body)

        def fin(s):
            s.wait(sob[0])
            s.wait(sob[1])
        b.on("sync", fin)
    return nc



def build_launch1(nc):
    with ExitStack() as octx:
        block = octx.enter_context(nc.Block())
        b = B(nc, octx, block)
        env = {"block": block, "b": b}
        x1ap = nc.dram_tensor("x1", [S, D], F32, kind="ExternalOutput").ap()
        build_attn0(nc, dict(x="x", wq="wq", wk="wk", wv="wv", wo="wo", out=x1ap), env=env)
        b.barrier()
        build_moe(nc, dict(x=x1ap, gate_w="gate_w", lb_bias="lb_bias", out="part"),
                  "partials", env=env)
        b.barrier()
    return nc


def build_launch2(nc):
    with ExitStack() as octx:
        block = octx.enter_context(nc.Block())
        b = B(nc, octx, block)
        env = {"block": block, "b": b}
        x2ap = nc.dram_tensor("x2i", [S, D], F32, kind="Internal").ap()
        x3ap = nc.dram_tensor("x3i", [S, D], F32, kind="Internal").ap()
        build_moe(nc, dict(x="x1", gate_w="gate_w", lb_bias="lb_bias", egw="egw",
                           euw="euw", edw="edw", sgw="sgw", suw="suw", sdw="sdw",
                           Sstack="Sstack", Cstack="Cstack", out=x2ap), "full", env=env)
        b.barrier()
        namesD = dict(x=x2ap, out=x3ap)
        for w in "pqkvo":
            namesD["W" + w] = "W" + w
            namesD["b" + w] = "b" + w
        build_attn1(nc, namesD, env=env)
        b.barrier()
        build_ffn1(nc, dict(x=x3ap, k0="k0", k1="k1", k2="k2", s0="s0", s1="s1",
                            s2="s2", out="out"), env=env)
        b.barrier()
    return nc



NCORES = 8


def kernel(**inputs):
    ii = {k: np.ascontiguousarray(v) for k, v in inputs.items()}
    x = ii["L0_attn_in_features"]
    cores = list(range(NCORES))
    lbr = ii["L0_ffn_lb_bias"].reshape(1, 8)

    nc1 = bass.Bass("TRN2", target_bir_lowering=False, debug=False)
    build_launch1(nc1)
    maps1 = [dict(x=x[i], wq=ii["L0_attn_q_proj_weight"], wk=ii["L0_attn_k_proj_weight"],
                  wv=ii["L0_attn_v_proj_weight"], wo=ii["L0_attn_o_proj_weight"],
                  gate_w=ii["L0_ffn_gate_w"], lb_bias=lbr) for i in cores]
    res1 = run_bass_kernel_spmd(nc1, maps1, core_ids=cores)
    x1 = [res1.results[i]["x1"] for i in cores]
    Sg = np.zeros((8, 8), np.float32)
    Cg = np.zeros((8, 8), np.float32)
    for i in cores:
        p = res1.results[i]["part"][:, 0]
        Sg[:, i] = p[0:8]
        Cg[:, i] = p[8:16]

    s0r = np.asarray(ii["L1_ffn_s0"], np.float32).reshape(1, 1)
    s1r = np.asarray(ii["L1_ffn_s1"], np.float32).reshape(1, 1)
    s2r = np.asarray(ii["L1_ffn_s2"], np.float32).reshape(1, 1)
    nc2 = bass.Bass("TRN2", target_bir_lowering=False, debug=False)
    build_launch2(nc2)
    maps2 = []
    for i in cores:
        m = dict(x1=x1[i], gate_w=ii["L0_ffn_gate_w"], lb_bias=lbr,
                 egw=ii["L0_ffn_expert_gate_w"], euw=ii["L0_ffn_expert_up_w"],
                 edw=ii["L0_ffn_expert_down_w"], sgw=ii["L0_ffn_shared_gate_w"],
                 suw=ii["L0_ffn_shared_up_w"], sdw=ii["L0_ffn_shared_down_w"],
                 Sstack=Sg, Cstack=Cg,
                 k0=ii["L1_ffn_k0"].view(np.uint8), k1=ii["L1_ffn_k1"].view(np.uint8),
                 k2=ii["L1_ffn_k2"].view(np.uint8), s0=s0r, s1=s1r, s2=s2r)
        for w in "pqkvo":
            m["W" + w] = ii["L1_attn_W" + w]
            m["b" + w] = ii["L1_attn_b" + w].reshape(1, D)
        maps2.append(m)
    res2 = run_bass_kernel_spmd(nc2, maps2, core_ids=cores)
    return np.stack([res2.results[i]["out"] for i in cores]).astype(np.float32)


# version 05
# speedup vs baseline: 1.9227x; 1.9227x over previous
import numpy as np
import ml_dtypes
from contextlib import ExitStack

import concourse.bass as bass
import concourse.mybir as mybir
from concourse.bass_utils import run_bass_kernel_spmd

F32 = mybir.dt.float32
BF16 = mybir.dt.bfloat16
U8 = mybir.dt.uint8
AF = mybir.ActivationFunctionType
ALU = mybir.AluOpType
AX = mybir.AxisListType

S = 4096
D = 1024
NT = S // 128
EPS = 1e-5

BF = ml_dtypes.bfloat16



class Ctr:

    def __init__(self, nc, ctx, name):
        self.sem = ctx.enter_context(nc.semaphore(name))
        self.val = 0

    def inc(self, inst, n=1):
        inst.then_inc(self.sem, n)
        self.val += n
        return self.val


class B:

    def __init__(self, nc, ctx, block):
        self.nc = nc
        self.ctx = ctx
        self.block = block
        self.c = {}
        self._last_wait = {}

    def ctr(self, name):
        if name not in self.c:
            self.c[name] = Ctr(self.nc, self.ctx, name)
        return self.c[name]

    def on(self, engine, fn):
        getattr(self.block, engine)(lambda eng: fn(W(self, engine, eng)))

    def barrier(self):
        cs = list(self.c.values())
        for eng in ("sync", "gpsimd", "tensor", "vector", "scalar"):
            def f(w, cs=cs):
                for c in cs:
                    w.wait(c)
            self.on(eng, f)


class W:

    def __init__(self, b, name, eng):
        self.b = b
        self.name = name
        self.eng = eng

    def wait(self, ctr, val=None):
        v = ctr.val if val is None else val
        if v <= 0:
            return
        key = (self.name, id(ctr))
        if self.b._last_wait.get(key, -1) >= v:
            return
        self.eng.wait_ge(ctr.sem, v)
        self.b._last_wait[key] = v

    def __getattr__(self, item):
        return getattr(self.eng, item)



def _dram(nc, names, key, shape, dt, kind):
    v = names[key]
    if isinstance(v, str):
        return nc.dram_tensor(v, shape, dt, kind=kind).ap()
    return v


def build_ffn1(nc, names, env=None, px='f1_'):
    x3 = _dram(nc, names, "x", [S, D], F32, "ExternalInput")
    k0 = _dram(nc, names, "k0", [2 * D, 4096], U8, "ExternalInput")
    k1 = _dram(nc, names, "k1", [4096, 4096], U8, "ExternalInput")
    k2 = _dram(nc, names, "k2", [4096, 1024], U8, "ExternalInput")
    s0 = _dram(nc, names, "s0", [1, 1], F32, "ExternalInput")
    s1 = _dram(nc, names, "s1", [1, 1], F32, "ExternalInput")
    s2 = _dram(nc, names, "s2", [1, 1], F32, "ExternalInput")
    out = _dram(nc, names, "out", [S, D], F32, "ExternalOutput")
    w0p = nc.dram_tensor("w0p_scratch", [8, 128, 4096], BF16, kind="Internal")

    with ExitStack() as ctx:
        e = ctx.enter_context
        onesf = e(nc.sbuf_tensor(f"{px}onesf", [1, 128], F32))
        onecol_bf = e(nc.sbuf_tensor(f"{px}onecol_bf", [128, 1], BF16))
        nhcol = e(nc.sbuf_tensor(f"{px}nhcol", [1, 128], F32))
        nhrow = e(nc.sbuf_tensor(f"{px}nhrow", [1, 1024], F32))
        svals = e(nc.sbuf_tensor(f"{px}svals", [1, 3], F32))
        vec2s0 = e(nc.sbuf_tensor(f"{px}vec2s0", [128, 1], F32))
        vec4s1 = e(nc.sbuf_tensor(f"{px}vec4s1", [128, 1], F32))
        vec4s2 = e(nc.sbuf_tensor(f"{px}vec4s2", [128, 1], F32))
        x3c = [e(nc.sbuf_tensor(f"{px}x3c{i}", [128, 4 * 1024], F32)) for i in range(2)]
        stat = e(nc.sbuf_tensor(f"{px}stat", [128, 12], F32))
        mv = e(nc.sbuf_tensor(f"{px}mv", [128, 2], F32))
        rstd = e(nc.sbuf_tensor(f"{px}rstd", [128, 1], F32))
        lnt = [e(nc.sbuf_tensor(f"{px}lnt{i}", [128, 1024], BF16)) for i in range(2)]
        ln3T = [e(nc.sbuf_tensor(f"{px}ln3T{i}", [128, 8 * 512], BF16, is_transpose_dest=True))
                for i in range(2)]
        r1 = e(nc.sbuf_tensor(f"{px}r1", [128, 32 * 512], BF16))
        r2 = e(nc.sbuf_tensor(f"{px}r2", [128, 32 * 512], BF16))
        NWB = 4
        wb = [e(nc.sbuf_tensor(f"{px}wb{i}", [128, 1024], BF16)) for i in range(NWB)]
        csum1 = e(nc.sbuf_tensor(f"{px}csum1", [1, 512], F32))
        csum2 = e(nc.sbuf_tensor(f"{px}csum2", [1, 512], F32))
        outb = [e(nc.sbuf_tensor(f"{px}outb{i}", [128, 1024], F32)) for i in range(2)]
        PS = e(nc.psum_tensor(f"{px}PS", [128, 4096], F32))

        if env is None:
            block = e(nc.Block())
            b = B(nc, ctx, block)
        else:
            block = env["block"]
            b = env["b"]
        sl = b.ctr("sl"); sg = b.ctr("sg"); pe = b.ctr("pe")
        dv = b.ctr("dv"); sc = b.ctr("sc"); sx = b.ctr("sx")
        sob = [b.ctr("sob0"), b.ctr("sob1")]

        def p0_sync(s):
            sl.inc(s.dma_start(out=svals[0:1, 0:1], in_=s0), 16)
            sl.inc(s.dma_start(out=svals[0:1, 1:2], in_=s1), 16)
            sl.inc(s.dma_start(out=svals[0:1, 2:3], in_=s2), 16)
        b.on("sync", p0_sync)

        def p0_vec(v):
            v.memset(onesf[:, :], 1.0)
            v.memset(onecol_bf[:, :], 1.0)
            v.memset(nhcol[:, :], -0.5)
            dv.inc(v.memset(nhrow[:, :], -0.5))
        b.on("vector", p0_vec)

        for i, (srci, dst, mlt) in enumerate([(0, vec2s0, 2.0), (1, vec4s1, 4.0), (2, vec4s2, 4.0)]):
            def t_body(t, i=i, srci=srci):
                t.wait(sl)
                t.wait(dv)
                pe.inc(t.matmul(PS[:, 0:1], onesf[0:1, :], svals[0:1, srci:srci + 1],
                                start=True, stop=True))
            b.on("tensor", t_body)

            def v_body(v, dst=dst, mlt=mlt):
                v.wait(pe)
                dv.inc(v.tensor_scalar_mul(dst[:, 0:1], PS[:, 0:1], float(mlt)))
            b.on("vector", v_body)

        with ExitStack() as wctx:
            w0a = wctx.enter_context(nc.sbuf_tensor(f"{px}w0a", [128, 4096], BF16))
            w0b = wctx.enter_context(nc.sbuf_tensor(f"{px}w0b", [128, 4096], BF16))
            w0d = wctx.enter_context(nc.sbuf_tensor(f"{px}w0d", [128, 4096], BF16))
            sw = b.ctr("sw")
            for it in range(8):
                def g_body(g, it=it):
                    g.wait(sw)
                    sg.inc(g.dma_start(out=w0a[:, :], in_=k0[it * 128:(it + 1) * 128, :]), 16)
                    sg.inc(g.dma_start(out=w0b[:, :], in_=k0[1024 + it * 128:1024 + (it + 1) * 128, :]), 16)
                b.on("gpsimd", g_body)

                def v_body(v, it=it):
                    v.wait(sg)
                    v.wait(sw)
                    dv.inc(v.tensor_sub(w0d[:, :], w0a[:, :], w0b[:, :]))
                b.on("vector", v_body)

                def s_body(s, it=it):
                    s.wait(dv)
                    sw.inc(s.dma_start(out=w0p.ap()[it, :, :], in_=w0d[:, :]), 16)
                b.on("sync", s_body)

        wload = [0] * NWB
        wcons = [0] * NWB
        bank_ev = [0] * 8
        bank_ev_kind = [None] * 8
        bank_ev[0] = dv.val
        bank_ev_kind[0] = 'dv'
        wslot = [0]
        ln_sx = [0, 0]
        ln_pe_free = [0, 0]
        x3_dv_free = [0, 0]
        sob_val = [0, 0]

        def emit_ln(c):
            cb = c % 2

            def s_body(s, c=c, cb=cb):
                s.wait(dv, x3_dv_free[cb])
                for st in range(4):
                    sl.inc(s.dma_start(out=x3c[cb][:, st * 1024:(st + 1) * 1024],
                                       in_=x3[(c * 4 + st) * 128:(c * 4 + st + 1) * 128, :]), 16)
            b.on("sync", s_body)
            ld_done = sl.val

            for st in range(4):
                lb = lnt[st % 2]

                def v_body(v, c=c, cb=cb, st=st, lb=lb):
                    v.wait(sl, ld_done)
                    v.wait(dv)
                    v.wait(sx)
                    v.bn_stats(stat[:, 0:6], x3c[cb][:, st * 1024:st * 1024 + 512])
                    dv.inc(v.bn_stats(stat[:, 6:12], x3c[cb][:, st * 1024 + 512:st * 1024 + 1024]))
                    v.wait(dv)
                    dv.inc(v.bn_aggr(mv[:, :], stat[:, :]))
                    v.wait(dv)
                    dv.inc(v.tensor_scalar_add(rstd[:, 0:1], mv[:, 1:2], EPS))
                b.on("vector", v_body)

                def s2_body(s):
                    s.wait(dv)
                    sc.inc(s.activation(rstd[:, 0:1], rstd[:, 0:1], AF.Sqrt))
                b.on("scalar", s2_body)

                def v2_body(v, c=c, cb=cb, st=st, lb=lb):
                    v.wait(sc)
                    dv.inc(v.reciprocal(rstd[:, 0:1], rstd[:, 0:1]))
                    v.wait(dv)
                    dv.inc(v.tensor_scalar(lb[:, :], x3c[cb][:, st * 1024:(st + 1) * 1024],
                                           mv[:, 0:1], rstd[:, 0:1], ALU.subtract, ALU.mult))
                b.on("vector", v2_body)
                apply_dv = dv.val

                def s3_body(s, c=c, cb=cb, st=st, lb=lb, apply_dv=apply_dv):
                    s.wait(dv, apply_dv)
                    if st == 0:
                        s.wait(pe, ln_pe_free[cb])
                    for it in range(8):
                        sx.inc(s.dma_start_transpose(
                            ln3T[cb][:, it * 512 + st * 128:it * 512 + (st + 1) * 128],
                            lb[:, it * 128:(it + 1) * 128]), 16)
                b.on("sync", s3_body)
            ln_sx[cb] = sx.val

        def stream_load(src_ap):
            i = wslot[0] % NWB
            wslot[0] += 1

            def g_body(g, i=i, src_ap=src_ap):
                g.wait(pe, wcons[i])
                sg.inc(g.dma_start(out=wb[i][:, :], in_=src_ap), 16)
            b.on("gpsimd", g_body)
            wload[i] = sg.val
            return i

        def emit_stage01(c, cb, stage):
            nk = 8 if stage == 0 else 32
            dst = r1 if stage == 0 else r2
            svec = vec2s0 if stage == 0 else vec4s1
            for mg in range(4):
                stop_pe = [0] * 8
                for kb in range(nk):
                    if stage == 0:
                        i = stream_load(w0p.ap()[kb, :, mg * 1024:(mg + 1) * 1024])
                    else:
                        i = stream_load(k1[kb * 128:(kb + 1) * 128, mg * 1024:(mg + 1) * 1024])

                    def t_body(t, c=c, cb=cb, mg=mg, kb=kb, i=i, nk=nk, stage=stage):
                        t.wait(sg, wload[i])
                        if kb == 0:
                            t.wait(sx, ln_sx[cb])
                            if stage == 1 and mg == 0:
                                t.wait(dv)
                        for j in range(8):
                            if kb == 0:
                                t.wait(*bank_dep(j))
                            q = t.matmul(PS[:, j * 512:(j + 1) * 512],
                                         wb[i][:, j * 128:(j + 1) * 128],
                                         ln3T[cb][:, kb * 512:(kb + 1) * 512] if stage == 0
                                         else r1[:, kb * 512:(kb + 1) * 512],
                                         start=(kb == 0), stop=(kb == nk - 1 and stage == 0))
                            if kb == nk - 1:
                                pe.inc(q)
                                stop_pe[j] = pe.val
                            elif j == 7:
                                pe.inc(q)
                    b.on("tensor", t_body)
                    wcons[i] = pe.val

                if stage == 1:
                    def t2_body(t, c=c, mg=mg):
                        for j in range(8):
                            pe.inc(t.matmul(PS[:, j * 512:(j + 1) * 512], nhcol[0:1, :],
                                            csum1[0:1, :], start=False, stop=True))
                            stop_pe[j] = pe.val
                    b.on("tensor", t2_body)


                for j in range(8):
                    def s_body(s, c=c, mg=mg, j=j, stage=stage):
                        s.wait(pe, stop_pe[j])
                        sc.inc(s.activation(dst[:, (mg * 8 + j) * 512:(mg * 8 + j + 1) * 512],
                                            PS[:, j * 512:(j + 1) * 512], AF.Relu,
                                            scale=svec[:, 0:1]))
                        bank_ev[j] = sc.val
                        bank_ev_kind[j] = 'sc'
                    b.on("scalar", s_body)

        def bank_dep(j):
            if bank_ev_kind[j] == 'sc':
                return (sc, bank_ev[j])
            if bank_ev_kind[j] == 'dv':
                return (dv, bank_ev[j])
            return (pe, 0)

        def emit_colsum(c, src, dstrow, rdy_kind, rdy_val):
            def t_body(t):
                t.wait(rdy_kind, rdy_val)
                t.wait(*bank_dep(0))
                for tt in range(32):
                    q = t.matmul(PS[0:1, 0:512], onecol_bf[:, 0:1],
                                 src[:, tt * 512:(tt + 1) * 512],
                                 start=(tt == 0), stop=(tt == 31))
                pe.inc(q)
            b.on("tensor", t_body)
            mmv = pe.val

            def v_body(v):
                v.wait(pe, mmv)
                dv.inc(v.tensor_copy(dstrow[0:1, :], PS[0:1, 0:512]))
            b.on("vector", v_body)
            bank_ev[0] = dv.val
            bank_ev_kind[0] = 'dv'

        def emit_stage2(c, cb):
            stop_pe = [0] * 8
            for tt in range(32):
                i = stream_load(k2[tt * 128:(tt + 1) * 128, :])

                def t_body(t, c=c, cb=cb, tt=tt, i=i):
                    t.wait(sg, wload[i])
                    for st in range(4):
                        for n in range(2):
                            j = st * 2 + n
                            if tt == 0:
                                t.wait(*bank_dep(j))
                                if j == 0:
                                    t.wait(dv)
                            q = t.matmul(PS[:, j * 512:(j + 1) * 512],
                                         r2[:, tt * 512 + st * 128:tt * 512 + (st + 1) * 128],
                                         wb[i][:, n * 512:(n + 1) * 512],
                                         start=(tt == 0), stop=False)
                            if st == 3 and n == 1:
                                pe.inc(q)
                b.on("tensor", t_body)
                wcons[i] = pe.val

            def t2_body(t, c=c):
                for st in range(4):
                    for n in range(2):
                        j = st * 2 + n
                        pe.inc(t.matmul(PS[:, j * 512:(j + 1) * 512],
                                        csum2[0:1, st * 128:(st + 1) * 128],
                                        nhrow[0:1, n * 512:(n + 1) * 512],
                                        start=False, stop=True))
                        stop_pe[j] = pe.val
            b.on("tensor", t2_body)

            for st in range(4):
                ob = outb[st % 2]

                def v_body(v, c=c, cb=cb, st=st, ob=ob):
                    v.wait(pe, stop_pe[st * 2 + 1])
                    v.wait(sob[st % 2], sob_val[st % 2])
                    for n in range(2):
                        j = st * 2 + n
                        dv.inc(v.scalar_tensor_tensor(
                            ob[:, n * 512:(n + 1) * 512], PS[:, j * 512:(j + 1) * 512],
                            vec4s2[:, 0:1],
                            x3c[cb][:, st * 1024 + n * 512:st * 1024 + (n + 1) * 512],
                            ALU.mult, ALU.add))
                        bank_ev[j] = dv.val
                        bank_ev_kind[j] = 'dv'
                b.on("vector", v_body)

                def s_body(s, c=c, st=st, ob=ob):
                    s.wait(dv)
                    sob[st % 2].inc(s.dma_start(out=out[(c * 4 + st) * 128:(c * 4 + st + 1) * 128, :],
                                                in_=ob[:, :]), 16)
                    sob_val[st % 2] = sob[st % 2].val
                b.on("sync", s_body)
            x3_dv_free[cb] = dv.val

        emit_ln(0)
        for c in range(8):
            cb = c % 2
            emit_stage01(c, cb, 0)
            ln_pe_free[cb] = pe.val
            emit_colsum(c, r1, csum1, sc, sc.val)
            emit_stage01(c, cb, 1)
            emit_colsum(c, r2, csum2, sc, sc.val)
            emit_stage2(c, cb)
            if c + 1 < 8:
                emit_ln(c + 1)

        def fin(s):
            s.wait(sob[0])
            s.wait(sob[1])
        b.on("sync", fin)
    return nc

def build_attn0(nc, names, env=None, px='a0_'):
    x0 = _dram(nc, names, "x", [S, D], F32, "ExternalInput")
    Wd = {w: _dram(nc, names, w, [D, D], F32, "ExternalInput")
          for w in ("wq", "wk", "wv", "wo")}
    out = _dram(nc, names, "out", [S, D], F32, "ExternalOutput")
    ln0h = nc.dram_tensor("ln0T_scratch", [8, 128, S], BF16, kind="Internal")
    yph = nc.dram_tensor("yp_scratch", [8, 128, S], F32, kind="Internal")

    masks_np = np.zeros((128, 4 * 1024), dtype=BF)
    for p in range(4):
        for kl in range(128):
            q0 = kl + 128 * p
            if q0 < 512:
                masks_np[kl, p * 1024 + q0:p * 1024 + 512] = 1.0
                masks_np[kl, p * 1024 + 512 + q0:p * 1024 + 1024] = 1.0

    with ExitStack() as ctx:
        e = ctx.enter_context
        ident = e(nc.sbuf_tensor(f"{px}ident", [128, 128], BF16))
        identf = e(nc.sbuf_tensor(f"{px}identf", [128, 128], F32))
        onesf = e(nc.sbuf_tensor(f"{px}onesf", [128, 64], F32))
        WT = {w: e(nc.sbuf_tensor(f"{px}WT{w}", [128, 8192], BF16)) for w in ("wq", "wk", "wv", "wo")}
        PSbox = [None]

        def psAB_ap(i, c0, c1):
            return PSbox[0][:, i * 1024 + c0:i * 1024 + c1]

        def pq_ap():
            return PSbox[0][:, 3584:4096]

        if env is None:
            block = e(nc.Block())
            b = B(nc, ctx, block)
        else:
            block = env["block"]
            b = env["b"]
        sl = b.ctr("sl"); sg = b.ctr("sg"); pe = b.ctr("pe")
        dv = b.ctr("dv"); sc = b.ctr("sc"); sp = b.ctr("sp"); sy = b.ctr("sy")
        sx = b.ctr("sx")
        sob = [b.ctr("sob0"), b.ctr("sob1")]
        soo = [b.ctr("soo0"), b.ctr("soo1")]

        identh = nc.inline_tensor(np.eye(128, dtype=BF), f"{px}identnp")
        identfh = nc.inline_tensor(np.eye(128, dtype=np.float32), f"{px}identfnp")
        masksh = nc.inline_tensor(masks_np, f"{px}masksnp")

        def a0_sync(s):
            sl.inc(s.dma_start(out=ident[:, :], in_=identh.ap()), 16)
            sl.inc(s.dma_start(out=identf[:, :], in_=identfh.ap()), 16)
        b.on("sync", a0_sync)

        def a0_vec(v):
            dv.inc(v.memset(onesf[:, :], 1.0))
        b.on("vector", a0_vec)

        with ExitStack() as actx:
            wstage = actx.enter_context(nc.sbuf_tensor(f"{px}wstage", [128, 8192], BF16))
            ln0T = actx.enter_context(nc.sbuf_tensor(f"{px}ln0T", [128, 8 * 4096], BF16,
                                                     is_transpose_dest=True))
            ptw = actx.enter_context(nc.psum_tensor(f"{px}ptw", [128, 1024], BF16))
            wcons_pe = [0]
            for wi, w in enumerate(("wq", "wk", "wv", "wo")):
                def g_body(g, w=w, wi=wi):
                    g.wait(pe, wcons_pe[0])
                    sg.inc(g.dma_start(out=wstage[:, :].rearrange("p (t i) -> p t i", t=8),
                                       in_=Wd[w].rearrange("(t p) i -> p t i", p=128)), 16)
                b.on("gpsimd", g_body)
                ld = sg.val

                for u in range(8):
                    def t_body(t, w=w, u=u, ld=ld):
                        t.wait(sg, ld)
                        t.wait(sl)
                        if u > 0:
                            t.wait(dv)
                        for tt in range(8):
                            q = t.transpose(ptw[:, tt * 128:(tt + 1) * 128],
                                            wstage[:, tt * 1024 + u * 128:tt * 1024 + (u + 1) * 128],
                                            ident[:, :])
                        pe.inc(q)
                    b.on("tensor", t_body)

                    def v_body(v, w=w, u=u):
                        v.wait(pe)
                        dv.inc(v.tensor_copy(WT[w][:, u * 1024:(u + 1) * 1024], ptw[:, :]))
                    b.on("vector", v_body)
                wcons_pe[0] = pe.val

            mvg = actx.enter_context(nc.sbuf_tensor(f"{px}mvg", [128, 8], F32))
            rstdg = actx.enter_context(nc.sbuf_tensor(f"{px}rstdg", [128, 4], F32))
            statg = actx.enter_context(nc.sbuf_tensor(f"{px}statg", [128, 48], F32))
            xg = [actx.enter_context(nc.sbuf_tensor(f"{px}xg{i}", [128, 4 * 1024], F32)) for i in range(2)]
            lng = [actx.enter_context(nc.sbuf_tensor(f"{px}lng{i}", [128, 4 * 1024], BF16)) for i in range(2)]
            ln_apl = [0, 0]
            ln_sx = [0, 0]
            for g in range(8):
                gb = g % 2

                def s_body(s, g=g, gb=gb):
                    s.wait(dv, ln_apl[gb])
                    for u in range(4):
                        st = g * 4 + u
                        sl.inc(s.dma_start(out=xg[gb][:, u * 1024:(u + 1) * 1024],
                                           in_=x0[st * 128:(st + 1) * 128, :]), 16)
                b.on("sync", s_body)
                ld = sl.val

                def v_body(v, g=g, gb=gb, ld=ld):
                    v.wait(sl, ld)
                    v.wait(dv)
                    for u in range(4):
                        v.bn_stats(statg[:, u * 12:u * 12 + 6], xg[gb][:, u * 1024:u * 1024 + 512])
                        v.bn_stats(statg[:, u * 12 + 6:u * 12 + 12], xg[gb][:, u * 1024 + 512:u * 1024 + 1024])
                    dv.inc(v.engine_nop())
                    v.wait(dv)
                    for u in range(4):
                        v.bn_aggr(mvg[:, u * 2:u * 2 + 2], statg[:, u * 12:u * 12 + 12])
                    dv.inc(v.engine_nop())
                    v.wait(dv)
                    for u in range(4):
                        q = v.tensor_scalar_add(rstdg[:, u:u + 1], mvg[:, u * 2 + 1:u * 2 + 2], EPS)
                    dv.inc(q)
                b.on("vector", v_body)

                def s2_body(s, g=g):
                    s.wait(dv)
                    s.activation(rstdg[:, :], rstdg[:, :], AF.Ln)
                    sc.inc(s.activation(rstdg[:, :], rstdg[:, :], AF.Exp, scale=-0.5))
                b.on("scalar", s2_body)

                def v2_body(v, g=g, gb=gb):
                    v.wait(sc)
                    v.wait(pe, ln_sx[gb])
                    for u in range(4):
                        q = v.tensor_scalar(lng[gb][:, u * 1024:(u + 1) * 1024],
                                            xg[gb][:, u * 1024:(u + 1) * 1024],
                                            mvg[:, u * 2:u * 2 + 1], rstdg[:, u:u + 1],
                                            ALU.subtract, ALU.mult)
                    dv.inc(q)
                b.on("vector", v2_body)
                ln_apl[gb] = dv.val
                apd = dv.val

                for u in range(4):
                    st = g * 4 + u

                    def t3_body(t, g=g, gb=gb, u=u, st=st, apd=apd):
                        t.wait(dv, apd)
                        t.wait(dv)
                        for it in range(8):
                            q = t.transpose(ptw[:, it * 128:(it + 1) * 128],
                                            lng[gb][:, u * 1024 + it * 128:u * 1024 + (it + 1) * 128],
                                            ident[:, :])
                        pe.inc(q)
                    b.on("tensor", t3_body)
                    trv = pe.val

                    def v3_body(v, g=g, st=st, trv=trv):
                        v.wait(pe, trv)
                        dst = bass.AP(ln0T, st * 128, [[8 * 4096, 128], [4096, 8], [1, 128]])
                        srcap = bass.AP(ptw, 0, [[1024, 128], [128, 8], [1, 128]])
                        dv.inc(v.tensor_copy(dst, srcap))
                    b.on("vector", v3_body)
                ln_sx[gb] = pe.val

            def spill(s):
                s.wait(dv)
                for it in range(8):
                    sp.inc(s.dma_start(out=ln0h.ap()[it, :, :], in_=ln0T[:, it * 4096:(it + 1) * 4096]), 16)
            b.on("sync", spill)

        PS = e(nc.psum_tensor(f"{px}PS", [128, 4096], F32))
        PSbox[0] = PS
        a2ctx = ExitStack()
        a2 = a2ctx.enter_context
        masks = a2(nc.sbuf_tensor(f"{px}masks", [128, 4 * 1024], BF16))
        KTg = a2(nc.sbuf_tensor(f"{px}KTg", [128, 4 * 4096], BF16))
        Vg = a2(nc.sbuf_tensor(f"{px}Vg", [128, 32 * 520], BF16))
        lncb = [a2(nc.sbuf_tensor(f"{px}lncb{i}", [128, 8 * 512], BF16)) for i in range(3)]
        qcb = a2(nc.sbuf_tensor(f"{px}qcb", [128, 4 * 512], BF16))
        ab = [a2(nc.sbuf_tensor(f"{px}ab{i}", [128, 1024], BF16)) for i in range(3)]
        brs2 = [a2(nc.sbuf_tensor(f"{px}brs2_{i}", [1, 1024], F32)) for i in range(2)]
        pvn = [a2(nc.sbuf_tensor(f"{px}pvn{i}", [65, 1024], F32)) for i in range(2)]
        brd = a2(nc.sbuf_tensor(f"{px}brd", [64, 512], F32))
        attnoT = a2(nc.sbuf_tensor(f"{px}attnoT", [128, 4 * 512], BF16))
        ybuf = [a2(nc.sbuf_tensor(f"{px}ybuf{i}", [128, 512], F32)) for i in range(2)]
        y0b = [a2(nc.sbuf_tensor(f"{px}y0b{i}", [128, 512], F32)) for i in range(2)]

        def mload(s):
            s.wait(sx)
            sl.inc(s.dma_start(out=masks[:, :], in_=masksh.ap()), 16)
        b.on("sync", mload)
        masks_ld = sl.val

        def ln_chunk_load(s, buf, n):
            sl.inc(s.dma_start(out=buf[:, :].rearrange("p (it w) -> p it w", it=8),
                               in_=ln0h.ap()[:, :, n * 512:(n + 1) * 512].rearrange("it p w -> p it w")), 16)
            return sl.val

        pvev = [0]
        recv_of = [0] * 4
        pbr_free = [0]
        abst = [0, 0, 0]
        psst = [[0, 'x'], [0, 'x']]
        lnld = [0, 0, 0]
        lncons = [0, 0, 0]
        qev = [0]
        pq_free = [0, 'p']
        yspill = [0, 0]
        y0ld = [0, 0]

        for gp in range(2):
            def v_init(v, gp=gp):
                v.wait(pe)
                v.wait(sp)
                dv.inc(v.memset(Vg[:, :], 1.0))
            b.on("vector", v_init)
            vg_init_dv = dv.val

            kv_ev = [0, 0]
            kv_kind = ['p', 'p']
            for n in range(8):
                buf = lncb[n % 3]

                def s_body(s, gp=gp, n=n, buf=buf):
                    s.wait(pe, lncons[n % 3])
                    if gp == 0 and n == 0:
                        s.wait(sp)
                    lnld[n % 3] = ln_chunk_load(s, buf, n)
                b.on("sync", s_body)

                for j in range(4):
                    hf = j % 2

                    def t_body(t, gp=gp, n=n, j=j, buf=buf, hf=hf):
                        t.wait(sl, lnld[n % 3])
                        if kv_kind[hf] == 's':
                            t.wait(sc, kv_ev[hf])
                        elif kv_kind[hf] == 'd':
                            t.wait(dv, kv_ev[hf])
                        for it in range(8):
                            q = t.matmul(PS[:, hf * 512:(hf + 1) * 512],
                                         WT["wk"][:, it * 1024 + gp * 512 + j * 128:it * 1024 + gp * 512 + (j + 1) * 128],
                                         buf[:, it * 512:(it + 1) * 512],
                                         start=(it == 0), stop=(it == 7))
                        pe.inc(q)
                    b.on("tensor", t_body)
                    mmv = pe.val
                    lncons[n % 3] = pe.val

                    def vc_body(v, gp=gp, n=n, j=j, hf=hf, mmv=mmv):
                        v.wait(pe, mmv)
                        dv.inc(v.tensor_copy(KTg[:, j * 4096 + n * 512:j * 4096 + (n + 1) * 512],
                                             PS[:, hf * 512:(hf + 1) * 512]))
                        kv_ev[hf] = dv.val
                        kv_kind[hf] = 'd'
                    b.on("vector", vc_body)

                for m4 in range(4):
                    stile = n * 4 + m4
                    hf = m4 % 2

                    def t_body(t, gp=gp, n=n, m4=m4, buf=buf, stile=stile, hf=hf):
                        if kv_kind[hf] == 's':
                            t.wait(sc, kv_ev[hf])
                        elif kv_kind[hf] == 'd':
                            t.wait(dv, kv_ev[hf])
                        if gp == 0 and n == 0 and m4 == 0:
                            t.wait(dv, vg_init_dv)
                        for it in range(8):
                            q = t.matmul(PS[:, hf * 512:(hf + 1) * 512],
                                         buf[:, it * 512 + m4 * 128:it * 512 + (m4 + 1) * 128],
                                         WT["wv"][:, it * 1024 + gp * 512:it * 1024 + (gp + 1) * 512],
                                         start=(it == 0), stop=(it == 7))
                        pe.inc(q)
                    b.on("tensor", t_body)
                    mmv = pe.val
                    lncons[n % 3] = pe.val

                    def v_body(v, gp=gp, stile=stile, hf=hf, mmv=mmv):
                        v.wait(pe, mmv)
                        v.wait(dv, vg_init_dv)
                        dst = bass.AP(Vg, stile * 520, [[32 * 520, 128], [65, 8], [1, 64]])
                        srcap = bass.AP(PS, hf * 512, [[4096, 128], [64, 8], [1, 64]])
                        dv.inc(v.tensor_copy(dst, srcap))
                        kv_ev[hf] = dv.val
                        kv_kind[hf] = 'd'
                    b.on("vector", v_body)

            kv_done_sc = sc.val
            kv_done_dv = dv.val

            for qc in range(8):
                buf = lncb[qc % 3]

                def s_body(s, gp=gp, qc=qc, buf=buf):
                    s.wait(pe, lncons[qc % 3])
                    lnld[qc % 3] = ln_chunk_load(s, buf, qc)
                b.on("sync", s_body)

                for j in range(4):
                    def t_q(t, gp=gp, qc=qc, j=j, buf=buf):
                        t.wait(sl, lnld[qc % 3])
                        if pq_free[1] == 's':
                            t.wait(sc, pq_free[0])
                        elif pq_free[1] == 'd':
                            t.wait(dv, pq_free[0])
                        if j == 0 and qc == 0:
                            t.wait(sc, kv_done_sc)
                            t.wait(dv, kv_done_dv)
                        for it in range(8):
                            q = t.matmul(pq_ap(),
                                         WT["wq"][:, it * 1024 + gp * 512 + j * 128:it * 1024 + gp * 512 + (j + 1) * 128],
                                         buf[:, it * 512:(it + 1) * 512],
                                         start=(it == 0), stop=(it == 7))
                        pe.inc(q)
                    b.on("tensor", t_q)
                    mmv = pe.val
                    lncons[qc % 3] = pe.val

                    def vc_q(v, gp=gp, qc=qc, j=j, mmv=mmv):
                        v.wait(pe, mmv)
                        dv.inc(v.tensor_copy(qcb[:, j * 512:(j + 1) * 512], pq_ap()))
                        pq_free[0] = dv.val
                        pq_free[1] = 'd'
                    b.on("vector", vc_q)
                qev[0] = dv.val

                nkb = 4 * (qc + 1)
                for j in range(4):
                    av_stop = [0]
                    sc_base = sc.val
                    dv_base = dv.val
                    smm = [0] * nkb
                    ab_wait = [0] * nkb
                    mdv_of = {}

                    def emit_av(kb, pbk, abk, diag, mdv, expv):
                        def t_av(t, gp=gp, qc=qc, j=j, kb=kb, pbk=pbk, abk=abk,
                                 diag=diag, mdv=mdv, expv=expv):
                            if diag:
                                t.wait(dv, mdv)
                            else:
                                t.wait(sc, expv)
                            if kb == 0:
                                t.wait(dv, pvev[0])
                            aA = ab[abk]
                            t.matmul(PS[0:65, 2048:2560], Vg[:, kb * 520 + 2 * j * 65:kb * 520 + 2 * j * 65 + 65],
                                     aA[:, 0:512], start=(kb == 0), stop=(kb == nkb - 1))
                            q = t.matmul(PS[0:65, 2560:3072], Vg[:, kb * 520 + (2 * j + 1) * 65:kb * 520 + (2 * j + 1) * 65 + 65],
                                         aA[:, 512:1024], start=(kb == 0), stop=(kb == nkb - 1))
                            pe.inc(q)
                            abst[abk] = pe.val
                        b.on("tensor", t_av)
                        av_stop[0] = pe.val

                    ndiag = [0]
                    pend = [None]
                    for kb in range(nkb):
                        pbk = kb % 2
                        abk = kb % 3
                        diag = (kb >= 4 * qc)
                        expv = sc_base + kb + 1
                        if diag:
                            ndiag[0] += 1
                            mdv_of[kb] = dv_base + ndiag[0]

                        def t_s(t, gp=gp, qc=qc, j=j, kb=kb, pbk=pbk, expv=expv):
                            t.wait(dv, qev[0])
                            if psst[pbk][1] == 's':
                                t.wait(sc, psst[pbk][0])
                            t.matmul(psAB_ap(pbk, 0, 512), KTg[0:64, j * 4096 + kb * 128:j * 4096 + (kb + 1) * 128],
                                     qcb[0:64, j * 512:(j + 1) * 512], start=True, stop=True,
                                     tile_position=(0, 0))
                            q = t.matmul(psAB_ap(pbk, 512, 1024), KTg[64:128, j * 4096 + kb * 128:j * 4096 + (kb + 1) * 128],
                                         qcb[64:128, j * 512:(j + 1) * 512], start=True, stop=True,
                                         tile_position=(64, 0))
                            pe.inc(q)
                        b.on("tensor", t_s)
                        smm[kb] = pe.val
                        psst[pbk][0] = sc_base + kb + 1
                        psst[pbk][1] = 's'
                        ab_wait[kb] = abst[abk]

                        if pend[0] is not None:
                            emit_av(*pend[0])
                        pend[0] = (kb, pbk, abk, diag, mdv_of.get(kb, 0), expv)
                    emit_av(*pend[0])
                    pend[0] = None

                    def v_m_all(v, gp=gp, qc=qc, j=j, nkb=nkb):
                        v.wait(sl, masks_ld)
                        for kb in range(4 * qc, nkb):
                            p_ = kb - 4 * qc
                            aA = ab[kb % 3]
                            v.wait(sc, sc_base + kb + 1)
                            dv.inc(v.tensor_mul(aA[:, :], aA[:, :], masks[:, p_ * 1024:(p_ + 1) * 1024]))
                    b.on("vector", v_m_all)

                    def sc_all(s, gp=gp, qc=qc, j=j, nkb=nkb):
                        for kb in range(nkb):
                            pbk = kb % 2
                            aA = ab[kb % 3]
                            s.wait(pe, smm[kb])
                            s.wait(pe, ab_wait[kb])
                            sc.inc(s.activation(aA[:, :], psAB_ap(pbk, 0, 1024), AF.Exp, scale=0.125))
                    b.on("scalar", sc_all)
                    assert sc.val == sc_base + nkb
                    assert dv.val == dv_base + ndiag[0]

                    def v_ev(v, gp=gp, qc=qc, j=j):
                        v.wait(pe, av_stop[0])
                        dv.inc(v.tensor_copy(pvn[j % 2][0:65, :], PS[0:65, 2048:3072]))
                        pvev[0] = dv.val
                        v.wait(dv)
                        v.reciprocal(brs2[j % 2][0:1, 0:512], pvn[j % 2][64:65, 0:512])
                        dv.inc(v.reciprocal(brs2[j % 2][0:1, 512:1024], pvn[j % 2][64:65, 512:1024]))
                    b.on("vector", v_ev)
                    recv_of[j] = dv.val

                    def emit_norm(jj):
                        for hh in range(2):
                            def t_n(t, gp=gp, qc=qc, jj=jj, hh=hh):
                                t.wait(dv, recv_of[jj])
                                t.wait(dv, pbr_free[0])
                                q = t.matmul(PS[0:64, 3072:3584], onesf[0:1, 0:64],
                                             brs2[jj % 2][0:1, hh * 512:(hh + 1) * 512],
                                             start=True, stop=True)
                                pe.inc(q)
                            b.on("tensor", t_n)
                            bcv = pe.val

                            def v_n2(v, gp=gp, qc=qc, jj=jj, hh=hh, bcv=bcv):
                                v.wait(pe, bcv)
                                dv.inc(v.tensor_copy(brd[:, :], PS[0:64, 3072:3584]))
                                pbr_free[0] = dv.val
                                v.wait(dv)
                                base = jj * 512
                                if hh == 0:
                                    dv.inc(v.tensor_mul(attnoT[0:64, base:base + 512],
                                                        pvn[jj % 2][0:64, 0:512], brd[0:64, :]))
                                else:
                                    dv.inc(v.tensor_mul(attnoT[64:128, base:base + 512],
                                                        pvn[jj % 2][0:64, 512:1024], brd[0:64, :]))
                            b.on("vector", v_n2)

                    if j > 0:
                        emit_norm(j - 1)
                    if j == 3:
                        emit_norm(3)

                at_dv = dv.val
                for m in range(8):
                    yb = ybuf[m % 2]
                    obk = m % 2

                    def t_o(t, gp=gp, qc=qc, m=m, at_dv=at_dv, obk=obk):
                        t.wait(dv, at_dv)
                        if psst[obk][1] == 's':
                            t.wait(sc, psst[obk][0])
                        elif psst[obk][1] == 'd':
                            t.wait(dv, psst[obk][0])
                        for itv in range(4):
                            q = t.matmul(PS[:, obk * 512:(obk + 1) * 512],
                                         WT["wo"][:, (4 * gp + itv) * 1024 + m * 128:(4 * gp + itv) * 1024 + (m + 1) * 128],
                                         attnoT[:, itv * 512:(itv + 1) * 512],
                                         start=(itv == 0), stop=(itv == 3))
                        pe.inc(q)
                    b.on("tensor", t_o)
                    omv = pe.val

                    if gp == 0:
                        def v_o(v, qc=qc, m=m, yb=yb, omv=omv, obk=obk):
                            v.wait(pe, omv)
                            v.wait(sy, yspill[m % 2])
                            dv.inc(v.tensor_copy(yb[:, :], PS[:, obk * 512:(obk + 1) * 512]))
                            psst[obk][0] = dv.val
                            psst[obk][1] = 'd'
                        b.on("vector", v_o)

                        def s_o(s, qc=qc, m=m, yb=yb):
                            s.wait(dv)
                            sy.inc(s.dma_start(out=yph.ap()[m, :, qc * 512:(qc + 1) * 512], in_=yb[:, :]), 16)
                            yspill[m % 2] = sy.val
                        b.on("sync", s_o)
                    else:
                        y0 = y0b[m % 2]

                        def s_o0(s, qc=qc, m=m, y0=y0):
                            s.wait(dv)
                            sy.inc(s.dma_start(out=y0[:, :], in_=yph.ap()[m, :, qc * 512:(qc + 1) * 512]), 16)
                            y0ld[m % 2] = sy.val
                        b.on("sync", s_o0)

                        def v_o(v, qc=qc, m=m, yb=yb, y0=y0, omv=omv, obk=obk):
                            v.wait(pe, omv)
                            v.wait(sy, y0ld[m % 2])
                            v.wait(sob[m % 2])
                            dv.inc(v.tensor_add(yb[:, :], PS[:, obk * 512:(obk + 1) * 512], y0[:, :]))
                            psst[obk][0] = dv.val
                            psst[obk][1] = 'd'
                        b.on("vector", v_o)

                        def s_o(s, qc=qc, m=m, yb=yb):
                            s.wait(dv)
                            sob[m % 2].inc(s.dma_start(out=yph.ap()[m, :, qc * 512:(qc + 1) * 512], in_=yb[:, :]), 16)
                        b.on("sync", s_o)

        a2ctx.close()
        x0t = [e(nc.sbuf_tensor(f"{px}x0t{i}", [128, 1024], F32)) for i in range(2)]
        ytb = [e(nc.sbuf_tensor(f"{px}ytb{i}", [128, 1024], F32)) for i in range(2)]
        outt = [e(nc.sbuf_tensor(f"{px}outt{i}", [128, 1024], F32)) for i in range(2)]
        pxt = [PS[:, 0:1024], PS[:, 1024:2048]]
        ep_pe = [0, 0]
        ep_dv = [0, 0]
        for st in range(NT):
            xb = x0t[st % 2]
            yt = ytb[st % 2]
            ob = outt[st % 2]

            def s_body(s, st=st, xb=xb, yt=yt):
                s.wait(dv, ep_dv[st % 2])
                if st < 2:
                    s.wait(sob[0]); s.wait(sob[1])
                    s.wait(pe); s.wait(dv); s.wait(sc)
                sl.inc(s.dma_start(out=xb[:, :], in_=x0[st * 128:(st + 1) * 128, :]), 16)
                sl.inc(s.dma_start(out=yt[:, :].rearrange("p (m w) -> p m w", m=8),
                                   in_=yph.ap()[:, :, st * 128:(st + 1) * 128].rearrange("m p w -> p m w")), 16)
            b.on("sync", s_body)
            ld = sl.val

            def t_body(t, st=st, yt=yt, ld=ld):
                t.wait(sl, ld)
                t.wait(dv, ep_dv[st % 2])
                if st < 2:
                    t.wait(sc)
                    t.wait(dv)
                for m in range(8):
                    q = t.transpose(pxt[st % 2][:, m * 128:(m + 1) * 128],
                                    yt[:, m * 128:(m + 1) * 128], identf[:, :])
                pe.inc(q)
            b.on("tensor", t_body)
            ep_pe[st % 2] = pe.val

            def v_body(v, st=st, xb=xb, ob=ob):
                v.wait(pe, ep_pe[st % 2])
                v.wait(soo[st % 2])
                dv.inc(v.tensor_add(ob[:, :], pxt[st % 2][:, :], xb[:, :]))
                ep_dv[st % 2] = dv.val
            b.on("vector", v_body)

            def s2_body(s, st=st, ob=ob):
                s.wait(dv, ep_dv[st % 2])
                soo[st % 2].inc(s.dma_start(out=out[st * 128:(st + 1) * 128, :], in_=ob[:, :]), 16)
            b.on("sync", s2_body)

        def fin(s):
            s.wait(soo[0]); s.wait(soo[1])
        b.on("sync", fin)
    return nc

def build_moe(nc, names, mode, env=None, px='mo_'):
    x1 = _dram(nc, names, "x", [S, D], F32, "ExternalInput")
    gw = _dram(nc, names, "gate_w", [8, D], F32, "ExternalInput")
    lb = _dram(nc, names, "lb_bias", [1, 8], F32, "ExternalInput")
    if mode == "full":
        egw = _dram(nc, names, "egw", [8, 512, D], F32, "ExternalInput")
        euw = _dram(nc, names, "euw", [8, 512, D], F32, "ExternalInput")
        edw = _dram(nc, names, "edw", [8, D, 512], F32, "ExternalInput")
        sgw = _dram(nc, names, "sgw", [D, D], F32, "ExternalInput")
        suw = _dram(nc, names, "suw", [D, D], F32, "ExternalInput")
        sdw = _dram(nc, names, "sdw", [D, D], F32, "ExternalInput")
        Sst = _dram(nc, names, "Sstack", [8, 8], F32, "ExternalInput")
        Cst = _dram(nc, names, "Cstack", [8, 8], F32, "ExternalInput")
        out = _dram(nc, names, "out", [S, D], F32, "ExternalOutput")
        gwT_h = nc.dram_tensor("gwT_scratch", [8, 128, 4096], BF16, kind="Internal")
        uwT_h = nc.dram_tensor("uwT_scratch", [8, 128, 4096], BF16, kind="Internal")
        dwT_h = nc.dram_tensor("dwT_scratch", [8, 128, 4096], BF16, kind="Internal")
        sgwT_h = nc.dram_tensor("sgwT_scratch", [128, 8192], BF16, kind="Internal")
        suwT_h = nc.dram_tensor("suwT_scratch", [128, 8192], BF16, kind="Internal")
        sdwT_h = nc.dram_tensor("sdwT_scratch", [128, 8192], BF16, kind="Internal")
    else:
        part = _dram(nc, names, "out", [16, 1], F32, "ExternalOutput")

    iota_np = np.broadcast_to(np.arange(8, dtype=np.float32), (128, 8)).copy()

    with ExitStack() as ctx:
        e = ctx.enter_context
        ident = e(nc.sbuf_tensor(f"{px}ident", [128, 128], BF16))
        identf = e(nc.sbuf_tensor(f"{px}identf", [128, 128], F32))
        onesf = e(nc.sbuf_tensor(f"{px}onesf", [1, 128], F32))
        onescol = e(nc.sbuf_tensor(f"{px}onescol", [128, 1], F32))
        iot = e(nc.sbuf_tensor(f"{px}iot", [128, 8], F32))
        iop8 = e(nc.sbuf_tensor(f"{px}iop8", [128, 8], F32))
        gws = e(nc.sbuf_tensor(f"{px}gws", [128, 1024], F32))
        gwsum = e(nc.sbuf_tensor(f"{px}gwsum", [128, 1], F32))
        gwTs = e(nc.sbuf_tensor(f"{px}gwTs", [128, 64], F32))
        lbB = e(nc.sbuf_tensor(f"{px}lbB", [128, 8], F32))
        gwsumB = e(nc.sbuf_tensor(f"{px}gwsumB", [128, 8], F32))
        meanA = e(nc.sbuf_tensor(f"{px}meanA", [128, NT], F32))
        rstdA = e(nc.sbuf_tensor(f"{px}rstdA", [128, NT], F32))
        coefA = e(nc.sbuf_tensor(f"{px}coefA", [128, NT * 8], F32))
        partsb = e(nc.sbuf_tensor(f"{px}partsb", [16, 1], F32))
        t8row = e(nc.sbuf_tensor(f"{px}t8row", [1, 16], F32))

        if env is None:
            block = e(nc.Block())
            b = B(nc, ctx, block)
        else:
            block = env["block"]
            b = env["b"]
        sl = b.ctr("sl"); sg = b.ctr("sg"); pe = b.ctr("pe")
        dv = b.ctr("dv"); sc = b.ctr("sc"); so = b.ctr("so"); sw = b.ctr("sw")

        identh = nc.inline_tensor(np.eye(128, dtype=BF), f"{px}identnp")
        identfh = nc.inline_tensor(np.eye(128, dtype=np.float32), f"{px}identfnp")
        ioth = nc.inline_tensor(iota_np, f"{px}iotanp")
        iop8h = nc.inline_tensor(iota_np + 8.0, f"{px}iop8np")

        def g0_vec0(v):
            v.memset(onesf[:, :], 1.0)
            v.memset(onescol[:, :], 1.0)
            dv.inc(v.memset(gws[:, :], 0.0))
        b.on("vector", g0_vec0)

        def g0_sync(s):
            sl.inc(s.dma_start(out=ident[:, :], in_=identh.ap()), 16)
            sl.inc(s.dma_start(out=identf[:, :], in_=identfh.ap()), 16)
            sl.inc(s.dma_start(out=iot[:, :], in_=ioth.ap()), 16)
            sl.inc(s.dma_start(out=iop8[:, :], in_=iop8h.ap()), 16)
            s.wait(dv)
            sl.inc(s.dma_start(out=gws[0:8, :], in_=gw), 16)
            sl.inc(s.dma_start(out=t8row[0:1, 8:16], in_=lb), 16)
        b.on("sync", g0_sync)

        def g0_vec(v):
            v.wait(sl)
            dv.inc(v.reduce_sum(gwsum[0:8, :], gws[0:8, :], axis=AX.X))
        b.on("vector", g0_vec)

        with nc.psum_tensor(f"{px}ps0", [128, 1024], F32) as ps0:
            for it in range(8):
                def t_body(t, it=it):
                    t.wait(sl)
                    t.wait(dv)
                    pe.inc(t.transpose(ps0[:, 0:128], gws[:, it * 128:(it + 1) * 128], identf[:, :]))
                b.on("tensor", t_body)
                mmv = pe.val

                def v_body(v, it=it, mmv=mmv):
                    v.wait(pe, mmv)
                    dv.inc(v.tensor_copy(gwTs[:, it * 8:(it + 1) * 8], ps0[:, 0:8]))
                b.on("vector", v_body)

            def t2_body(t):
                t.wait(dv)
                pe.inc(t.matmul(ps0[0:1, 512:520], gwsum[0:8, 0:1], identf[0:8, 0:8],
                                start=True, stop=True))
            b.on("tensor", t2_body)
            mm2 = pe.val

            def v2_body(v, mm2=mm2):
                v.wait(pe, mm2)
                dv.inc(v.tensor_copy(t8row[0:1, 0:8], ps0[0:1, 512:520]))
            b.on("vector", v2_body)

            def t3_body(t):
                t.wait(dv)
                t.wait(sl)
                t.matmul(ps0[:, 512:520], onesf[0:1, :], t8row[0:1, 0:8], start=True, stop=True)
                pe.inc(t.matmul(ps0[:, 520:528], onesf[0:1, :], t8row[0:1, 8:16], start=True, stop=True))
            b.on("tensor", t3_body)
            mm3 = pe.val

            def v3_body(v, mm3=mm3):
                v.wait(pe, mm3)
                v.tensor_copy(gwsumB[:, :], ps0[:, 512:520])
                dv.inc(v.tensor_copy(lbB[:, :], ps0[:, 520:528]))
            b.on("vector", v3_body)

        if mode == "full":
            with ExitStack() as wctx:
                wstage = [wctx.enter_context(nc.sbuf_tensor(f"{px}wstage{i}", [128, 8192], BF16))
                          for i in range(2)]
                wTsb = [wctx.enter_context(nc.sbuf_tensor(f"{px}wTsb{i}", [128, 8192], BF16))
                        for i in range(2)]
                ptwb = wctx.enter_context(nc.psum_tensor(f"{px}ptwb", [128, 1024], BF16))
                jobs = []
                for ee in range(8):
                    jobs += [(egw[ee], gwT_h.ap()[ee], 4, 1024), (euw[ee], uwT_h.ap()[ee], 4, 1024),
                             (edw[ee], dwT_h.ap()[ee], 8, 512)]
                jobs += [(sgw, sgwT_h.ap(), 8, 1024), (suw, suwT_h.ap(), 8, 1024),
                         (sdw, sdwT_h.ap(), 8, 1024)]
                wst_ld = [0, 0]
                wst_pe = [0, 0]
                wT_dv = [0]
                wT_sw = [0, 0]
                for ji, (src, dsth, nt, ncols) in enumerate(jobs):
                    sb = ji % 2

                    def g_body(g, ji=ji, src=src, nt=nt, ncols=ncols, sb=sb):
                        g.wait(pe, wst_pe[sb])
                        sg.inc(g.dma_start(
                            out=wstage[sb][:, 0:nt * ncols].rearrange("p (t i) -> p t i", t=nt),
                            in_=src.rearrange("(t p) i -> p t i", p=128)), 16)
                    b.on("gpsimd", g_body)
                    wst_ld[sb] = sg.val

                    nu = ncols // 128
                    for u in range(nu):
                        def t_body(t, ji=ji, u=u, nt=nt, ncols=ncols, sb=sb):
                            t.wait(sg, wst_ld[sb])
                            t.wait(sl)
                            t.wait(dv, wT_dv[0])
                            for tt in range(nt):
                                q = t.transpose(ptwb[:, tt * 128:(tt + 1) * 128],
                                                wstage[sb][:, tt * ncols + u * 128:tt * ncols + (u + 1) * 128],
                                                ident[:, :])
                            pe.inc(q)
                        b.on("tensor", t_body)
                        mmv = pe.val

                        def v_body(v, ji=ji, u=u, nt=nt, sb=sb, mmv=mmv):
                            v.wait(pe, mmv)
                            v.wait(sw, wT_sw[sb])
                            dv.inc(v.tensor_copy(wTsb[sb][:, u * nt * 128:(u + 1) * nt * 128],
                                                 ptwb[:, 0:nt * 128]))
                            wT_dv[0] = dv.val
                        b.on("vector", v_body)
                    wst_pe[sb] = pe.val

                    def s_body(s, ji=ji, dsth=dsth, nt=nt, ncols=ncols, sb=sb):
                        s.wait(dv, wT_dv[0])
                        nu_ = ncols // 128
                        sw.inc(s.dma_start(out=dsth, in_=wTsb[sb][:, 0:nu_ * nt * 128]), 16)
                        wT_sw[sb] = sw.val
                    b.on("sync", s_body)

        PS = e(nc.psum_tensor(f"{px}PS", [128, 4096], F32))
        g1ctx = ExitStack()
        g1a = g1ctx.enter_context
        x1g = [g1a(nc.sbuf_tensor(f"{px}x1g{i}", [128, 4 * 1024], F32)) for i in range(2)]
        x1Tg = g1a(nc.sbuf_tensor(f"{px}x1Tg", [128, 1024], F32))
        statg = g1a(nc.sbuf_tensor(f"{px}statg", [128, 48], F32))
        mvg = g1a(nc.sbuf_tensor(f"{px}mvg", [128, 8], F32))
        rstdg = g1a(nc.sbuf_tensor(f"{px}rstdg", [128, 4], F32))
        rws = g1a(nc.sbuf_tensor(f"{px}rws", [8, 512], F32))
        rawg = g1a(nc.sbuf_tensor(f"{px}rawg", [128, 32], F32))
        t8a = g1a(nc.sbuf_tensor(f"{px}t8a", [128, 8], F32))
        t8b = g1a(nc.sbuf_tensor(f"{px}t8b", [128, 8], F32))
        scoresb = g1a(nc.sbuf_tensor(f"{px}scoresb", [128, 8], F32))
        w1 = g1a(nc.sbuf_tensor(f"{px}w1", [128, 1], F32))
        w2 = g1a(nc.sbuf_tensor(f"{px}w2", [128, 1], F32))
        i1 = g1a(nc.sbuf_tensor(f"{px}i1", [128, 1], F32))
        i2 = g1a(nc.sbuf_tensor(f"{px}i2", [128, 1], F32))
        m1 = g1a(nc.sbuf_tensor(f"{px}m1", [128, 8], F32))
        m2 = g1a(nc.sbuf_tensor(f"{px}m2", [128, 8], F32))
        s2t = g1a(nc.sbuf_tensor(f"{px}s2t", [128, 8], F32))
        winv = g1a(nc.sbuf_tensor(f"{px}winv", [128, 1], F32))
        tc = g1a(nc.sbuf_tensor(f"{px}tc", [128, 16], F32))

        g1_dv = [0, 0]
        raw_sc = [0, 0]
        x1T_dv = [0]
        for g in range(8):
            gb = g % 2

            def s_body(s, g=g, gb=gb):
                s.wait(dv, g1_dv[gb])
                for u in range(4):
                    st = g * 4 + u
                    sl.inc(s.dma_start(out=x1g[gb][:, u * 1024:(u + 1) * 1024],
                                       in_=x1[st * 128:(st + 1) * 128, :]), 16)
            b.on("sync", s_body)
            ld = sl.val

            def v_body(v, g=g, gb=gb, ld=ld):
                v.wait(sl, ld)
                v.wait(dv)
                for u in range(4):
                    v.bn_stats(statg[:, u * 12:u * 12 + 6], x1g[gb][:, u * 1024:u * 1024 + 512])
                    v.bn_stats(statg[:, u * 12 + 6:u * 12 + 12], x1g[gb][:, u * 1024 + 512:u * 1024 + 1024])
                dv.inc(v.engine_nop())
                v.wait(dv)
                for u in range(4):
                    v.bn_aggr(mvg[:, u * 2:u * 2 + 2], statg[:, u * 12:u * 12 + 12])
                dv.inc(v.engine_nop())
                v.wait(dv)
                for u in range(4):
                    st = g * 4 + u
                    v.tensor_copy(meanA[:, st:st + 1], mvg[:, u * 2:u * 2 + 1])
                    q = v.tensor_scalar_add(rstdg[:, u:u + 1], mvg[:, u * 2 + 1:u * 2 + 2], EPS)
                dv.inc(q)
            b.on("vector", v_body)

            def s2_body(s, g=g):
                s.wait(dv)
                s.activation(rstdg[:, :], rstdg[:, :], AF.Ln)
                sc.inc(s.activation(rstdg[:, :], rstdg[:, :], AF.Exp, scale=-0.5))
            b.on("scalar", s2_body)

            def v2c(v, g=g):
                v.wait(sc)
                for u in range(4):
                    st = g * 4 + u
                    q = v.tensor_copy(rstdA[:, st:st + 1], rstdg[:, u:u + 1])
                dv.inc(q)
            b.on("vector", v2c)

            for u in range(4):
                st = g * 4 + u
                rbk = u % 2

                def t_tr(t, g=g, gb=gb, u=u, st=st, ld=ld):
                    t.wait(sl, ld)
                    if g == 0 and u == 0:
                        t.wait(dv)
                    t.wait(dv, x1T_dv[0])
                    for it in range(8):
                        q = t.transpose(PS[:, it * 128:(it + 1) * 128],
                                        x1g[gb][:, u * 1024 + it * 128:u * 1024 + (it + 1) * 128],
                                        identf[:, :])
                    pe.inc(q)
                b.on("tensor", t_tr)
                trv = pe.val

                def v_tr(v, g=g, u=u, st=st, trv=trv):
                    v.wait(pe, trv)
                    dv.inc(v.tensor_copy(x1Tg[:, :], PS[:, 0:1024]))
                    x1T_dv[0] = dv.val
                b.on("vector", v_tr)
                xtv = dv.val

                def t_raw(t, g=g, u=u, st=st, xtv=xtv, rbk=rbk):
                    t.wait(dv, xtv)
                    t.wait(sc, raw_sc[rbk])
                    for it in range(8):
                        q = t.matmul(PS[0:8, (2 + rbk) * 512:(2 + rbk) * 512 + 128],
                                     gwTs[:, it * 8:(it + 1) * 8],
                                     x1Tg[:, it * 128:(it + 1) * 128], start=(it == 0), stop=(it == 7))
                    pe.inc(q)
                b.on("tensor", t_raw)
                rawv = pe.val

                def sc_r(s, g=g, u=u, rawv=rawv, rbk=rbk):
                    s.wait(pe, rawv)
                    sc.inc(s.copy(rws[:, u * 128:(u + 1) * 128], PS[0:8, (2 + rbk) * 512:(2 + rbk) * 512 + 128]))
                    raw_sc[rbk] = sc.val
                b.on("scalar", sc_r)
            rws_sc = sc.val

            def t_rt(t, g=g, rws_sc=rws_sc):
                t.wait(sc, rws_sc)
                t.wait(dv)
                for u in range(4):
                    q = t.matmul(PS[:, 2048 + u * 8:2048 + (u + 1) * 8], rws[:, u * 128:(u + 1) * 128],
                                 identf[0:8, 0:8], start=True, stop=True)
                pe.inc(q)
            b.on("tensor", t_rt)
            rtv = pe.val

            def v_rt(v, g=g, rtv=rtv):
                v.wait(pe, rtv)
                dv.inc(v.tensor_copy(rawg[:, :], PS[:, 2048:2080]))
            b.on("vector", v_rt)

            for u in range(4):
                st = g * 4 + u

                def v4b(v, g=g, u=u, st=st):
                    v.wait(dv)
                    dv.inc(v.tensor_scalar_mul(t8a[:, :], gwsumB[:, :], meanA[:, st:st + 1]))
                    v.wait(dv)
                    dv.inc(v.tensor_sub(t8b[:, :], rawg[:, u * 8:(u + 1) * 8], t8a[:, :]))
                    v.wait(dv)
                    dv.inc(v.tensor_scalar_mul(t8b[:, :], t8b[:, :], rstdA[:, st:st + 1]))
                    v.wait(dv)
                    dv.inc(v.tensor_add(t8b[:, :], t8b[:, :], lbB[:, :]))
                b.on("vector", v4b)

                def sc3b(s, g=g, u=u):
                    s.wait(dv)
                    sc.inc(s.activation(scoresb[:, :], t8b[:, :], AF.Sigmoid))
                b.on("scalar", sc3b)

                def v5b(v, g=g, u=u, st=st):
                    v.wait(sc)
                    dv.inc(v.reduce_max(w1[:, :], scoresb[:, :], axis=AX.X))
                    v.wait(dv)
                    dv.inc(v.tensor_scalar(m1[:, :], scoresb[:, :], w1[:, 0:1], None, ALU.is_ge))
                    v.wait(dv)
                    dv.inc(v.scalar_tensor_tensor(t8a[:, :], m1[:, :], -8.0, iop8[:, :], ALU.mult, ALU.add))
                    v.wait(dv)
                    dv.inc(v.tensor_reduce(i1[:, :], t8a[:, :], axis=AX.X, op=ALU.min))
                    v.wait(dv)
                    dv.inc(v.tensor_scalar(m1[:, :], iot[:, :], i1[:, 0:1], None, ALU.is_equal))
                    v.wait(dv)
                    dv.inc(v.scalar_tensor_tensor(s2t[:, :], m1[:, :], -1e30, scoresb[:, :], ALU.mult, ALU.add))
                    v.wait(dv)
                    dv.inc(v.reduce_max(w2[:, :], s2t[:, :], axis=AX.X))
                    v.wait(dv)
                    dv.inc(v.tensor_scalar(m2[:, :], s2t[:, :], w2[:, 0:1], None, ALU.is_ge))
                    v.wait(dv)
                    dv.inc(v.scalar_tensor_tensor(t8a[:, :], m2[:, :], -8.0, iop8[:, :], ALU.mult, ALU.add))
                    v.wait(dv)
                    dv.inc(v.tensor_reduce(i2[:, :], t8a[:, :], axis=AX.X, op=ALU.min))
                    v.wait(dv)
                    dv.inc(v.tensor_scalar(m2[:, :], iot[:, :], i2[:, 0:1], None, ALU.is_equal))
                    v.wait(dv)
                    dv.inc(v.tensor_add(winv[:, :], w1[:, :], w2[:, :]))
                    v.wait(dv)
                    dv.inc(v.tensor_scalar_add(winv[:, :], winv[:, :], 1e-8))
                    v.wait(dv)
                    dv.inc(v.reciprocal(winv[:, :], winv[:, :]))
                    v.wait(dv)
                    dv.inc(v.tensor_scalar_mul(t8a[:, :], m1[:, :], w1[:, 0:1]))
                    v.wait(dv)
                    dv.inc(v.tensor_scalar_mul(t8b[:, :], m2[:, :], w2[:, 0:1]))
                    v.wait(dv)
                    dv.inc(v.tensor_add(t8a[:, :], t8a[:, :], t8b[:, :]))
                    v.wait(dv)
                    dv.inc(v.tensor_scalar_mul(tc[:, 0:8], t8a[:, :], winv[:, 0:1]))
                    v.wait(dv)
                    dv.inc(v.tensor_add(tc[:, 8:16], m1[:, :], m2[:, :]))
                    v.wait(dv)
                    dv.inc(v.tensor_copy(coefA[:, st * 8:(st + 1) * 8], tc[:, 8:16]))
                b.on("vector", v5b)
                tcv = dv.val

                def t4b(t, g=g, u=u, st=st, tcv=tcv):
                    t.wait(dv, tcv)
                    pe.inc(t.matmul(PS[0:16, 2560:2561], tc[:, :], onescol[:, :],
                                    start=(st == 0), stop=(st == NT - 1)))
                b.on("tensor", t4b)
            g1_dv[gb] = dv.val

        def v6b(v):
            v.wait(pe)
            dv.inc(v.tensor_copy(partsb[:, :], PS[0:16, 2560:2561]))
        b.on("vector", v6b)

        if mode == "partials":
            def s_out(s):
                s.wait(dv)
                so.inc(s.dma_start(out=part, in_=partsb[:, :]), 16)
                s.wait(so)
            b.on("sync", s_out)
            g1ctx.close()
            return nc

        g1ctx.close()
        wmB = e(nc.sbuf_tensor(f"{px}wmB", [128, 8], F32))
        Ssb = e(nc.sbuf_tensor(f"{px}Ssb", [8, 8], F32))
        Csb = e(nc.sbuf_tensor(f"{px}Csb", [8, 8], F32))
        Sv = e(nc.sbuf_tensor(f"{px}Sv", [8, 1], F32))
        Cv = e(nc.sbuf_tensor(f"{px}Cv", [8, 1], F32))

        def s2_body(s):
            s.wait(dv); s.wait(pe); s.wait(sc)
            sl.inc(s.dma_start(out=Ssb[:, :], in_=Sst), 16)
            sl.inc(s.dma_start(out=Csb[:, :], in_=Cst), 16)
        b.on("sync", s2_body)
        stld = sl.val

        def v7b(v, stld=stld):
            v.wait(sl, stld)
            v.wait(dv)
            v.reduce_sum(Sv[:, :], Ssb[:, :], axis=AX.X)
            dv.inc(v.reduce_sum(Cv[:, :], Csb[:, :], axis=AX.X))
            v.wait(dv)
            dv.inc(v.tensor_scalar_max(Cv[:, :], Cv[:, :], 1.0))
            v.wait(dv)
            dv.inc(v.reciprocal(Cv[:, :], Cv[:, :]))
            v.wait(dv)
            dv.inc(v.tensor_mul(Sv[:, :], Sv[:, :], Cv[:, :]))
        b.on("vector", v7b)

        def t5b(t):
            t.wait(dv)
            pe.inc(t.matmul(PS[0:1, 3072:3080], Sv[:, 0:1], identf[0:8, 0:8], start=True, stop=True))
        b.on("tensor", t5b)
        m5 = pe.val

        def v8b(v, m5=m5):
            v.wait(pe, m5)
            dv.inc(v.tensor_copy(t8row[0:1, 0:8], PS[0:1, 3072:3080]))
        b.on("vector", v8b)

        def t6b(t):
            t.wait(dv)
            pe.inc(t.matmul(PS[:, 3072:3080], onesf[0:1, :], t8row[0:1, 0:8], start=True, stop=True))
        b.on("tensor", t6b)
        m6 = pe.val

        def v9b(v, m6=m6):
            v.wait(pe, m6)
            dv.inc(v.tensor_copy(wmB[:, :], PS[:, 3072:3080]))
            v.wait(dv)
            for st in range(NT):
                q = v.tensor_mul(coefA[:, st * 8:(st + 1) * 8], coefA[:, st * 8:(st + 1) * 8], wmB[:, :])
            dv.inc(q)
        b.on("vector", v9b)

        acc = e(nc.sbuf_tensor(f"{px}acc", [128, 8 * 1024], F32))
        ln1T = e(nc.sbuf_tensor(f"{px}ln1T", [128, 8 * 1024], BF16))
        lntf = e(nc.sbuf_tensor(f"{px}lntf", [128, 1024], F32))
        wg = [e(nc.sbuf_tensor(f"{px}wg{i}", [128, 8192], BF16)) for i in range(2)]
        wu = [e(nc.sbuf_tensor(f"{px}wu{i}", [128, 8192], BF16)) for i in range(2)]
        wd = [e(nc.sbuf_tensor(f"{px}wd{i}", [128, 8192], BF16)) for i in range(2)]
        silb = [e(nc.sbuf_tensor(f"{px}silb{i}", [128, 512], BF16)) for i in range(2)]
        actb = e(nc.sbuf_tensor(f"{px}actb", [128, 8 * 512], BF16))

        wld = [0, 0]
        wcons = [0, 0]
        bank_fr = [(0, 'p')] * 8
        acc_out_dv = [0]

        def bank_wait(t, j):
            v_, k_ = bank_fr[j]
            if k_ == 's':
                t.wait(sc, v_)
            elif k_ == 'd':
                t.wait(dv, v_)

        for qt in range(4):
            def s_body(s, qt=qt):
                s.wait(dv, acc_out_dv[0])
                s.wait(sw)
                if qt == 0:
                    s.wait(dv); s.wait(pe); s.wait(sc)
                sl.inc(s.dma_start(out=acc[:, :].rearrange("p (t d) -> p t d", t=8),
                                   in_=x1[qt * 1024:(qt + 1) * 1024, :].rearrange("(t p) d -> p t d", p=128)), 16)
            b.on("sync", s_body)
            accld = sl.val

            for stt in range(8):
                gst = qt * 8 + stt

                def v_body(v, qt=qt, stt=stt, gst=gst, accld=accld):
                    v.wait(sl, accld)
                    v.wait(pe)
                    dv.inc(v.tensor_scalar(lntf[:, :], acc[:, stt * 1024:(stt + 1) * 1024],
                                           meanA[:, gst:gst + 1], rstdA[:, gst:gst + 1],
                                           ALU.subtract, ALU.mult))
                b.on("vector", v_body)
                apv = dv.val

                def t_body(t, qt=qt, stt=stt, apv=apv):
                    t.wait(dv, apv)
                    for j in range(2):
                        bank_wait(t, j)
                    for it in range(8):
                        q = t.transpose(PS[:, it * 128:(it + 1) * 128],
                                        lntf[:, it * 128:(it + 1) * 128], identf[:, :])
                    pe.inc(q)
                b.on("tensor", t_body)
                trv = pe.val

                def v2_body(v, qt=qt, stt=stt, trv=trv):
                    v.wait(pe, trv)
                    dst = bass.AP(ln1T, stt * 128, [[8 * 1024, 128], [1024, 8], [1, 128]])
                    srcap = bass.AP(PS, 0, [[4096, 128], [128, 8], [1, 128]])
                    dv.inc(v.tensor_copy(dst, srcap))
                b.on("vector", v2_body)
                for j in range(2):
                    bank_fr[j] = (dv.val, 'd')

            ln_dv = dv.val

            for ee in range(9):
                shared = (ee == 8)
                nI = 8 if shared else 4
                sb = ee % 2

                def g_body(g, qt=qt, ee=ee, shared=shared, sb=sb):
                    g.wait(pe, wcons[sb])
                    if qt == 0 and ee < 2:
                        g.wait(sw)
                    if shared:
                        sg.inc(g.dma_start(out=wg[sb][:, :], in_=sgwT_h.ap()), 16)
                        sg.inc(g.dma_start(out=wu[sb][:, :], in_=suwT_h.ap()), 16)
                        sg.inc(g.dma_start(out=wd[sb][:, :], in_=sdwT_h.ap()), 16)
                    else:
                        sg.inc(g.dma_start(out=wg[sb][:, 0:4096], in_=gwT_h.ap()[ee]), 16)
                        sg.inc(g.dma_start(out=wu[sb][:, 0:4096], in_=uwT_h.ap()[ee]), 16)
                        sg.inc(g.dma_start(out=wd[sb][:, 0:4096], in_=dwT_h.ap()[ee]), 16)
                b.on("gpsimd", g_body)
                wld[sb] = sg.val

                for sc_ in range(2):
                    for wave in range(nI // 4):
                        gstop = [0] * 4
                        for it in range(8):
                            def t_gu(t, qt=qt, ee=ee, sc_=sc_, wave=wave, it=it, sb=sb, nI=nI):
                                t.wait(sg, wld[sb])
                                if it == 0:
                                    t.wait(dv, ln_dv)
                                for It4 in range(4):
                                    It = wave * 4 + It4
                                    if it == 0:
                                        bank_wait(t, It4)
                                        bank_wait(t, 4 + It4)
                                    t.matmul(PS[:, It4 * 512:(It4 + 1) * 512],
                                             wg[sb][:, it * (nI * 128) + It * 128:it * (nI * 128) + (It + 1) * 128],
                                             ln1T[:, it * 1024 + sc_ * 512:it * 1024 + (sc_ + 1) * 512],
                                             start=(it == 0), stop=(it == 7))
                                    q = t.matmul(PS[:, 2048 + It4 * 512:2048 + (It4 + 1) * 512],
                                                 wu[sb][:, it * (nI * 128) + It * 128:it * (nI * 128) + (It + 1) * 128],
                                                 ln1T[:, it * 1024 + sc_ * 512:it * 1024 + (sc_ + 1) * 512],
                                                 start=(it == 0), stop=(it == 7))
                                    if it == 7:
                                        pe.inc(q)
                                        gstop[It4] = pe.val
                                if it != 7:
                                    pe.inc(q)
                            b.on("tensor", t_gu)
                        wcons[sb] = pe.val

                        for It4 in range(4):
                            It = wave * 4 + It4

                            def sc_body(s, It4=It4, It=It, gs=gstop):
                                s.wait(pe, gs[It4])
                                s.wait(dv)
                                sc.inc(s.activation(silb[It4 % 2][:, :], PS[:, It4 * 512:(It4 + 1) * 512],
                                                    AF.Silu))
                                bank_fr[It4] = (sc.val, 's')
                            b.on("scalar", sc_body)
                            scv = sc.val

                            def v_body(v, It4=It4, It=It, scv=scv):
                                v.wait(sc, scv)
                                dv.inc(v.tensor_mul(actb[:, It * 512:(It + 1) * 512],
                                                    silb[It4 % 2][:, :],
                                                    PS[:, 2048 + It4 * 512:2048 + (It4 + 1) * 512]))
                                bank_fr[4 + It4] = (dv.val, 'd')
                            b.on("vector", v_body)
                    act_dv = dv.val

                    for stt4 in range(4):
                        bp = stt4 * 2

                        def t_dn(t, qt=qt, ee=ee, sc_=sc_, stt4=stt4, sb=sb, nI=nI, bp=bp, act_dv=act_dv):
                            t.wait(dv, act_dv)
                            bank_wait(t, bp)
                            bank_wait(t, bp + 1)
                            for It in range(nI):
                                for n in range(2):
                                    q = t.matmul(PS[:, (bp + n) * 512:(bp + n + 1) * 512],
                                                 actb[:, It * 512 + stt4 * 128:It * 512 + (stt4 + 1) * 128],
                                                 wd[sb][:, It * 1024 + n * 512:It * 1024 + (n + 1) * 512],
                                                 start=(It == 0), stop=(It == nI - 1))
                            pe.inc(q)
                        b.on("tensor", t_dn)
                        dnv = pe.val
                        wcons[sb] = pe.val

                        def v_fma(v, qt=qt, ee=ee, sc_=sc_, stt4=stt4, shared=shared, bp=bp, dnv=dnv):
                            v.wait(pe, dnv)
                            stt = sc_ * 4 + stt4
                            gst = qt * 8 + stt
                            coef = 1.0 if shared else coefA[:, gst * 8 + ee:gst * 8 + ee + 1]
                            for n in range(2):
                                q = v.scalar_tensor_tensor(acc[:, stt * 1024 + n * 512:stt * 1024 + (n + 1) * 512],
                                                           PS[:, (bp + n) * 512:(bp + n + 1) * 512],
                                                           coef,
                                                           acc[:, stt * 1024 + n * 512:stt * 1024 + (n + 1) * 512],
                                                           ALU.mult, ALU.add)
                            dv.inc(q)
                            bank_fr[bp] = (dv.val, 'd')
                            bank_fr[bp + 1] = (dv.val, 'd')
                        b.on("vector", v_fma)

            def s2q(s, qt=qt):
                s.wait(dv)
                so.inc(s.dma_start(out=out[qt * 1024:(qt + 1) * 1024, :].rearrange("(t p) d -> p t d", p=128),
                                   in_=acc[:, :].rearrange("p (t d) -> p t d", t=8)), 16)
            b.on("sync", s2q)
            acc_out_dv[0] = dv.val

        def fin(s):
            s.wait(so)
        b.on("sync", fin)
    return nc

def build_attn1(nc, names, env=None, px='a1_'):
    x2 = _dram(nc, names, "x", [S, D], F32, "ExternalInput")
    wnames = ["p", "q", "k", "v", "o"]
    Wd = {w: _dram(nc, names, "W" + w, [D, D], F32, "ExternalInput") for w in wnames}
    Bd = {w: _dram(nc, names, "b" + w, [1, D], F32, "ExternalInput") for w in wnames}
    out = _dram(nc, names, "out", [S, D], F32, "ExternalOutput")
    ln2h = nc.dram_tensor("ln2T_scratch", [8, 128, S + 2], BF16, kind="Internal")

    bsum_np = np.zeros((128, 8 * 16), dtype=BF)
    bbc_np = np.zeros((16, 8 * 128), dtype=BF)
    for t in range(8):
        bsum_np[0:64, t * 16 + 2 * t] = 1
        bsum_np[64:128, t * 16 + 2 * t + 1] = 1
        bbc_np[2 * t, t * 128:t * 128 + 64] = 1
        bbc_np[2 * t + 1, t * 128 + 64:t * 128 + 128] = 1

    with ExitStack() as ctx:
        e = ctx.enter_context
        ident = e(nc.sbuf_tensor(f"{px}ident", [128, 128], BF16))
        identf = e(nc.sbuf_tensor(f"{px}identf", [128, 128], F32))
        bsum = e(nc.sbuf_tensor(f"{px}bsum", [128, 8 * 16], BF16))
        bbc = e(nc.sbuf_tensor(f"{px}bbc", [16, 8 * 128], BF16))
        WT = {w: e(nc.sbuf_tensor(f"{px}WT{w}", [128, 8192], BF16)) for w in wnames}
        bcol = {w: e(nc.sbuf_tensor(f"{px}bcol{w}", [128, 8], F32)) for w in wnames}
        x2t = [e(nc.sbuf_tensor(f"{px}x2t{i}", [128, 1024], F32)) for i in range(2)]
        stat = e(nc.sbuf_tensor(f"{px}stat", [128, 12], F32))
        mv = e(nc.sbuf_tensor(f"{px}mv", [128, 2], F32))
        rstd = e(nc.sbuf_tensor(f"{px}rstd", [128, 1], F32))
        lnt = e(nc.sbuf_tensor(f"{px}lnt", [128, 1024], BF16))
        d1ctx = ExitStack()
        wstage = d1ctx.enter_context(nc.sbuf_tensor(f"{px}wstage", [128, 8192], BF16))
        ln2T = d1ctx.enter_context(nc.sbuf_tensor(f"{px}ln2T", [128, 8 * 4096], BF16))
        if env is None:
            block = e(nc.Block())
            b = B(nc, ctx, block)
        else:
            block = env["block"]
            b = env["b"]
        sl = b.ctr("sl"); sg = b.ctr("sg"); pe = b.ctr("pe")
        dv = b.ctr("dv"); sc = b.ctr("sc"); sp = b.ctr("sp")
        sob = [b.ctr("sob0"), b.ctr("sob1")]

        identh = nc.inline_tensor(np.eye(128, dtype=BF), f"{px}identnp")
        identfh = nc.inline_tensor(np.eye(128, dtype=np.float32), f"{px}identfnp")
        bsumh = nc.inline_tensor(bsum_np, f"{px}bsumnp")
        bbch = nc.inline_tensor(bbc_np, f"{px}bbcnp")

        def d0_sync(s):
            sl.inc(s.dma_start(out=ident[:, :], in_=identh.ap()), 16)
            sl.inc(s.dma_start(out=identf[:, :], in_=identfh.ap()), 16)
            sl.inc(s.dma_start(out=bsum[:, :], in_=bsumh.ap()), 16)
            sl.inc(s.dma_start(out=bbc[:, :], in_=bbch.ap()), 16)
            for w in wnames:
                for m in range(8):
                    sl.inc(s.dma_start(out=bcol[w][:, m:m + 1],
                                       in_=Bd[w][0:1, m * 128:(m + 1) * 128].rearrange("o p -> p o")), 16)
        b.on("sync", d0_sync)

        def d0_vec(v):
            v.wait(sl)
            dv.inc(v.tensor_scalar_mul(bcol["q"][:, :], bcol["q"][:, :], 0.125))
        b.on("vector", d0_vec)

        for wi, w in enumerate(wnames):
            def g_body(g, w=w, wi=wi):
                if wi > 0:
                    g.wait(pe)
                sg.inc(g.dma_start(out=wstage[:, :].rearrange("p (t i) -> p t i", t=8),
                                   in_=Wd[w].rearrange("(t p) i -> p t i", p=128)), 16)
            b.on("gpsimd", g_body)

            with nc.psum_tensor(f"{px}ptw_{w}", [128, 1024], BF16) as ptw:
                for u in range(8):
                    def t_body(t, w=w, u=u):
                        t.wait(sg)
                        t.wait(sl)
                        if u > 0:
                            t.wait(dv)
                        for tt in range(8):
                            pe.inc(t.transpose(ptw[:, tt * 128:(tt + 1) * 128],
                                               wstage[:, tt * 1024 + u * 128:tt * 1024 + (u + 1) * 128],
                                               ident[:, :]))
                    b.on("tensor", t_body)

                    def v_body(v, w=w, u=u):
                        v.wait(pe)
                        dv.inc(v.tensor_copy(WT[w][:, u * 1024:(u + 1) * 1024], ptw[:, :]))
                    b.on("vector", v_body)

        for st in range(NT):
            xb = x2t[st % 2]

            def s_body(s, st=st, xb=xb):
                s.wait(dv)
                sl.inc(s.dma_start(out=xb[:, :], in_=x2[st * 128:(st + 1) * 128, :]), 16)
            b.on("sync", s_body)

            def v_body(v, st=st, xb=xb):
                v.wait(sl)
                v.wait(dv)
                v.bn_stats(stat[:, 0:6], xb[:, 0:512])
                dv.inc(v.bn_stats(stat[:, 6:12], xb[:, 512:1024]))
                v.wait(dv)
                dv.inc(v.bn_aggr(mv[:, :], stat[:, :]))
                v.wait(dv)
                dv.inc(v.tensor_scalar_add(rstd[:, 0:1], mv[:, 1:2], EPS))
            b.on("vector", v_body)

            def s2_body(s):
                s.wait(dv)
                sc.inc(s.activation(rstd[:, 0:1], rstd[:, 0:1], AF.Sqrt))
            b.on("scalar", s2_body)

            def v2_body(v, st=st, xb=xb):
                v.wait(sc)
                dv.inc(v.reciprocal(rstd[:, 0:1], rstd[:, 0:1]))
                v.wait(dv)
                dv.inc(v.tensor_scalar(lnt[:, :], xb[:, :], mv[:, 0:1], rstd[:, 0:1],
                                       ALU.subtract, ALU.mult))
            b.on("vector", v2_body)

            with nc.psum_tensor(f"{px}plt_{st}", [128, 1024], BF16) as plt:
                def t_body(t, st=st):
                    t.wait(dv)
                    for it in range(8):
                        pe.inc(t.transpose(plt[:, it * 128:(it + 1) * 128],
                                           lnt[:, it * 128:(it + 1) * 128], ident[:, :]))
                b.on("tensor", t_body)

                def v3_body(v, st=st):
                    v.wait(pe)
                    for it in range(8):
                        i = v.tensor_copy(ln2T[:, it * 4096 + st * 128:it * 4096 + (st + 1) * 128],
                                          plt[:, it * 128:(it + 1) * 128])
                    dv.inc(i)
                b.on("vector", v3_body)

        def spill(s):
            s.wait(dv)
            for it in range(8):
                sp.inc(s.dma_start(out=ln2h.ap()[it, :, 1:S + 1], in_=ln2T[:, it * 4096:(it + 1) * 4096]), 16)
                sp.inc(s.dma_start(out=ln2h.ap()[it, :, 0:1], in_=ln2T[:, it * 4096 + 4095:it * 4096 + 4096],
                                   allow_slow_non_contiguous=True), 16)
                sp.inc(s.dma_start(out=ln2h.ap()[it, :, S + 1:S + 2], in_=ln2T[:, it * 4096:it * 4096 + 1],
                                   allow_slow_non_contiguous=True), 16)
        b.on("sync", spill)
        d1ctx.close()

        lnc = e(nc.sbuf_tensor(f"{px}lnc", [128, 8 * 514], BF16))
        xpc = e(nc.sbuf_tensor(f"{px}xpc", [128, 8 * 514], BF16))
        qc = e(nc.sbuf_tensor(f"{px}qc", [128, 8 * 512], BF16))
        kc = e(nc.sbuf_tensor(f"{px}kc", [128, 8 * 514], BF16))
        vc = e(nc.sbuf_tensor(f"{px}vc", [128, 8 * 514], BF16))
        prodb = [e(nc.sbuf_tensor(f"{px}prodb{i}", [128, 512], BF16)) for i in range(2)]
        expb = e(nc.sbuf_tensor(f"{px}expb", [16, 3 * 512], F32))
        dnm = e(nc.sbuf_tensor(f"{px}dnm", [16, 512], F32))
        attnb = e(nc.sbuf_tensor(f"{px}attnb", [16, 3 * 512], BF16))
        acc = e(nc.sbuf_tensor(f"{px}acc", [128, 8 * 512], F32))
        tmpb = e(nc.sbuf_tensor(f"{px}tmpb", [128, 512], F32))
        attnoT = e(nc.sbuf_tensor(f"{px}attnoT", [128, 8 * 512], BF16))
        yT = e(nc.sbuf_tensor(f"{px}yT", [128, 8 * 512], F32))
        outt = [e(nc.sbuf_tensor(f"{px}outt{i}", [128, 1024], F32)) for i in range(2)]

        def mm_proj(t, w, m, dst_main, dst_halo, src, width):
            for it in range(8):
                i = t.matmul(dst_main[:, :], WT[w][:, it * 1024 + m * 128:it * 1024 + (m + 1) * 128],
                             src[:, it * 514 + 1:it * 514 + 513],
                             start=(it == 0), stop=(it == 7))
            if width == 514:
                for it in range(8):
                    i = t.matmul(dst_halo[:, 0:2], WT[w][:, it * 1024 + m * 128:it * 1024 + (m + 1) * 128],
                                 src[:, it * 514:it * 514 + 514:513],
                                 start=(it == 0), stop=(it == 7))
            return i

        for c in range(8):
            def s_body(s, c=c):
                s.wait(pe)
                if c == 0:
                    s.wait(sp)
                sl.inc(s.dma_start(out=lnc[:, :].rearrange("p (it w) -> p it w", it=8),
                                   in_=ln2h.ap()[:, :, c * 512:c * 512 + 514].rearrange("it p w -> p it w")), 16)
            b.on("sync", s_body)

            with ExitStack() as sctx:
                pmain = [sctx.enter_context(nc.psum_tensor(f"{px}pm_{c}_{i}", [128, 512], F32)) for i in range(2)]
                phalo = [sctx.enter_context(nc.psum_tensor(f"{px}phh_{c}_{i}", [128, 2], F32)) for i in range(2)]
                for m in range(8):
                    def t_body(t, c=c, m=m):
                        t.wait(sl)
                        t.wait(dv)
                        t.wait(sc)
                        pe.inc(mm_proj(t, "p", m, pmain[m % 2], phalo[m % 2], lnc, 514))
                    b.on("tensor", t_body)

                    def sc_body(s, c=c, m=m):
                        s.wait(pe)
                        s.wait(dv)
                        s.activation(xpc[:, m * 514 + 1:m * 514 + 513], pmain[m % 2][:, :], AF.Identity,
                                     bias=bcol["p"][:, m:m + 1])
                        i = s.activation(xpc[:, m * 514:m * 514 + 514:513], phalo[m % 2][:, :], AF.Identity,
                                         bias=bcol["p"][:, m:m + 1])
                        sc.inc(i)
                    b.on("scalar", sc_body)

                for m in range(8):
                    def t_body(t, c=c, m=m):
                        t.wait(sc)
                        pe.inc(mm_proj(t, "q", m, pmain[m % 2], None, xpc, 512))
                    b.on("tensor", t_body)

                    def sc_body(s, c=c, m=m):
                        s.wait(pe)
                        sc.inc(s.activation(qc[:, m * 512:(m + 1) * 512], pmain[m % 2][:, :], AF.Identity,
                                            bias=bcol["q"][:, m:m + 1], scale=0.125))
                    b.on("scalar", sc_body)

                for dst, w in ((kc, "k"), (vc, "v")):
                    for m in range(8):
                        def t_body(t, c=c, m=m, w=w):
                            t.wait(sc)
                            pe.inc(mm_proj(t, w, m, pmain[m % 2], phalo[m % 2], xpc, 514))
                        b.on("tensor", t_body)

                        def sc_body(s, c=c, m=m, w=w, dst=dst):
                            s.wait(pe)
                            s.activation(dst[:, m * 514 + 1:m * 514 + 513], pmain[m % 2][:, :], AF.Identity,
                                         bias=bcol[w][:, m:m + 1])
                            sc.inc(s.activation(dst[:, m * 514:m * 514 + 514:513], phalo[m % 2][:, :],
                                                AF.Identity, bias=bcol[w][:, m:m + 1]))
                        b.on("scalar", sc_body)

            with ExitStack() as sctx:
                psc = sctx.enter_context(nc.psum_tensor(f"{px}psc_{c}", [16, 3 * 512], F32))
                for n in range(3):
                    for t8 in range(8):
                        pb_ = prodb[(n * 8 + t8) % 2]

                        def v_body(v, c=c, n=n, t8=t8, pb_=pb_):
                            v.wait(sc)
                            v.wait(pe)
                            dv.inc(v.tensor_mul(pb_[:, :], qc[:, t8 * 512:(t8 + 1) * 512],
                                                kc[:, t8 * 514 + n:t8 * 514 + n + 512]))
                        b.on("vector", v_body)

                        def t_body(t, c=c, n=n, t8=t8, pb_=pb_):
                            t.wait(dv)
                            pe.inc(t.matmul(psc[:, n * 512:(n + 1) * 512], bsum[:, t8 * 16:(t8 + 1) * 16],
                                            pb_[:, :], start=(t8 == 0), stop=(t8 == 7)))
                        b.on("tensor", t_body)

                def sc_body(s, c=c):
                    s.wait(pe)
                    sc.inc(s.activation(expb[:, :], psc[:, :], AF.Exp))
                b.on("scalar", sc_body)

                def v_body(v, c=c):
                    v.wait(sc)
                    dv.inc(v.tensor_add(dnm[:, :], expb[:, 0:512], expb[:, 512:1024]))
                    v.wait(dv)
                    dv.inc(v.tensor_add(dnm[:, :], dnm[:, :], expb[:, 1024:1536]))
                    v.wait(dv)
                    dv.inc(v.reciprocal(dnm[:, :], dnm[:, :]))
                    v.wait(dv)
                    for n in range(3):
                        i = v.tensor_mul(attnb[:, n * 512:(n + 1) * 512],
                                         expb[:, n * 512:(n + 1) * 512], dnm[:, :])
                    dv.inc(i)
                b.on("vector", v_body)

            with ExitStack() as sctx:
                pb2 = [sctx.enter_context(nc.psum_tensor(f"{px}pb2_{c}_{i}", [128, 512], F32)) for i in range(2)]
                for n in range(3):
                    for t8 in range(8):
                        pp = pb2[(n * 8 + t8) % 2]

                        def t_body(t, c=c, n=n, t8=t8, pp=pp):
                            t.wait(dv)
                            pe.inc(t.matmul(pp[:, :], bbc[:, t8 * 128:(t8 + 1) * 128],
                                            attnb[:, n * 512:(n + 1) * 512], start=True, stop=True))
                        b.on("tensor", t_body)

                        def v_body(v, c=c, n=n, t8=t8, pp=pp):
                            v.wait(pe)
                            vsl = vc[:, t8 * 514 + n:t8 * 514 + n + 512]
                            if n == 0:
                                dv.inc(v.tensor_mul(acc[:, t8 * 512:(t8 + 1) * 512], pp[:, :], vsl))
                            else:
                                dv.inc(v.tensor_mul(tmpb[:, :], pp[:, :], vsl))
                                v.wait(dv)
                                dst = acc if n == 1 else attnoT
                                dv.inc(v.tensor_add(dst[:, t8 * 512:(t8 + 1) * 512],
                                                    acc[:, t8 * 512:(t8 + 1) * 512], tmpb[:, :]))
                        b.on("vector", v_body)

            with ExitStack() as sctx:
                pyo = [sctx.enter_context(nc.psum_tensor(f"{px}pyo_{c}_{i}", [128, 512], F32)) for i in range(2)]
                for m in range(8):
                    def t_body(t, c=c, m=m):
                        t.wait(dv)
                        t.wait(sc)
                        for it in range(8):
                            pe.inc(t.matmul(pyo[m % 2][:, :], WT["o"][:, it * 1024 + m * 128:it * 1024 + (m + 1) * 128],
                                            attnoT[:, it * 512:(it + 1) * 512],
                                            start=(it == 0), stop=(it == 7)))
                    b.on("tensor", t_body)

                    def sc_body(s, c=c, m=m):
                        s.wait(pe)
                        sc.inc(s.activation(yT[:, m * 512:(m + 1) * 512], pyo[m % 2][:, :], AF.Identity,
                                            bias=bcol["o"][:, m:m + 1]))
                    b.on("scalar", sc_body)

            with ExitStack() as sctx:
                pxt = [sctx.enter_context(nc.psum_tensor(f"{px}pxt_{c}_{i}", [128, 1024], F32)) for i in range(2)]
                for st in range(4):
                    gt = c * 4 + st
                    xb = x2t[st % 2]
                    ob = outt[st % 2]

                    def s_body(s, c=c, st=st, xb=xb):
                        s.wait(dv)
                        sl.inc(s.dma_start(out=xb[:, :], in_=x2[(c * 4 + st) * 128:(c * 4 + st + 1) * 128, :]), 16)
                    b.on("sync", s_body)

                    def t_body(t, c=c, st=st):
                        t.wait(sc)
                        t.wait(dv)
                        for m in range(8):
                            pe.inc(t.transpose(pxt[st % 2][:, m * 128:(m + 1) * 128],
                                               yT[:, m * 512 + st * 128:m * 512 + (st + 1) * 128],
                                               identf[:, :]))
                    b.on("tensor", t_body)

                    def v_body(v, c=c, st=st, xb=xb, ob=ob, sob_=sob[st % 2]):
                        v.wait(pe)
                        v.wait(sl)
                        v.wait(sob_)
                        dv.inc(v.tensor_add(ob[:, :], pxt[st % 2][:, :], xb[:, :]))
                    b.on("vector", v_body)

                    def s2_body(s, c=c, st=st, ob=ob, sob_=sob[st % 2], gt=gt):
                        s.wait(dv)
                        sob_.inc(s.dma_start(out=out[gt * 128:(gt + 1) * 128, :], in_=ob[:, :]), 16)
                    b.on("sync", s2_body)

        def fin(s):
            s.wait(sob[0])
            s.wait(sob[1])
        b.on("sync", fin)
    return nc



def build_launch1(nc):
    with ExitStack() as octx:
        block = octx.enter_context(nc.Block())
        b = B(nc, octx, block)
        env = {"block": block, "b": b}
        x1ap = nc.dram_tensor("x1", [S, D], F32, kind="ExternalOutput").ap()
        build_attn0(nc, dict(x="x", wq="wq", wk="wk", wv="wv", wo="wo", out=x1ap), env=env)
        b.barrier()
        build_moe(nc, dict(x=x1ap, gate_w="gate_w", lb_bias="lb_bias", out="part"),
                  "partials", env=env)
        b.barrier()
    return nc


def build_launch2(nc):
    with ExitStack() as octx:
        block = octx.enter_context(nc.Block())
        b = B(nc, octx, block)
        env = {"block": block, "b": b}
        x2ap = nc.dram_tensor("x2i", [S, D], F32, kind="Internal").ap()
        x3ap = nc.dram_tensor("x3i", [S, D], F32, kind="Internal").ap()
        build_moe(nc, dict(x="x1", gate_w="gate_w", lb_bias="lb_bias", egw="egw",
                           euw="euw", edw="edw", sgw="sgw", suw="suw", sdw="sdw",
                           Sstack="Sstack", Cstack="Cstack", out=x2ap), "full", env=env)
        b.barrier()
        namesD = dict(x=x2ap, out=x3ap)
        for w in "pqkvo":
            namesD["W" + w] = "W" + w
            namesD["b" + w] = "b" + w
        build_attn1(nc, namesD, env=env)
        b.barrier()
        build_ffn1(nc, dict(x=x3ap, k0="k0", k1="k1", k2="k2", s0="s0", s1="s1",
                            s2="s2", out="out"), env=env)
        b.barrier()
    return nc



NCORES = 8


def kernel(**inputs):
    ii = {k: np.ascontiguousarray(v) for k, v in inputs.items()}
    x = ii["L0_attn_in_features"]
    cores = list(range(NCORES))
    lbr = ii["L0_ffn_lb_bias"].reshape(1, 8)

    nc1 = bass.Bass("TRN2", target_bir_lowering=False, debug=False)
    build_launch1(nc1)
    maps1 = [dict(x=x[i], wq=ii["L0_attn_q_proj_weight"], wk=ii["L0_attn_k_proj_weight"],
                  wv=ii["L0_attn_v_proj_weight"], wo=ii["L0_attn_o_proj_weight"],
                  gate_w=ii["L0_ffn_gate_w"], lb_bias=lbr) for i in cores]
    res1 = run_bass_kernel_spmd(nc1, maps1, core_ids=cores)
    x1 = [res1.results[i]["x1"] for i in cores]
    Sg = np.zeros((8, 8), np.float32)
    Cg = np.zeros((8, 8), np.float32)
    for i in cores:
        p = res1.results[i]["part"][:, 0]
        Sg[:, i] = p[0:8]
        Cg[:, i] = p[8:16]

    s0r = np.asarray(ii["L1_ffn_s0"], np.float32).reshape(1, 1)
    s1r = np.asarray(ii["L1_ffn_s1"], np.float32).reshape(1, 1)
    s2r = np.asarray(ii["L1_ffn_s2"], np.float32).reshape(1, 1)
    nc2 = bass.Bass("TRN2", target_bir_lowering=False, debug=False)
    build_launch2(nc2)
    maps2 = []
    for i in cores:
        m = dict(x1=x1[i], gate_w=ii["L0_ffn_gate_w"], lb_bias=lbr,
                 egw=ii["L0_ffn_expert_gate_w"], euw=ii["L0_ffn_expert_up_w"],
                 edw=ii["L0_ffn_expert_down_w"], sgw=ii["L0_ffn_shared_gate_w"],
                 suw=ii["L0_ffn_shared_up_w"], sdw=ii["L0_ffn_shared_down_w"],
                 Sstack=Sg, Cstack=Cg,
                 k0=ii["L1_ffn_k0"].view(np.uint8), k1=ii["L1_ffn_k1"].view(np.uint8),
                 k2=ii["L1_ffn_k2"].view(np.uint8), s0=s0r, s1=s1r, s2=s2r)
        for w in "pqkvo":
            m["W" + w] = ii["L1_attn_W" + w]
            m["b" + w] = ii["L1_attn_b" + w].reshape(1, D)
        maps2.append(m)
    res2 = run_bass_kernel_spmd(nc2, maps2, core_ids=cores)
    return np.stack([res2.results[i]["out"] for i in cores]).astype(np.float32)


# version 06 (final)
# speedup vs baseline: 2.2175x; 1.1533x over previous
import numpy as np
import ml_dtypes
from contextlib import ExitStack

import concourse.bass as bass
import concourse.mybir as mybir
from concourse.bass_utils import run_bass_kernel_spmd

F32 = mybir.dt.float32
BF16 = mybir.dt.bfloat16
U8 = mybir.dt.uint8
AF = mybir.ActivationFunctionType
ALU = mybir.AluOpType
AX = mybir.AxisListType

S = 4096
D = 1024
NT = S // 128
EPS = 1e-5

BF = ml_dtypes.bfloat16



class Ctr:

    def __init__(self, nc, ctx, name):
        self.sem = ctx.enter_context(nc.semaphore(name))
        self.val = 0

    def inc(self, inst, n=1):
        inst.then_inc(self.sem, n)
        self.val += n
        return self.val


class B:

    def __init__(self, nc, ctx, block):
        self.nc = nc
        self.ctx = ctx
        self.block = block
        self.c = {}
        self._last_wait = {}

    def ctr(self, name):
        if name not in self.c:
            self.c[name] = Ctr(self.nc, self.ctx, name)
        return self.c[name]

    def on(self, engine, fn):
        getattr(self.block, engine)(lambda eng: fn(W(self, engine, eng)))

    def barrier(self):
        cs = list(self.c.values())
        for eng in ("sync", "gpsimd", "tensor", "vector", "scalar"):
            def f(w, cs=cs):
                for c in cs:
                    w.wait(c)
            self.on(eng, f)


class W:

    def __init__(self, b, name, eng):
        self.b = b
        self.name = name
        self.eng = eng

    def wait(self, ctr, val=None):
        v = ctr.val if val is None else val
        if v <= 0:
            return
        key = (self.name, id(ctr))
        if self.b._last_wait.get(key, -1) >= v:
            return
        self.eng.wait_ge(ctr.sem, v)
        self.b._last_wait[key] = v

    def __getattr__(self, item):
        return getattr(self.eng, item)



def _dram(nc, names, key, shape, dt, kind):
    v = names[key]
    if isinstance(v, str):
        return nc.dram_tensor(v, shape, dt, kind=kind).ap()
    return v


def build_ffn1(nc, names, env=None, px='f1_'):
    x3 = _dram(nc, names, "x", [S, D], F32, "ExternalInput")
    k0 = _dram(nc, names, "k0", [2 * D, 4096], U8, "ExternalInput")
    k1 = _dram(nc, names, "k1", [4096, 4096], U8, "ExternalInput")
    k2 = _dram(nc, names, "k2", [4096, 1024], U8, "ExternalInput")
    s0 = _dram(nc, names, "s0", [1, 1], F32, "ExternalInput")
    s1 = _dram(nc, names, "s1", [1, 1], F32, "ExternalInput")
    s2 = _dram(nc, names, "s2", [1, 1], F32, "ExternalInput")
    out = _dram(nc, names, "out", [S, D], F32, "ExternalOutput")
    w0p = nc.dram_tensor("w0p_scratch", [8, 128, 4096], BF16, kind="Internal")

    with ExitStack() as ctx:
        e = ctx.enter_context
        onesf = e(nc.sbuf_tensor(f"{px}onesf", [1, 128], F32))
        onecol_bf = e(nc.sbuf_tensor(f"{px}onecol_bf", [128, 1], BF16))
        nhcol = e(nc.sbuf_tensor(f"{px}nhcol", [1, 128], F32))
        nhrow = e(nc.sbuf_tensor(f"{px}nhrow", [1, 1024], F32))
        svals = e(nc.sbuf_tensor(f"{px}svals", [1, 3], F32))
        vec2s0 = e(nc.sbuf_tensor(f"{px}vec2s0", [128, 1], F32))
        vec4s1 = e(nc.sbuf_tensor(f"{px}vec4s1", [128, 1], F32))
        vec4s2 = e(nc.sbuf_tensor(f"{px}vec4s2", [128, 1], F32))
        x3c = [e(nc.sbuf_tensor(f"{px}x3c{i}", [128, 4 * 1024], F32)) for i in range(2)]
        stat = e(nc.sbuf_tensor(f"{px}stat", [128, 12], F32))
        mv = e(nc.sbuf_tensor(f"{px}mv", [128, 2], F32))
        rstd = e(nc.sbuf_tensor(f"{px}rstd", [128, 1], F32))
        lnt = [e(nc.sbuf_tensor(f"{px}lnt{i}", [128, 1024], BF16)) for i in range(2)]
        ln3T = [e(nc.sbuf_tensor(f"{px}ln3T{i}", [128, 8 * 512], BF16, is_transpose_dest=True))
                for i in range(2)]
        r1 = e(nc.sbuf_tensor(f"{px}r1", [128, 32 * 512], BF16))
        r2 = e(nc.sbuf_tensor(f"{px}r2", [128, 32 * 512], BF16))
        NWB = 4
        wb = [e(nc.sbuf_tensor(f"{px}wb{i}", [128, 1024], BF16)) for i in range(NWB)]
        csum1 = e(nc.sbuf_tensor(f"{px}csum1", [1, 512], F32))
        csum2 = e(nc.sbuf_tensor(f"{px}csum2", [1, 512], F32))
        outb = [e(nc.sbuf_tensor(f"{px}outb{i}", [128, 1024], F32)) for i in range(2)]
        PS = e(nc.psum_tensor(f"{px}PS", [128, 4096], F32))

        if env is None:
            block = e(nc.Block())
            b = B(nc, ctx, block)
        else:
            block = env["block"]
            b = env["b"]
        sl = b.ctr("sl"); sg = b.ctr("sg"); pe = b.ctr("pe")
        dv = b.ctr("dv"); sc = b.ctr("sc"); sx = b.ctr("sx")
        sob = [b.ctr("sob0"), b.ctr("sob1")]

        def p0_sync(s):
            sl.inc(s.dma_start(out=svals[0:1, 0:1], in_=s0), 16)
            sl.inc(s.dma_start(out=svals[0:1, 1:2], in_=s1), 16)
            sl.inc(s.dma_start(out=svals[0:1, 2:3], in_=s2), 16)
        b.on("sync", p0_sync)

        def p0_vec(v):
            v.memset(onesf[:, :], 1.0)
            v.memset(onecol_bf[:, :], 1.0)
            v.memset(nhcol[:, :], -0.5)
            dv.inc(v.memset(nhrow[:, :], -0.5))
        b.on("vector", p0_vec)

        for i, (srci, dst, mlt) in enumerate([(0, vec2s0, 2.0), (1, vec4s1, 4.0), (2, vec4s2, 4.0)]):
            def t_body(t, i=i, srci=srci):
                t.wait(sl)
                t.wait(dv)
                pe.inc(t.matmul(PS[:, 0:1], onesf[0:1, :], svals[0:1, srci:srci + 1],
                                start=True, stop=True))
            b.on("tensor", t_body)

            def v_body(v, dst=dst, mlt=mlt):
                v.wait(pe)
                dv.inc(v.tensor_scalar_mul(dst[:, 0:1], PS[:, 0:1], float(mlt)))
            b.on("vector", v_body)

        with ExitStack() as wctx:
            w0a = wctx.enter_context(nc.sbuf_tensor(f"{px}w0a", [128, 4096], BF16))
            w0b = wctx.enter_context(nc.sbuf_tensor(f"{px}w0b", [128, 4096], BF16))
            w0d = wctx.enter_context(nc.sbuf_tensor(f"{px}w0d", [128, 4096], BF16))
            sw = b.ctr("sw")
            for it in range(8):
                def g_body(g, it=it):
                    g.wait(sw)
                    sg.inc(g.dma_start(out=w0a[:, :], in_=k0[it * 128:(it + 1) * 128, :]), 16)
                    sg.inc(g.dma_start(out=w0b[:, :], in_=k0[1024 + it * 128:1024 + (it + 1) * 128, :]), 16)
                b.on("gpsimd", g_body)

                def v_body(v, it=it):
                    v.wait(sg)
                    v.wait(sw)
                    dv.inc(v.tensor_sub(w0d[:, :], w0a[:, :], w0b[:, :]))
                b.on("vector", v_body)

                def s_body(s, it=it):
                    s.wait(dv)
                    sw.inc(s.dma_start(out=w0p.ap()[it, :, :], in_=w0d[:, :]), 16)
                b.on("sync", s_body)

        wload = [0] * NWB
        wcons = [0] * NWB
        bank_ev = [0] * 8
        bank_ev_kind = [None] * 8
        bank_ev[0] = dv.val
        bank_ev_kind[0] = 'dv'
        wslot = [0]
        ln_sx = [0, 0]
        ln_pe_free = [0, 0]
        x3_dv_free = [0, 0]
        sob_val = [0, 0]

        def emit_ln(c):
            cb = c % 2

            def s_body(s, c=c, cb=cb):
                s.wait(dv, x3_dv_free[cb])
                for st in range(4):
                    sl.inc(s.dma_start(out=x3c[cb][:, st * 1024:(st + 1) * 1024],
                                       in_=x3[(c * 4 + st) * 128:(c * 4 + st + 1) * 128, :]), 16)
            b.on("sync", s_body)
            ld_done = sl.val

            for st in range(4):
                lb = lnt[st % 2]

                def v_body(v, c=c, cb=cb, st=st, lb=lb):
                    v.wait(sl, ld_done)
                    v.wait(dv)
                    v.wait(sx)
                    v.bn_stats(stat[:, 0:6], x3c[cb][:, st * 1024:st * 1024 + 512])
                    dv.inc(v.bn_stats(stat[:, 6:12], x3c[cb][:, st * 1024 + 512:st * 1024 + 1024]))
                    v.wait(dv)
                    dv.inc(v.bn_aggr(mv[:, :], stat[:, :]))
                    v.wait(dv)
                    dv.inc(v.tensor_scalar_add(rstd[:, 0:1], mv[:, 1:2], EPS))
                b.on("vector", v_body)

                def s2_body(s):
                    s.wait(dv)
                    sc.inc(s.activation(rstd[:, 0:1], rstd[:, 0:1], AF.Sqrt))
                b.on("scalar", s2_body)

                def v2_body(v, c=c, cb=cb, st=st, lb=lb):
                    v.wait(sc)
                    dv.inc(v.reciprocal(rstd[:, 0:1], rstd[:, 0:1]))
                    v.wait(dv)
                    dv.inc(v.tensor_scalar(lb[:, :], x3c[cb][:, st * 1024:(st + 1) * 1024],
                                           mv[:, 0:1], rstd[:, 0:1], ALU.subtract, ALU.mult))
                b.on("vector", v2_body)
                apply_dv = dv.val

                def s3_body(s, c=c, cb=cb, st=st, lb=lb, apply_dv=apply_dv):
                    s.wait(dv, apply_dv)
                    if st == 0:
                        s.wait(pe, ln_pe_free[cb])
                    for it in range(8):
                        sx.inc(s.dma_start_transpose(
                            ln3T[cb][:, it * 512 + st * 128:it * 512 + (st + 1) * 128],
                            lb[:, it * 128:(it + 1) * 128]), 16)
                b.on("sync", s3_body)
            ln_sx[cb] = sx.val

        def stream_load(src_ap):
            i = wslot[0] % NWB
            wslot[0] += 1

            def g_body(g, i=i, src_ap=src_ap):
                g.wait(pe, wcons[i])
                sg.inc(g.dma_start(out=wb[i][:, :], in_=src_ap), 16)
            b.on("gpsimd", g_body)
            wload[i] = sg.val
            return i

        def emit_stage01(c, cb, stage):
            nk = 8 if stage == 0 else 32
            dst = r1 if stage == 0 else r2
            svec = vec2s0 if stage == 0 else vec4s1
            for mg in range(4):
                stop_pe = [0] * 8
                for kb in range(nk):
                    if stage == 0:
                        i = stream_load(w0p.ap()[kb, :, mg * 1024:(mg + 1) * 1024])
                    else:
                        i = stream_load(k1[kb * 128:(kb + 1) * 128, mg * 1024:(mg + 1) * 1024])

                    def t_body(t, c=c, cb=cb, mg=mg, kb=kb, i=i, nk=nk, stage=stage):
                        t.wait(sg, wload[i])
                        if kb == 0:
                            t.wait(sx, ln_sx[cb])
                            if stage == 1 and mg == 0:
                                t.wait(dv)
                        for j in range(8):
                            if kb == 0:
                                t.wait(*bank_dep(j))
                            q = t.matmul(PS[:, j * 512:(j + 1) * 512],
                                         wb[i][:, j * 128:(j + 1) * 128],
                                         ln3T[cb][:, kb * 512:(kb + 1) * 512] if stage == 0
                                         else r1[:, kb * 512:(kb + 1) * 512],
                                         start=(kb == 0), stop=(kb == nk - 1 and stage == 0))
                            if kb == nk - 1:
                                pe.inc(q)
                                stop_pe[j] = pe.val
                            elif j == 7:
                                pe.inc(q)
                    b.on("tensor", t_body)
                    wcons[i] = pe.val

                if stage == 1:
                    def t2_body(t, c=c, mg=mg):
                        for j in range(8):
                            pe.inc(t.matmul(PS[:, j * 512:(j + 1) * 512], nhcol[0:1, :],
                                            csum1[0:1, :], start=False, stop=True))
                            stop_pe[j] = pe.val
                    b.on("tensor", t2_body)


                for j in range(8):
                    def s_body(s, c=c, mg=mg, j=j, stage=stage):
                        s.wait(pe, stop_pe[j])
                        sc.inc(s.activation(dst[:, (mg * 8 + j) * 512:(mg * 8 + j + 1) * 512],
                                            PS[:, j * 512:(j + 1) * 512], AF.Relu,
                                            scale=svec[:, 0:1]))
                        bank_ev[j] = sc.val
                        bank_ev_kind[j] = 'sc'
                    b.on("scalar", s_body)

        def bank_dep(j):
            if bank_ev_kind[j] == 'sc':
                return (sc, bank_ev[j])
            if bank_ev_kind[j] == 'dv':
                return (dv, bank_ev[j])
            return (pe, 0)

        def emit_colsum(c, src, dstrow, rdy_kind, rdy_val):
            def t_body(t):
                t.wait(rdy_kind, rdy_val)
                t.wait(*bank_dep(0))
                for tt in range(32):
                    q = t.matmul(PS[0:1, 0:512], onecol_bf[:, 0:1],
                                 src[:, tt * 512:(tt + 1) * 512],
                                 start=(tt == 0), stop=(tt == 31))
                pe.inc(q)
            b.on("tensor", t_body)
            mmv = pe.val

            def v_body(v):
                v.wait(pe, mmv)
                dv.inc(v.tensor_copy(dstrow[0:1, :], PS[0:1, 0:512]))
            b.on("vector", v_body)
            bank_ev[0] = dv.val
            bank_ev_kind[0] = 'dv'

        def emit_stage2(c, cb):
            stop_pe = [0] * 8
            for tt in range(32):
                i = stream_load(k2[tt * 128:(tt + 1) * 128, :])

                def t_body(t, c=c, cb=cb, tt=tt, i=i):
                    t.wait(sg, wload[i])
                    for st in range(4):
                        for n in range(2):
                            j = st * 2 + n
                            if tt == 0:
                                t.wait(*bank_dep(j))
                                if j == 0:
                                    t.wait(dv)
                            q = t.matmul(PS[:, j * 512:(j + 1) * 512],
                                         r2[:, tt * 512 + st * 128:tt * 512 + (st + 1) * 128],
                                         wb[i][:, n * 512:(n + 1) * 512],
                                         start=(tt == 0), stop=False)
                            if st == 3 and n == 1:
                                pe.inc(q)
                b.on("tensor", t_body)
                wcons[i] = pe.val

            def t2_body(t, c=c):
                for st in range(4):
                    for n in range(2):
                        j = st * 2 + n
                        pe.inc(t.matmul(PS[:, j * 512:(j + 1) * 512],
                                        csum2[0:1, st * 128:(st + 1) * 128],
                                        nhrow[0:1, n * 512:(n + 1) * 512],
                                        start=False, stop=True))
                        stop_pe[j] = pe.val
            b.on("tensor", t2_body)

            for st in range(4):
                ob = outb[st % 2]

                def v_body(v, c=c, cb=cb, st=st, ob=ob):
                    v.wait(pe, stop_pe[st * 2 + 1])
                    v.wait(sob[st % 2], sob_val[st % 2])
                    for n in range(2):
                        j = st * 2 + n
                        dv.inc(v.scalar_tensor_tensor(
                            ob[:, n * 512:(n + 1) * 512], PS[:, j * 512:(j + 1) * 512],
                            vec4s2[:, 0:1],
                            x3c[cb][:, st * 1024 + n * 512:st * 1024 + (n + 1) * 512],
                            ALU.mult, ALU.add))
                        bank_ev[j] = dv.val
                        bank_ev_kind[j] = 'dv'
                b.on("vector", v_body)

                def s_body(s, c=c, st=st, ob=ob):
                    s.wait(dv)
                    sob[st % 2].inc(s.dma_start(out=out[(c * 4 + st) * 128:(c * 4 + st + 1) * 128, :],
                                                in_=ob[:, :]), 16)
                    sob_val[st % 2] = sob[st % 2].val
                b.on("sync", s_body)
            x3_dv_free[cb] = dv.val

        emit_ln(0)
        for c in range(8):
            cb = c % 2
            emit_stage01(c, cb, 0)
            ln_pe_free[cb] = pe.val
            emit_colsum(c, r1, csum1, sc, sc.val)
            emit_stage01(c, cb, 1)
            emit_colsum(c, r2, csum2, sc, sc.val)
            emit_stage2(c, cb)
            if c + 1 < 8:
                emit_ln(c + 1)

        def fin(s):
            s.wait(sob[0])
            s.wait(sob[1])
        b.on("sync", fin)
    return nc

def build_attn0(nc, names, env=None, px='a0_'):
    x0 = _dram(nc, names, "x", [S, D], F32, "ExternalInput")
    Wd = {w: _dram(nc, names, w, [D, D], F32, "ExternalInput")
          for w in ("wq", "wk", "wv", "wo")}
    out = _dram(nc, names, "out", [S, D], F32, "ExternalOutput")
    ln0h = nc.dram_tensor("ln0T_scratch", [8, 128, S], BF16, kind="Internal")
    yph = nc.dram_tensor("yp_scratch", [8, 128, S], F32, kind="Internal")

    masks_np = np.zeros((128, 4 * 1024), dtype=BF)
    for p in range(4):
        for kl in range(128):
            q0 = kl + 128 * p
            if q0 < 512:
                masks_np[kl, p * 1024 + q0:p * 1024 + 512] = 1.0
                masks_np[kl, p * 1024 + 512 + q0:p * 1024 + 1024] = 1.0

    with ExitStack() as ctx:
        e = ctx.enter_context
        ident = e(nc.sbuf_tensor(f"{px}ident", [128, 128], BF16))
        identf = e(nc.sbuf_tensor(f"{px}identf", [128, 128], F32))
        onesf = e(nc.sbuf_tensor(f"{px}onesf", [128, 64], F32))
        WT = {w: e(nc.sbuf_tensor(f"{px}WT{w}", [128, 8192], BF16)) for w in ("wq", "wk", "wv", "wo")}
        PSbox = [None]

        def psAB_ap(i, c0, c1):
            return PSbox[0][:, i * 1024 + c0:i * 1024 + c1]

        def pq_ap():
            return PSbox[0][:, 3584:4096]

        if env is None:
            block = e(nc.Block())
            b = B(nc, ctx, block)
        else:
            block = env["block"]
            b = env["b"]
        sl = b.ctr("sl"); sg = b.ctr("sg"); pe = b.ctr("pe")
        dv = b.ctr("dv"); sc = b.ctr("sc"); sp = b.ctr("sp"); sy = b.ctr("sy")
        sx = b.ctr("sx")
        sob = [b.ctr("sob0"), b.ctr("sob1")]
        soo = [b.ctr("soo0"), b.ctr("soo1")]

        identh = nc.inline_tensor(np.eye(128, dtype=BF), f"{px}identnp")
        identfh = nc.inline_tensor(np.eye(128, dtype=np.float32), f"{px}identfnp")
        masksh = nc.inline_tensor(masks_np, f"{px}masksnp")

        def a0_sync(s):
            sl.inc(s.dma_start(out=ident[:, :], in_=identh.ap()), 16)
            sl.inc(s.dma_start(out=identf[:, :], in_=identfh.ap()), 16)
        b.on("sync", a0_sync)

        def a0_vec(v):
            dv.inc(v.memset(onesf[:, :], 1.0))
        b.on("vector", a0_vec)

        with ExitStack() as actx:
            wstage = actx.enter_context(nc.sbuf_tensor(f"{px}wstage", [128, 8192], BF16))
            ln0T = actx.enter_context(nc.sbuf_tensor(f"{px}ln0T", [128, 8 * 4096], BF16,
                                                     is_transpose_dest=True))
            ptw = actx.enter_context(nc.psum_tensor(f"{px}ptw", [128, 1024], BF16))
            wcons_pe = [0]
            for wi, w in enumerate(("wq", "wk", "wv", "wo")):
                def g_body(g, w=w, wi=wi):
                    g.wait(pe, wcons_pe[0])
                    sg.inc(g.dma_start(out=wstage[:, :].rearrange("p (t i) -> p t i", t=8),
                                       in_=Wd[w].rearrange("(t p) i -> p t i", p=128)), 16)
                b.on("gpsimd", g_body)
                ld = sg.val

                for u in range(8):
                    def t_body(t, w=w, u=u, ld=ld):
                        t.wait(sg, ld)
                        t.wait(sl)
                        if u > 0:
                            t.wait(dv)
                        for tt in range(8):
                            q = t.transpose(ptw[:, tt * 128:(tt + 1) * 128],
                                            wstage[:, tt * 1024 + u * 128:tt * 1024 + (u + 1) * 128],
                                            ident[:, :])
                        pe.inc(q)
                    b.on("tensor", t_body)

                    def v_body(v, w=w, u=u):
                        v.wait(pe)
                        dv.inc(v.tensor_copy(WT[w][:, u * 1024:(u + 1) * 1024], ptw[:, :]))
                    b.on("vector", v_body)
                wcons_pe[0] = pe.val

            mvg = actx.enter_context(nc.sbuf_tensor(f"{px}mvg", [128, 8], F32))
            rstdg = actx.enter_context(nc.sbuf_tensor(f"{px}rstdg", [128, 4], F32))
            statg = actx.enter_context(nc.sbuf_tensor(f"{px}statg", [128, 48], F32))
            xg = [actx.enter_context(nc.sbuf_tensor(f"{px}xg{i}", [128, 4 * 1024], F32)) for i in range(2)]
            lng = [actx.enter_context(nc.sbuf_tensor(f"{px}lng{i}", [128, 4 * 1024], BF16)) for i in range(2)]
            ln_apl = [0, 0]
            ln_sx = [0, 0]
            for g in range(8):
                gb = g % 2

                def s_body(s, g=g, gb=gb):
                    s.wait(dv, ln_apl[gb])
                    for u in range(4):
                        st = g * 4 + u
                        sl.inc(s.dma_start(out=xg[gb][:, u * 1024:(u + 1) * 1024],
                                           in_=x0[st * 128:(st + 1) * 128, :]), 16)
                b.on("sync", s_body)
                ld = sl.val

                def v_body(v, g=g, gb=gb, ld=ld):
                    v.wait(sl, ld)
                    v.wait(dv)
                    for u in range(4):
                        v.bn_stats(statg[:, u * 12:u * 12 + 6], xg[gb][:, u * 1024:u * 1024 + 512])
                        v.bn_stats(statg[:, u * 12 + 6:u * 12 + 12], xg[gb][:, u * 1024 + 512:u * 1024 + 1024])
                    dv.inc(v.engine_nop())
                    v.wait(dv)
                    for u in range(4):
                        v.bn_aggr(mvg[:, u * 2:u * 2 + 2], statg[:, u * 12:u * 12 + 12])
                    dv.inc(v.engine_nop())
                    v.wait(dv)
                    for u in range(4):
                        q = v.tensor_scalar_add(rstdg[:, u:u + 1], mvg[:, u * 2 + 1:u * 2 + 2], EPS)
                    dv.inc(q)
                b.on("vector", v_body)

                def s2_body(s, g=g):
                    s.wait(dv)
                    s.activation(rstdg[:, :], rstdg[:, :], AF.Ln)
                    sc.inc(s.activation(rstdg[:, :], rstdg[:, :], AF.Exp, scale=-0.5))
                b.on("scalar", s2_body)

                def v2_body(v, g=g, gb=gb):
                    v.wait(sc)
                    v.wait(pe, ln_sx[gb])
                    for u in range(4):
                        q = v.tensor_scalar(lng[gb][:, u * 1024:(u + 1) * 1024],
                                            xg[gb][:, u * 1024:(u + 1) * 1024],
                                            mvg[:, u * 2:u * 2 + 1], rstdg[:, u:u + 1],
                                            ALU.subtract, ALU.mult)
                    dv.inc(q)
                b.on("vector", v2_body)
                ln_apl[gb] = dv.val
                apd = dv.val

                for u in range(4):
                    st = g * 4 + u

                    def t3_body(t, g=g, gb=gb, u=u, st=st, apd=apd):
                        t.wait(dv, apd)
                        t.wait(dv)
                        for it in range(8):
                            q = t.transpose(ptw[:, it * 128:(it + 1) * 128],
                                            lng[gb][:, u * 1024 + it * 128:u * 1024 + (it + 1) * 128],
                                            ident[:, :])
                        pe.inc(q)
                    b.on("tensor", t3_body)
                    trv = pe.val

                    def v3_body(v, g=g, st=st, trv=trv):
                        v.wait(pe, trv)
                        dst = bass.AP(ln0T, st * 128, [[8 * 4096, 128], [4096, 8], [1, 128]])
                        srcap = bass.AP(ptw, 0, [[1024, 128], [128, 8], [1, 128]])
                        dv.inc(v.tensor_copy(dst, srcap))
                    b.on("vector", v3_body)
                ln_sx[gb] = pe.val

            def spill(s):
                s.wait(dv)
                for it in range(8):
                    sp.inc(s.dma_start(out=ln0h.ap()[it, :, :], in_=ln0T[:, it * 4096:(it + 1) * 4096]), 16)
            b.on("sync", spill)

        PS = e(nc.psum_tensor(f"{px}PS", [128, 4096], F32))
        PSbox[0] = PS
        a2ctx = ExitStack()
        a2 = a2ctx.enter_context
        masks = a2(nc.sbuf_tensor(f"{px}masks", [128, 4 * 1024], BF16))
        KTg = a2(nc.sbuf_tensor(f"{px}KTg", [128, 4 * 4096], BF16))
        Vg = a2(nc.sbuf_tensor(f"{px}Vg", [128, 32 * 520], BF16))
        lncb = [a2(nc.sbuf_tensor(f"{px}lncb{i}", [128, 8 * 512], BF16)) for i in range(3)]
        qcb = a2(nc.sbuf_tensor(f"{px}qcb", [128, 4 * 512], BF16))
        ab = [a2(nc.sbuf_tensor(f"{px}ab{i}", [128, 1024], BF16)) for i in range(3)]
        brs2 = [a2(nc.sbuf_tensor(f"{px}brs2_{i}", [1, 1024], F32)) for i in range(2)]
        pvn = [a2(nc.sbuf_tensor(f"{px}pvn{i}", [65, 1024], F32)) for i in range(2)]
        brd = a2(nc.sbuf_tensor(f"{px}brd", [64, 512], F32))
        attnoT = a2(nc.sbuf_tensor(f"{px}attnoT", [128, 4 * 512], BF16))
        ybuf = [a2(nc.sbuf_tensor(f"{px}ybuf{i}", [128, 512], F32)) for i in range(2)]
        y0b = [a2(nc.sbuf_tensor(f"{px}y0b{i}", [128, 512], F32)) for i in range(2)]

        def mload(s):
            s.wait(sx)
            sl.inc(s.dma_start(out=masks[:, :], in_=masksh.ap()), 16)
        b.on("sync", mload)
        masks_ld = sl.val

        def ln_chunk_load(s, buf, n):
            sl.inc(s.dma_start(out=buf[:, :].rearrange("p (it w) -> p it w", it=8),
                               in_=ln0h.ap()[:, :, n * 512:(n + 1) * 512].rearrange("it p w -> p it w")), 16)
            return sl.val

        pvev = [0]
        recv_of = [0] * 4
        pbr_free = [0]
        abst = [0, 0, 0]
        psst = [[0, 'x'], [0, 'x']]
        lnld = [0, 0, 0]
        lncons = [0, 0, 0]
        qev = [0]
        pq_free = [0, 'p']
        yspill = [0, 0]
        y0ld = [0, 0]

        for gp in range(2):
            def v_init(v, gp=gp):
                v.wait(pe)
                v.wait(sp)
                dv.inc(v.memset(Vg[:, :], 1.0))
            b.on("vector", v_init)
            vg_init_dv = dv.val

            kv_ev = [0, 0]
            kv_kind = ['p', 'p']
            for n in range(8):
                buf = lncb[n % 3]

                def s_body(s, gp=gp, n=n, buf=buf):
                    s.wait(pe, lncons[n % 3])
                    if gp == 0 and n == 0:
                        s.wait(sp)
                    lnld[n % 3] = ln_chunk_load(s, buf, n)
                b.on("sync", s_body)

                for j in range(4):
                    hf = j % 2

                    def t_body(t, gp=gp, n=n, j=j, buf=buf, hf=hf):
                        t.wait(sl, lnld[n % 3])
                        if kv_kind[hf] == 's':
                            t.wait(sc, kv_ev[hf])
                        elif kv_kind[hf] == 'd':
                            t.wait(dv, kv_ev[hf])
                        for it in range(8):
                            q = t.matmul(PS[:, hf * 512:(hf + 1) * 512],
                                         WT["wk"][:, it * 1024 + gp * 512 + j * 128:it * 1024 + gp * 512 + (j + 1) * 128],
                                         buf[:, it * 512:(it + 1) * 512],
                                         start=(it == 0), stop=(it == 7))
                        pe.inc(q)
                    b.on("tensor", t_body)
                    mmv = pe.val
                    lncons[n % 3] = pe.val

                    def vc_body(v, gp=gp, n=n, j=j, hf=hf, mmv=mmv):
                        v.wait(pe, mmv)
                        dv.inc(v.tensor_copy(KTg[:, j * 4096 + n * 512:j * 4096 + (n + 1) * 512],
                                             PS[:, hf * 512:(hf + 1) * 512]))
                        kv_ev[hf] = dv.val
                        kv_kind[hf] = 'd'
                    b.on("vector", vc_body)

                for m4 in range(4):
                    stile = n * 4 + m4
                    hf = m4 % 2

                    def t_body(t, gp=gp, n=n, m4=m4, buf=buf, stile=stile, hf=hf):
                        if kv_kind[hf] == 's':
                            t.wait(sc, kv_ev[hf])
                        elif kv_kind[hf] == 'd':
                            t.wait(dv, kv_ev[hf])
                        if gp == 0 and n == 0 and m4 == 0:
                            t.wait(dv, vg_init_dv)
                        for it in range(8):
                            q = t.matmul(PS[:, hf * 512:(hf + 1) * 512],
                                         buf[:, it * 512 + m4 * 128:it * 512 + (m4 + 1) * 128],
                                         WT["wv"][:, it * 1024 + gp * 512:it * 1024 + (gp + 1) * 512],
                                         start=(it == 0), stop=(it == 7))
                        pe.inc(q)
                    b.on("tensor", t_body)
                    mmv = pe.val
                    lncons[n % 3] = pe.val

                    def v_body(v, gp=gp, stile=stile, hf=hf, mmv=mmv):
                        v.wait(pe, mmv)
                        v.wait(dv, vg_init_dv)
                        dst = bass.AP(Vg, stile * 520, [[32 * 520, 128], [65, 8], [1, 64]])
                        srcap = bass.AP(PS, hf * 512, [[4096, 128], [64, 8], [1, 64]])
                        dv.inc(v.tensor_copy(dst, srcap))
                        kv_ev[hf] = dv.val
                        kv_kind[hf] = 'd'
                    b.on("vector", v_body)

            kv_done_sc = sc.val
            kv_done_dv = dv.val

            for qc in range(8):
                buf = lncb[qc % 3]

                def s_body(s, gp=gp, qc=qc, buf=buf):
                    s.wait(pe, lncons[qc % 3])
                    lnld[qc % 3] = ln_chunk_load(s, buf, qc)
                b.on("sync", s_body)

                for j in range(4):
                    def t_q(t, gp=gp, qc=qc, j=j, buf=buf):
                        t.wait(sl, lnld[qc % 3])
                        if pq_free[1] == 's':
                            t.wait(sc, pq_free[0])
                        elif pq_free[1] == 'd':
                            t.wait(dv, pq_free[0])
                        if j == 0 and qc == 0:
                            t.wait(sc, kv_done_sc)
                            t.wait(dv, kv_done_dv)
                        for it in range(8):
                            q = t.matmul(pq_ap(),
                                         WT["wq"][:, it * 1024 + gp * 512 + j * 128:it * 1024 + gp * 512 + (j + 1) * 128],
                                         buf[:, it * 512:(it + 1) * 512],
                                         start=(it == 0), stop=(it == 7))
                        pe.inc(q)
                    b.on("tensor", t_q)
                    mmv = pe.val
                    lncons[qc % 3] = pe.val

                    def vc_q(v, gp=gp, qc=qc, j=j, mmv=mmv):
                        v.wait(pe, mmv)
                        dv.inc(v.tensor_copy(qcb[:, j * 512:(j + 1) * 512], pq_ap()))
                        pq_free[0] = dv.val
                        pq_free[1] = 'd'
                    b.on("vector", vc_q)
                qev[0] = dv.val

                nkb = 4 * (qc + 1)
                for j in range(4):
                    av_stop = [0]
                    sc_base = sc.val
                    dv_base = dv.val
                    smm = [0] * nkb
                    ab_wait = [0] * nkb
                    mdv_of = {}

                    def emit_av(kb, pbk, abk, diag, mdv, expv):
                        def t_av(t, gp=gp, qc=qc, j=j, kb=kb, pbk=pbk, abk=abk,
                                 diag=diag, mdv=mdv, expv=expv):
                            if diag:
                                t.wait(dv, mdv)
                            else:
                                t.wait(sc, expv)
                            if kb == 0:
                                t.wait(dv, pvev[0])
                            aA = ab[abk]
                            t.matmul(PS[0:65, 2048:2560], Vg[:, kb * 520 + 2 * j * 65:kb * 520 + 2 * j * 65 + 65],
                                     aA[:, 0:512], start=(kb == 0), stop=(kb == nkb - 1))
                            q = t.matmul(PS[0:65, 2560:3072], Vg[:, kb * 520 + (2 * j + 1) * 65:kb * 520 + (2 * j + 1) * 65 + 65],
                                         aA[:, 512:1024], start=(kb == 0), stop=(kb == nkb - 1))
                            pe.inc(q)
                            abst[abk] = pe.val
                        b.on("tensor", t_av)
                        av_stop[0] = pe.val

                    ndiag = [0]
                    pend = [None]
                    for kb in range(nkb):
                        pbk = kb % 2
                        abk = kb % 3
                        diag = (kb >= 4 * qc)
                        expv = sc_base + kb + 1
                        if diag:
                            ndiag[0] += 1
                            mdv_of[kb] = dv_base + ndiag[0]

                        def t_s(t, gp=gp, qc=qc, j=j, kb=kb, pbk=pbk, expv=expv):
                            t.wait(dv, qev[0])
                            if psst[pbk][1] == 's':
                                t.wait(sc, psst[pbk][0])
                            t.matmul(psAB_ap(pbk, 0, 512), KTg[0:64, j * 4096 + kb * 128:j * 4096 + (kb + 1) * 128],
                                     qcb[0:64, j * 512:(j + 1) * 512], start=True, stop=True,
                                     tile_position=(0, 0))
                            q = t.matmul(psAB_ap(pbk, 512, 1024), KTg[64:128, j * 4096 + kb * 128:j * 4096 + (kb + 1) * 128],
                                         qcb[64:128, j * 512:(j + 1) * 512], start=True, stop=True,
                                         tile_position=(64, 0))
                            pe.inc(q)
                        b.on("tensor", t_s)
                        smm[kb] = pe.val
                        psst[pbk][0] = sc_base + kb + 1
                        psst[pbk][1] = 's'
                        ab_wait[kb] = abst[abk]

                        if pend[0] is not None:
                            emit_av(*pend[0])
                        pend[0] = (kb, pbk, abk, diag, mdv_of.get(kb, 0), expv)
                    emit_av(*pend[0])
                    pend[0] = None

                    def v_m_all(v, gp=gp, qc=qc, j=j, nkb=nkb):
                        v.wait(sl, masks_ld)
                        for kb in range(4 * qc, nkb):
                            p_ = kb - 4 * qc
                            aA = ab[kb % 3]
                            v.wait(sc, sc_base + kb + 1)
                            dv.inc(v.tensor_mul(aA[:, :], aA[:, :], masks[:, p_ * 1024:(p_ + 1) * 1024]))
                    b.on("vector", v_m_all)

                    def sc_all(s, gp=gp, qc=qc, j=j, nkb=nkb):
                        for kb in range(nkb):
                            pbk = kb % 2
                            aA = ab[kb % 3]
                            s.wait(pe, smm[kb])
                            s.wait(pe, ab_wait[kb])
                            sc.inc(s.activation(aA[:, :], psAB_ap(pbk, 0, 1024), AF.Exp, scale=0.125))
                    b.on("scalar", sc_all)
                    assert sc.val == sc_base + nkb
                    assert dv.val == dv_base + ndiag[0]

                    def v_ev(v, gp=gp, qc=qc, j=j):
                        v.wait(pe, av_stop[0])
                        dv.inc(v.tensor_copy(pvn[j % 2][0:65, :], PS[0:65, 2048:3072]))
                        pvev[0] = dv.val
                        v.wait(dv)
                        v.reciprocal(brs2[j % 2][0:1, 0:512], pvn[j % 2][64:65, 0:512])
                        dv.inc(v.reciprocal(brs2[j % 2][0:1, 512:1024], pvn[j % 2][64:65, 512:1024]))
                    b.on("vector", v_ev)
                    recv_of[j] = dv.val

                    def emit_norm(jj):
                        for hh in range(2):
                            def t_n(t, gp=gp, qc=qc, jj=jj, hh=hh):
                                t.wait(dv, recv_of[jj])
                                t.wait(dv, pbr_free[0])
                                q = t.matmul(PS[0:64, 3072:3584], onesf[0:1, 0:64],
                                             brs2[jj % 2][0:1, hh * 512:(hh + 1) * 512],
                                             start=True, stop=True)
                                pe.inc(q)
                            b.on("tensor", t_n)
                            bcv = pe.val

                            def v_n2(v, gp=gp, qc=qc, jj=jj, hh=hh, bcv=bcv):
                                v.wait(pe, bcv)
                                dv.inc(v.tensor_copy(brd[:, :], PS[0:64, 3072:3584]))
                                pbr_free[0] = dv.val
                                v.wait(dv)
                                base = jj * 512
                                if hh == 0:
                                    dv.inc(v.tensor_mul(attnoT[0:64, base:base + 512],
                                                        pvn[jj % 2][0:64, 0:512], brd[0:64, :]))
                                else:
                                    dv.inc(v.tensor_mul(attnoT[64:128, base:base + 512],
                                                        pvn[jj % 2][0:64, 512:1024], brd[0:64, :]))
                            b.on("vector", v_n2)

                    if j > 0:
                        emit_norm(j - 1)
                    if j == 3:
                        emit_norm(3)

                at_dv = dv.val
                for m in range(8):
                    yb = ybuf[m % 2]
                    obk = m % 2

                    def t_o(t, gp=gp, qc=qc, m=m, at_dv=at_dv, obk=obk):
                        t.wait(dv, at_dv)
                        if psst[obk][1] == 's':
                            t.wait(sc, psst[obk][0])
                        elif psst[obk][1] == 'd':
                            t.wait(dv, psst[obk][0])
                        for itv in range(4):
                            q = t.matmul(PS[:, obk * 512:(obk + 1) * 512],
                                         WT["wo"][:, (4 * gp + itv) * 1024 + m * 128:(4 * gp + itv) * 1024 + (m + 1) * 128],
                                         attnoT[:, itv * 512:(itv + 1) * 512],
                                         start=(itv == 0), stop=(itv == 3))
                        pe.inc(q)
                    b.on("tensor", t_o)
                    omv = pe.val

                    if gp == 0:
                        def v_o(v, qc=qc, m=m, yb=yb, omv=omv, obk=obk):
                            v.wait(pe, omv)
                            v.wait(sy, yspill[m % 2])
                            dv.inc(v.tensor_copy(yb[:, :], PS[:, obk * 512:(obk + 1) * 512]))
                            psst[obk][0] = dv.val
                            psst[obk][1] = 'd'
                        b.on("vector", v_o)

                        def s_o(s, qc=qc, m=m, yb=yb):
                            s.wait(dv)
                            sy.inc(s.dma_start(out=yph.ap()[m, :, qc * 512:(qc + 1) * 512], in_=yb[:, :]), 16)
                            yspill[m % 2] = sy.val
                        b.on("sync", s_o)
                    else:
                        y0 = y0b[m % 2]

                        def s_o0(s, qc=qc, m=m, y0=y0):
                            s.wait(dv)
                            sy.inc(s.dma_start(out=y0[:, :], in_=yph.ap()[m, :, qc * 512:(qc + 1) * 512]), 16)
                            y0ld[m % 2] = sy.val
                        b.on("sync", s_o0)

                        def v_o(v, qc=qc, m=m, yb=yb, y0=y0, omv=omv, obk=obk):
                            v.wait(pe, omv)
                            v.wait(sy, y0ld[m % 2])
                            v.wait(sob[m % 2])
                            dv.inc(v.tensor_add(yb[:, :], PS[:, obk * 512:(obk + 1) * 512], y0[:, :]))
                            psst[obk][0] = dv.val
                            psst[obk][1] = 'd'
                        b.on("vector", v_o)

                        def s_o(s, qc=qc, m=m, yb=yb):
                            s.wait(dv)
                            sob[m % 2].inc(s.dma_start(out=yph.ap()[m, :, qc * 512:(qc + 1) * 512], in_=yb[:, :]), 16)
                        b.on("sync", s_o)

        a2ctx.close()
        x0t = [e(nc.sbuf_tensor(f"{px}x0t{i}", [128, 1024], F32)) for i in range(2)]
        ytb = [e(nc.sbuf_tensor(f"{px}ytb{i}", [128, 1024], F32)) for i in range(2)]
        outt = [e(nc.sbuf_tensor(f"{px}outt{i}", [128, 1024], F32)) for i in range(2)]
        pxt = [PS[:, 0:1024], PS[:, 1024:2048]]
        ep_pe = [0, 0]
        ep_dv = [0, 0]
        for st in range(NT):
            xb = x0t[st % 2]
            yt = ytb[st % 2]
            ob = outt[st % 2]

            def s_body(s, st=st, xb=xb, yt=yt):
                s.wait(dv, ep_dv[st % 2])
                if st < 2:
                    s.wait(sob[0]); s.wait(sob[1])
                    s.wait(pe); s.wait(dv); s.wait(sc)
                sl.inc(s.dma_start(out=xb[:, :], in_=x0[st * 128:(st + 1) * 128, :]), 16)
                sl.inc(s.dma_start(out=yt[:, :].rearrange("p (m w) -> p m w", m=8),
                                   in_=yph.ap()[:, :, st * 128:(st + 1) * 128].rearrange("m p w -> p m w")), 16)
            b.on("sync", s_body)
            ld = sl.val

            def t_body(t, st=st, yt=yt, ld=ld):
                t.wait(sl, ld)
                t.wait(dv, ep_dv[st % 2])
                if st < 2:
                    t.wait(sc)
                    t.wait(dv)
                for m in range(8):
                    q = t.transpose(pxt[st % 2][:, m * 128:(m + 1) * 128],
                                    yt[:, m * 128:(m + 1) * 128], identf[:, :])
                pe.inc(q)
            b.on("tensor", t_body)
            ep_pe[st % 2] = pe.val

            def v_body(v, st=st, xb=xb, ob=ob):
                v.wait(pe, ep_pe[st % 2])
                v.wait(soo[st % 2])
                dv.inc(v.tensor_add(ob[:, :], pxt[st % 2][:, :], xb[:, :]))
                ep_dv[st % 2] = dv.val
            b.on("vector", v_body)

            def s2_body(s, st=st, ob=ob):
                s.wait(dv, ep_dv[st % 2])
                soo[st % 2].inc(s.dma_start(out=out[st * 128:(st + 1) * 128, :], in_=ob[:, :]), 16)
            b.on("sync", s2_body)

        def fin(s):
            s.wait(soo[0]); s.wait(soo[1])
        b.on("sync", fin)
    return nc

def build_moe(nc, names, mode, env=None, px='mo_'):
    x1 = _dram(nc, names, "x", [S, D], F32, "ExternalInput")
    gw = _dram(nc, names, "gate_w", [8, D], F32, "ExternalInput")
    lb = _dram(nc, names, "lb_bias", [1, 8], F32, "ExternalInput")
    if mode == "full":
        egw = _dram(nc, names, "egw", [8, 512, D], F32, "ExternalInput")
        euw = _dram(nc, names, "euw", [8, 512, D], F32, "ExternalInput")
        edw = _dram(nc, names, "edw", [8, D, 512], F32, "ExternalInput")
        sgw = _dram(nc, names, "sgw", [D, D], F32, "ExternalInput")
        suw = _dram(nc, names, "suw", [D, D], F32, "ExternalInput")
        sdw = _dram(nc, names, "sdw", [D, D], F32, "ExternalInput")
        Sst = _dram(nc, names, "Sstack", [8, 8], F32, "ExternalInput")
        Cst = _dram(nc, names, "Cstack", [8, 8], F32, "ExternalInput")
        out = _dram(nc, names, "out", [S, D], F32, "ExternalOutput")
        gwT_h = nc.dram_tensor("gwT_scratch", [8, 128, 4096], BF16, kind="Internal")
        uwT_h = nc.dram_tensor("uwT_scratch", [8, 128, 4096], BF16, kind="Internal")
        dwT_h = nc.dram_tensor("dwT_scratch", [8, 128, 4096], BF16, kind="Internal")
        sgwT_h = nc.dram_tensor("sgwT_scratch", [128, 8192], BF16, kind="Internal")
        suwT_h = nc.dram_tensor("suwT_scratch", [128, 8192], BF16, kind="Internal")
        sdwT_h = nc.dram_tensor("sdwT_scratch", [128, 8192], BF16, kind="Internal")
    else:
        part = _dram(nc, names, "out", [16, 1], F32, "ExternalOutput")

    iota_np = np.broadcast_to(np.arange(8, dtype=np.float32), (128, 8)).copy()

    with ExitStack() as ctx:
        e = ctx.enter_context
        ident = e(nc.sbuf_tensor(f"{px}ident", [128, 128], BF16))
        identf = e(nc.sbuf_tensor(f"{px}identf", [128, 128], F32))
        onesf = e(nc.sbuf_tensor(f"{px}onesf", [1, 128], F32))
        onescol = e(nc.sbuf_tensor(f"{px}onescol", [128, 1], F32))
        iot = e(nc.sbuf_tensor(f"{px}iot", [128, 8], F32))
        iop8 = e(nc.sbuf_tensor(f"{px}iop8", [128, 8], F32))
        gws = e(nc.sbuf_tensor(f"{px}gws", [128, 1024], F32))
        gwsum = e(nc.sbuf_tensor(f"{px}gwsum", [128, 1], F32))
        gwTs = e(nc.sbuf_tensor(f"{px}gwTs", [128, 64], F32))
        lbB = e(nc.sbuf_tensor(f"{px}lbB", [128, 8], F32))
        gwsumB = e(nc.sbuf_tensor(f"{px}gwsumB", [128, 8], F32))
        meanA = e(nc.sbuf_tensor(f"{px}meanA", [128, NT], F32))
        rstdA = e(nc.sbuf_tensor(f"{px}rstdA", [128, NT], F32))
        coefA = e(nc.sbuf_tensor(f"{px}coefA", [128, NT * 8], F32))
        partsb = e(nc.sbuf_tensor(f"{px}partsb", [16, 1], F32))
        t8row = e(nc.sbuf_tensor(f"{px}t8row", [1, 16], F32))

        if env is None:
            block = e(nc.Block())
            b = B(nc, ctx, block)
        else:
            block = env["block"]
            b = env["b"]
        sl = b.ctr("sl"); sg = b.ctr("sg"); pe = b.ctr("pe")
        dv = b.ctr("dv"); sc = b.ctr("sc"); so = b.ctr("so"); sw = b.ctr("sw")

        identh = nc.inline_tensor(np.eye(128, dtype=BF), f"{px}identnp")
        identfh = nc.inline_tensor(np.eye(128, dtype=np.float32), f"{px}identfnp")
        ioth = nc.inline_tensor(iota_np, f"{px}iotanp")
        iop8h = nc.inline_tensor(iota_np + 8.0, f"{px}iop8np")

        def g0_vec0(v):
            v.memset(onesf[:, :], 1.0)
            v.memset(onescol[:, :], 1.0)
            dv.inc(v.memset(gws[:, :], 0.0))
        b.on("vector", g0_vec0)

        def g0_sync(s):
            sl.inc(s.dma_start(out=ident[:, :], in_=identh.ap()), 16)
            sl.inc(s.dma_start(out=identf[:, :], in_=identfh.ap()), 16)
            sl.inc(s.dma_start(out=iot[:, :], in_=ioth.ap()), 16)
            sl.inc(s.dma_start(out=iop8[:, :], in_=iop8h.ap()), 16)
            s.wait(dv)
            sl.inc(s.dma_start(out=gws[0:8, :], in_=gw), 16)
            sl.inc(s.dma_start(out=t8row[0:1, 8:16], in_=lb), 16)
        b.on("sync", g0_sync)

        def g0_vec(v):
            v.wait(sl)
            dv.inc(v.reduce_sum(gwsum[0:8, :], gws[0:8, :], axis=AX.X))
        b.on("vector", g0_vec)

        with nc.psum_tensor(f"{px}ps0", [128, 1024], F32) as ps0:
            for it in range(8):
                def t_body(t, it=it):
                    t.wait(sl)
                    t.wait(dv)
                    pe.inc(t.transpose(ps0[:, 0:128], gws[:, it * 128:(it + 1) * 128], identf[:, :]))
                b.on("tensor", t_body)
                mmv = pe.val

                def v_body(v, it=it, mmv=mmv):
                    v.wait(pe, mmv)
                    dv.inc(v.tensor_copy(gwTs[:, it * 8:(it + 1) * 8], ps0[:, 0:8]))
                b.on("vector", v_body)

            def t2_body(t):
                t.wait(dv)
                pe.inc(t.matmul(ps0[0:1, 512:520], gwsum[0:8, 0:1], identf[0:8, 0:8],
                                start=True, stop=True))
            b.on("tensor", t2_body)
            mm2 = pe.val

            def v2_body(v, mm2=mm2):
                v.wait(pe, mm2)
                dv.inc(v.tensor_copy(t8row[0:1, 0:8], ps0[0:1, 512:520]))
            b.on("vector", v2_body)

            def t3_body(t):
                t.wait(dv)
                t.wait(sl)
                t.matmul(ps0[:, 512:520], onesf[0:1, :], t8row[0:1, 0:8], start=True, stop=True)
                pe.inc(t.matmul(ps0[:, 520:528], onesf[0:1, :], t8row[0:1, 8:16], start=True, stop=True))
            b.on("tensor", t3_body)
            mm3 = pe.val

            def v3_body(v, mm3=mm3):
                v.wait(pe, mm3)
                v.tensor_copy(gwsumB[:, :], ps0[:, 512:520])
                dv.inc(v.tensor_copy(lbB[:, :], ps0[:, 520:528]))
            b.on("vector", v3_body)

        if mode == "full":
            with ExitStack() as wctx:
                wstage = [wctx.enter_context(nc.sbuf_tensor(f"{px}wstage{i}", [128, 8192], BF16))
                          for i in range(2)]
                wTsb = [wctx.enter_context(nc.sbuf_tensor(f"{px}wTsb{i}", [128, 8192], BF16))
                        for i in range(2)]
                ptwb = wctx.enter_context(nc.psum_tensor(f"{px}ptwb", [128, 1024], BF16))
                jobs = []
                for ee in range(8):
                    jobs += [(egw[ee], gwT_h.ap()[ee], 4, 1024), (euw[ee], uwT_h.ap()[ee], 4, 1024),
                             (edw[ee], dwT_h.ap()[ee], 8, 512)]
                jobs += [(sgw, sgwT_h.ap(), 8, 1024), (suw, suwT_h.ap(), 8, 1024),
                         (sdw, sdwT_h.ap(), 8, 1024)]
                wst_ld = [0, 0]
                wst_pe = [0, 0]
                wT_dv = [0]
                wT_sw = [0, 0]
                for ji, (src, dsth, nt, ncols) in enumerate(jobs):
                    sb = ji % 2

                    def g_body(g, ji=ji, src=src, nt=nt, ncols=ncols, sb=sb):
                        g.wait(pe, wst_pe[sb])
                        sg.inc(g.dma_start(
                            out=wstage[sb][:, 0:nt * ncols].rearrange("p (t i) -> p t i", t=nt),
                            in_=src.rearrange("(t p) i -> p t i", p=128)), 16)
                    b.on("gpsimd", g_body)
                    wst_ld[sb] = sg.val

                    nu = ncols // 128
                    for u in range(nu):
                        def t_body(t, ji=ji, u=u, nt=nt, ncols=ncols, sb=sb):
                            t.wait(sg, wst_ld[sb])
                            t.wait(sl)
                            t.wait(dv, wT_dv[0])
                            for tt in range(nt):
                                q = t.transpose(ptwb[:, tt * 128:(tt + 1) * 128],
                                                wstage[sb][:, tt * ncols + u * 128:tt * ncols + (u + 1) * 128],
                                                ident[:, :])
                            pe.inc(q)
                        b.on("tensor", t_body)
                        mmv = pe.val

                        def v_body(v, ji=ji, u=u, nt=nt, sb=sb, mmv=mmv):
                            v.wait(pe, mmv)
                            v.wait(sw, wT_sw[sb])
                            dv.inc(v.tensor_copy(wTsb[sb][:, u * nt * 128:(u + 1) * nt * 128],
                                                 ptwb[:, 0:nt * 128]))
                            wT_dv[0] = dv.val
                        b.on("vector", v_body)
                    wst_pe[sb] = pe.val

                    def s_body(s, ji=ji, dsth=dsth, nt=nt, ncols=ncols, sb=sb):
                        s.wait(dv, wT_dv[0])
                        nu_ = ncols // 128
                        sw.inc(s.dma_start(out=dsth, in_=wTsb[sb][:, 0:nu_ * nt * 128]), 16)
                        wT_sw[sb] = sw.val
                    b.on("sync", s_body)

        PS = e(nc.psum_tensor(f"{px}PS", [128, 4096], F32))
        g1ctx = ExitStack()
        g1a = g1ctx.enter_context
        x1g = [g1a(nc.sbuf_tensor(f"{px}x1g{i}", [128, 4 * 1024], F32)) for i in range(2)]
        x1Tg = g1a(nc.sbuf_tensor(f"{px}x1Tg", [128, 1024], F32))
        statg = g1a(nc.sbuf_tensor(f"{px}statg", [128, 48], F32))
        mvg = g1a(nc.sbuf_tensor(f"{px}mvg", [128, 8], F32))
        rstdg = g1a(nc.sbuf_tensor(f"{px}rstdg", [128, 4], F32))
        rws = g1a(nc.sbuf_tensor(f"{px}rws", [8, 512], F32))
        rawg = g1a(nc.sbuf_tensor(f"{px}rawg", [128, 32], F32))
        t8a = g1a(nc.sbuf_tensor(f"{px}t8a", [128, 8], F32))
        t8b = g1a(nc.sbuf_tensor(f"{px}t8b", [128, 8], F32))
        scoresb = g1a(nc.sbuf_tensor(f"{px}scoresb", [128, 8], F32))
        w1 = g1a(nc.sbuf_tensor(f"{px}w1", [128, 1], F32))
        w2 = g1a(nc.sbuf_tensor(f"{px}w2", [128, 1], F32))
        i1 = g1a(nc.sbuf_tensor(f"{px}i1", [128, 1], F32))
        i2 = g1a(nc.sbuf_tensor(f"{px}i2", [128, 1], F32))
        m1 = g1a(nc.sbuf_tensor(f"{px}m1", [128, 8], F32))
        m2 = g1a(nc.sbuf_tensor(f"{px}m2", [128, 8], F32))
        s2t = g1a(nc.sbuf_tensor(f"{px}s2t", [128, 8], F32))
        winv = g1a(nc.sbuf_tensor(f"{px}winv", [128, 1], F32))
        tc = g1a(nc.sbuf_tensor(f"{px}tc", [128, 16], F32))

        g1_dv = [0, 0]
        raw_sc = [0, 0]
        x1T_dv = [0]
        for g in range(8):
            gb = g % 2

            def s_body(s, g=g, gb=gb):
                s.wait(dv, g1_dv[gb])
                for u in range(4):
                    st = g * 4 + u
                    sl.inc(s.dma_start(out=x1g[gb][:, u * 1024:(u + 1) * 1024],
                                       in_=x1[st * 128:(st + 1) * 128, :]), 16)
            b.on("sync", s_body)
            ld = sl.val

            def v_body(v, g=g, gb=gb, ld=ld):
                v.wait(sl, ld)
                v.wait(dv)
                for u in range(4):
                    v.bn_stats(statg[:, u * 12:u * 12 + 6], x1g[gb][:, u * 1024:u * 1024 + 512])
                    v.bn_stats(statg[:, u * 12 + 6:u * 12 + 12], x1g[gb][:, u * 1024 + 512:u * 1024 + 1024])
                dv.inc(v.engine_nop())
                v.wait(dv)
                for u in range(4):
                    v.bn_aggr(mvg[:, u * 2:u * 2 + 2], statg[:, u * 12:u * 12 + 12])
                dv.inc(v.engine_nop())
                v.wait(dv)
                for u in range(4):
                    st = g * 4 + u
                    v.tensor_copy(meanA[:, st:st + 1], mvg[:, u * 2:u * 2 + 1])
                    q = v.tensor_scalar_add(rstdg[:, u:u + 1], mvg[:, u * 2 + 1:u * 2 + 2], EPS)
                dv.inc(q)
            b.on("vector", v_body)

            def s2_body(s, g=g):
                s.wait(dv)
                s.activation(rstdg[:, :], rstdg[:, :], AF.Ln)
                sc.inc(s.activation(rstdg[:, :], rstdg[:, :], AF.Exp, scale=-0.5))
            b.on("scalar", s2_body)

            def v2c(v, g=g):
                v.wait(sc)
                for u in range(4):
                    st = g * 4 + u
                    q = v.tensor_copy(rstdA[:, st:st + 1], rstdg[:, u:u + 1])
                dv.inc(q)
            b.on("vector", v2c)

            for u in range(4):
                st = g * 4 + u
                rbk = u % 2

                def t_tr(t, g=g, gb=gb, u=u, st=st, ld=ld):
                    t.wait(sl, ld)
                    if g == 0 and u == 0:
                        t.wait(dv)
                    t.wait(dv, x1T_dv[0])
                    for it in range(8):
                        q = t.transpose(PS[:, it * 128:(it + 1) * 128],
                                        x1g[gb][:, u * 1024 + it * 128:u * 1024 + (it + 1) * 128],
                                        identf[:, :])
                    pe.inc(q)
                b.on("tensor", t_tr)
                trv = pe.val

                def v_tr(v, g=g, u=u, st=st, trv=trv):
                    v.wait(pe, trv)
                    dv.inc(v.tensor_copy(x1Tg[:, :], PS[:, 0:1024]))
                    x1T_dv[0] = dv.val
                b.on("vector", v_tr)
                xtv = dv.val

                def t_raw(t, g=g, u=u, st=st, xtv=xtv, rbk=rbk):
                    t.wait(dv, xtv)
                    t.wait(sc, raw_sc[rbk])
                    for it in range(8):
                        q = t.matmul(PS[0:8, (2 + rbk) * 512:(2 + rbk) * 512 + 128],
                                     gwTs[:, it * 8:(it + 1) * 8],
                                     x1Tg[:, it * 128:(it + 1) * 128], start=(it == 0), stop=(it == 7))
                    pe.inc(q)
                b.on("tensor", t_raw)
                rawv = pe.val

                def sc_r(s, g=g, u=u, rawv=rawv, rbk=rbk):
                    s.wait(pe, rawv)
                    sc.inc(s.copy(rws[:, u * 128:(u + 1) * 128], PS[0:8, (2 + rbk) * 512:(2 + rbk) * 512 + 128]))
                    raw_sc[rbk] = sc.val
                b.on("scalar", sc_r)
            rws_sc = sc.val

            def t_rt(t, g=g, rws_sc=rws_sc):
                t.wait(sc, rws_sc)
                t.wait(dv)
                for u in range(4):
                    q = t.matmul(PS[:, 2048 + u * 8:2048 + (u + 1) * 8], rws[:, u * 128:(u + 1) * 128],
                                 identf[0:8, 0:8], start=True, stop=True)
                pe.inc(q)
            b.on("tensor", t_rt)
            rtv = pe.val

            def v_rt(v, g=g, rtv=rtv):
                v.wait(pe, rtv)
                dv.inc(v.tensor_copy(rawg[:, :], PS[:, 2048:2080]))
            b.on("vector", v_rt)

            for u in range(4):
                st = g * 4 + u

                def v4b(v, g=g, u=u, st=st):
                    v.wait(dv)
                    dv.inc(v.tensor_scalar_mul(t8a[:, :], gwsumB[:, :], meanA[:, st:st + 1]))
                    v.wait(dv)
                    dv.inc(v.tensor_sub(t8b[:, :], rawg[:, u * 8:(u + 1) * 8], t8a[:, :]))
                    v.wait(dv)
                    dv.inc(v.tensor_scalar_mul(t8b[:, :], t8b[:, :], rstdA[:, st:st + 1]))
                    v.wait(dv)
                    dv.inc(v.tensor_add(t8b[:, :], t8b[:, :], lbB[:, :]))
                b.on("vector", v4b)

                def sc3b(s, g=g, u=u):
                    s.wait(dv)
                    sc.inc(s.activation(scoresb[:, :], t8b[:, :], AF.Sigmoid))
                b.on("scalar", sc3b)

                def v5b(v, g=g, u=u, st=st):
                    v.wait(sc)
                    dv.inc(v.reduce_max(w1[:, :], scoresb[:, :], axis=AX.X))
                    v.wait(dv)
                    dv.inc(v.tensor_scalar(m1[:, :], scoresb[:, :], w1[:, 0:1], None, ALU.is_ge))
                    v.wait(dv)
                    dv.inc(v.scalar_tensor_tensor(t8a[:, :], m1[:, :], -8.0, iop8[:, :], ALU.mult, ALU.add))
                    v.wait(dv)
                    dv.inc(v.tensor_reduce(i1[:, :], t8a[:, :], axis=AX.X, op=ALU.min))
                    v.wait(dv)
                    dv.inc(v.tensor_scalar(m1[:, :], iot[:, :], i1[:, 0:1], None, ALU.is_equal))
                    v.wait(dv)
                    dv.inc(v.scalar_tensor_tensor(s2t[:, :], m1[:, :], -1e30, scoresb[:, :], ALU.mult, ALU.add))
                    v.wait(dv)
                    dv.inc(v.reduce_max(w2[:, :], s2t[:, :], axis=AX.X))
                    v.wait(dv)
                    dv.inc(v.tensor_scalar(m2[:, :], s2t[:, :], w2[:, 0:1], None, ALU.is_ge))
                    v.wait(dv)
                    dv.inc(v.scalar_tensor_tensor(t8a[:, :], m2[:, :], -8.0, iop8[:, :], ALU.mult, ALU.add))
                    v.wait(dv)
                    dv.inc(v.tensor_reduce(i2[:, :], t8a[:, :], axis=AX.X, op=ALU.min))
                    v.wait(dv)
                    dv.inc(v.tensor_scalar(m2[:, :], iot[:, :], i2[:, 0:1], None, ALU.is_equal))
                    v.wait(dv)
                    dv.inc(v.tensor_add(winv[:, :], w1[:, :], w2[:, :]))
                    v.wait(dv)
                    dv.inc(v.tensor_scalar_add(winv[:, :], winv[:, :], 1e-8))
                    v.wait(dv)
                    dv.inc(v.reciprocal(winv[:, :], winv[:, :]))
                    v.wait(dv)
                    dv.inc(v.tensor_scalar_mul(t8a[:, :], m1[:, :], w1[:, 0:1]))
                    v.wait(dv)
                    dv.inc(v.tensor_scalar_mul(t8b[:, :], m2[:, :], w2[:, 0:1]))
                    v.wait(dv)
                    dv.inc(v.tensor_add(t8a[:, :], t8a[:, :], t8b[:, :]))
                    v.wait(dv)
                    dv.inc(v.tensor_scalar_mul(tc[:, 0:8], t8a[:, :], winv[:, 0:1]))
                    v.wait(dv)
                    dv.inc(v.tensor_add(tc[:, 8:16], m1[:, :], m2[:, :]))
                    v.wait(dv)
                    dv.inc(v.tensor_copy(coefA[:, st * 8:(st + 1) * 8], tc[:, 8:16]))
                b.on("vector", v5b)
                tcv = dv.val

                def t4b(t, g=g, u=u, st=st, tcv=tcv):
                    t.wait(dv, tcv)
                    pe.inc(t.matmul(PS[0:16, 2560:2561], tc[:, :], onescol[:, :],
                                    start=(st == 0), stop=(st == NT - 1)))
                b.on("tensor", t4b)
            g1_dv[gb] = dv.val

        def v6b(v):
            v.wait(pe)
            dv.inc(v.tensor_copy(partsb[:, :], PS[0:16, 2560:2561]))
        b.on("vector", v6b)

        if mode == "partials":
            def s_out(s):
                s.wait(dv)
                so.inc(s.dma_start(out=part, in_=partsb[:, :]), 16)
                s.wait(so)
            b.on("sync", s_out)
            g1ctx.close()
            return nc

        g1ctx.close()
        wmB = e(nc.sbuf_tensor(f"{px}wmB", [128, 8], F32))
        Ssb = e(nc.sbuf_tensor(f"{px}Ssb", [8, 8], F32))
        Csb = e(nc.sbuf_tensor(f"{px}Csb", [8, 8], F32))
        Sv = e(nc.sbuf_tensor(f"{px}Sv", [8, 1], F32))
        Cv = e(nc.sbuf_tensor(f"{px}Cv", [8, 1], F32))

        def s2_body(s):
            s.wait(dv); s.wait(pe); s.wait(sc)
            sl.inc(s.dma_start(out=Ssb[:, :], in_=Sst), 16)
            sl.inc(s.dma_start(out=Csb[:, :], in_=Cst), 16)
        b.on("sync", s2_body)
        stld = sl.val

        def v7b(v, stld=stld):
            v.wait(sl, stld)
            v.wait(dv)
            v.reduce_sum(Sv[:, :], Ssb[:, :], axis=AX.X)
            dv.inc(v.reduce_sum(Cv[:, :], Csb[:, :], axis=AX.X))
            v.wait(dv)
            dv.inc(v.tensor_scalar_max(Cv[:, :], Cv[:, :], 1.0))
            v.wait(dv)
            dv.inc(v.reciprocal(Cv[:, :], Cv[:, :]))
            v.wait(dv)
            dv.inc(v.tensor_mul(Sv[:, :], Sv[:, :], Cv[:, :]))
        b.on("vector", v7b)

        def t5b(t):
            t.wait(dv)
            pe.inc(t.matmul(PS[0:1, 3072:3080], Sv[:, 0:1], identf[0:8, 0:8], start=True, stop=True))
        b.on("tensor", t5b)
        m5 = pe.val

        def v8b(v, m5=m5):
            v.wait(pe, m5)
            dv.inc(v.tensor_copy(t8row[0:1, 0:8], PS[0:1, 3072:3080]))
        b.on("vector", v8b)

        def t6b(t):
            t.wait(dv)
            pe.inc(t.matmul(PS[:, 3072:3080], onesf[0:1, :], t8row[0:1, 0:8], start=True, stop=True))
        b.on("tensor", t6b)
        m6 = pe.val

        def v9b(v, m6=m6):
            v.wait(pe, m6)
            dv.inc(v.tensor_copy(wmB[:, :], PS[:, 3072:3080]))
            v.wait(dv)
            for st in range(NT):
                q = v.tensor_mul(coefA[:, st * 8:(st + 1) * 8], coefA[:, st * 8:(st + 1) * 8], wmB[:, :])
            dv.inc(q)
        b.on("vector", v9b)

        acc = e(nc.sbuf_tensor(f"{px}acc", [128, 8 * 1024], F32))
        ln1T = e(nc.sbuf_tensor(f"{px}ln1T", [128, 8 * 1024], BF16))
        lntf = e(nc.sbuf_tensor(f"{px}lntf", [128, 1024], F32))
        wg = [e(nc.sbuf_tensor(f"{px}wg{i}", [128, 8192], BF16)) for i in range(2)]
        wu = [e(nc.sbuf_tensor(f"{px}wu{i}", [128, 8192], BF16)) for i in range(2)]
        wd = [e(nc.sbuf_tensor(f"{px}wd{i}", [128, 8192], BF16)) for i in range(2)]
        silb = [e(nc.sbuf_tensor(f"{px}silb{i}", [128, 512], BF16)) for i in range(2)]
        actb = e(nc.sbuf_tensor(f"{px}actb", [128, 8 * 512], BF16))

        wld = [0, 0]
        wcons = [0, 0]
        bank_fr = [(0, 'p')] * 8
        acc_out_dv = [0]

        def bank_wait(t, j):
            v_, k_ = bank_fr[j]
            if k_ == 's':
                t.wait(sc, v_)
            elif k_ == 'd':
                t.wait(dv, v_)

        for qt in range(4):
            def s_body(s, qt=qt):
                s.wait(dv, acc_out_dv[0])
                s.wait(sw)
                if qt == 0:
                    s.wait(dv); s.wait(pe); s.wait(sc)
                sl.inc(s.dma_start(out=acc[:, :].rearrange("p (t d) -> p t d", t=8),
                                   in_=x1[qt * 1024:(qt + 1) * 1024, :].rearrange("(t p) d -> p t d", p=128)), 16)
            b.on("sync", s_body)
            accld = sl.val

            for stt in range(8):
                gst = qt * 8 + stt

                def v_body(v, qt=qt, stt=stt, gst=gst, accld=accld):
                    v.wait(sl, accld)
                    v.wait(pe)
                    dv.inc(v.tensor_scalar(lntf[:, :], acc[:, stt * 1024:(stt + 1) * 1024],
                                           meanA[:, gst:gst + 1], rstdA[:, gst:gst + 1],
                                           ALU.subtract, ALU.mult))
                b.on("vector", v_body)
                apv = dv.val

                def t_body(t, qt=qt, stt=stt, apv=apv):
                    t.wait(dv, apv)
                    for j in range(2):
                        bank_wait(t, j)
                    for it in range(8):
                        q = t.transpose(PS[:, it * 128:(it + 1) * 128],
                                        lntf[:, it * 128:(it + 1) * 128], identf[:, :])
                    pe.inc(q)
                b.on("tensor", t_body)
                trv = pe.val

                def v2_body(v, qt=qt, stt=stt, trv=trv):
                    v.wait(pe, trv)
                    dst = bass.AP(ln1T, stt * 128, [[8 * 1024, 128], [1024, 8], [1, 128]])
                    srcap = bass.AP(PS, 0, [[4096, 128], [128, 8], [1, 128]])
                    dv.inc(v.tensor_copy(dst, srcap))
                b.on("vector", v2_body)
                for j in range(2):
                    bank_fr[j] = (dv.val, 'd')

            ln_dv = dv.val

            for ee in range(9):
                shared = (ee == 8)
                nI = 8 if shared else 4
                sb = ee % 2

                def g_body(g, qt=qt, ee=ee, shared=shared, sb=sb):
                    g.wait(pe, wcons[sb])
                    if qt == 0 and ee < 2:
                        g.wait(sw)
                    if shared:
                        sg.inc(g.dma_start(out=wg[sb][:, :], in_=sgwT_h.ap()), 16)
                        sg.inc(g.dma_start(out=wu[sb][:, :], in_=suwT_h.ap()), 16)
                        sg.inc(g.dma_start(out=wd[sb][:, :], in_=sdwT_h.ap()), 16)
                    else:
                        sg.inc(g.dma_start(out=wg[sb][:, 0:4096], in_=gwT_h.ap()[ee]), 16)
                        sg.inc(g.dma_start(out=wu[sb][:, 0:4096], in_=uwT_h.ap()[ee]), 16)
                        sg.inc(g.dma_start(out=wd[sb][:, 0:4096], in_=dwT_h.ap()[ee]), 16)
                b.on("gpsimd", g_body)
                wld[sb] = sg.val

                for sc_ in range(2):
                    for wave in range(nI // 4):
                        gstop = [0] * 4
                        for it in range(8):
                            def t_gu(t, qt=qt, ee=ee, sc_=sc_, wave=wave, it=it, sb=sb, nI=nI):
                                t.wait(sg, wld[sb])
                                if it == 0:
                                    t.wait(dv, ln_dv)
                                for It4 in range(4):
                                    It = wave * 4 + It4
                                    if it == 0:
                                        bank_wait(t, It4)
                                        bank_wait(t, 4 + It4)
                                    t.matmul(PS[:, It4 * 512:(It4 + 1) * 512],
                                             wg[sb][:, it * (nI * 128) + It * 128:it * (nI * 128) + (It + 1) * 128],
                                             ln1T[:, it * 1024 + sc_ * 512:it * 1024 + (sc_ + 1) * 512],
                                             start=(it == 0), stop=(it == 7))
                                    q = t.matmul(PS[:, 2048 + It4 * 512:2048 + (It4 + 1) * 512],
                                                 wu[sb][:, it * (nI * 128) + It * 128:it * (nI * 128) + (It + 1) * 128],
                                                 ln1T[:, it * 1024 + sc_ * 512:it * 1024 + (sc_ + 1) * 512],
                                                 start=(it == 0), stop=(it == 7))
                                    if it == 7:
                                        pe.inc(q)
                                        gstop[It4] = pe.val
                                if it != 7:
                                    pe.inc(q)
                            b.on("tensor", t_gu)
                        wcons[sb] = pe.val

                        for It4 in range(4):
                            It = wave * 4 + It4

                            def sc_body(s, It4=It4, It=It, gs=gstop):
                                s.wait(pe, gs[It4])
                                s.wait(dv)
                                sc.inc(s.activation(silb[It4 % 2][:, :], PS[:, It4 * 512:(It4 + 1) * 512],
                                                    AF.Silu))
                                bank_fr[It4] = (sc.val, 's')
                            b.on("scalar", sc_body)
                            scv = sc.val

                            def v_body(v, It4=It4, It=It, scv=scv):
                                v.wait(sc, scv)
                                dv.inc(v.tensor_mul(actb[:, It * 512:(It + 1) * 512],
                                                    silb[It4 % 2][:, :],
                                                    PS[:, 2048 + It4 * 512:2048 + (It4 + 1) * 512]))
                                bank_fr[4 + It4] = (dv.val, 'd')
                            b.on("vector", v_body)
                    act_dv = dv.val

                    for stt4 in range(4):
                        bp = stt4 * 2

                        def t_dn(t, qt=qt, ee=ee, sc_=sc_, stt4=stt4, sb=sb, nI=nI, bp=bp, act_dv=act_dv):
                            t.wait(dv, act_dv)
                            bank_wait(t, bp)
                            bank_wait(t, bp + 1)
                            for It in range(nI):
                                for n in range(2):
                                    q = t.matmul(PS[:, (bp + n) * 512:(bp + n + 1) * 512],
                                                 actb[:, It * 512 + stt4 * 128:It * 512 + (stt4 + 1) * 128],
                                                 wd[sb][:, It * 1024 + n * 512:It * 1024 + (n + 1) * 512],
                                                 start=(It == 0), stop=(It == nI - 1))
                            pe.inc(q)
                        b.on("tensor", t_dn)
                        dnv = pe.val
                        wcons[sb] = pe.val

                        def v_fma(v, qt=qt, ee=ee, sc_=sc_, stt4=stt4, shared=shared, bp=bp, dnv=dnv):
                            v.wait(pe, dnv)
                            stt = sc_ * 4 + stt4
                            gst = qt * 8 + stt
                            coef = 1.0 if shared else coefA[:, gst * 8 + ee:gst * 8 + ee + 1]
                            for n in range(2):
                                q = v.scalar_tensor_tensor(acc[:, stt * 1024 + n * 512:stt * 1024 + (n + 1) * 512],
                                                           PS[:, (bp + n) * 512:(bp + n + 1) * 512],
                                                           coef,
                                                           acc[:, stt * 1024 + n * 512:stt * 1024 + (n + 1) * 512],
                                                           ALU.mult, ALU.add)
                            dv.inc(q)
                            bank_fr[bp] = (dv.val, 'd')
                            bank_fr[bp + 1] = (dv.val, 'd')
                        b.on("vector", v_fma)

            def s2q(s, qt=qt):
                s.wait(dv)
                so.inc(s.dma_start(out=out[qt * 1024:(qt + 1) * 1024, :].rearrange("(t p) d -> p t d", p=128),
                                   in_=acc[:, :].rearrange("p (t d) -> p t d", t=8)), 16)
            b.on("sync", s2q)
            acc_out_dv[0] = dv.val

        def fin(s):
            s.wait(so)
        b.on("sync", fin)
    return nc



def build_attn1(nc, names, env=None, px='a1_'):
    x2 = _dram(nc, names, "x", [S, D], F32, "ExternalInput")
    wnames = ["p", "q", "k", "v", "o"]
    Wd = {w: _dram(nc, names, "W" + w, [D, D], F32, "ExternalInput") for w in wnames}
    Bd = {w: _dram(nc, names, "b" + w, [1, D], F32, "ExternalInput") for w in wnames}
    out = _dram(nc, names, "out", [S, D], F32, "ExternalOutput")

    bsum_np = np.zeros((128, 8 * 16), dtype=BF)
    bbc_np = np.zeros((16, 8 * 128), dtype=BF)
    for t in range(8):
        bsum_np[0:64, t * 16 + 2 * t] = 1
        bsum_np[64:128, t * 16 + 2 * t + 1] = 1
        bbc_np[2 * t, t * 128:t * 128 + 64] = 1
        bbc_np[2 * t + 1, t * 128 + 64:t * 128 + 128] = 1

    with ExitStack() as ctx:
        e = ctx.enter_context
        smb = e(nc.sbuf_tensor(f"{px}smb", [128, 256], BF16))
        ident = smb[:, 0:128]
        bsum = smb[:, 128:256]
        smf = e(nc.sbuf_tensor(f"{px}smf", [128, 256], F32))
        identf = smf[:, 0:128]
        bcol = {w: smf[:, 128 + 8 * wi:136 + 8 * wi] for wi, w in enumerate(wnames)}
        stat = smf[:, 168:216]
        mvg = smf[:, 216:224]
        rstd4 = smf[:, 224:228]
        sm16b = e(nc.sbuf_tensor(f"{px}sm16b", [16, 2560], BF16))
        bbc = sm16b[:, 0:1024]
        x2tb = e(nc.sbuf_tensor(f"{px}x2tb", [128, 4096], F32))
        x2t = [x2tb[:, i * 1024:(i + 1) * 1024] for i in range(4)]

        WT = {w: e(nc.sbuf_tensor(f"{px}WT{w}", [128, 8192], BF16)) for w in ["q", "k", "v"]}
        WTp = e(nc.sbuf_tensor(f"{px}WTp", [128, 8192], BF16))
        WT["o"] = WTp
        xpc = e(nc.sbuf_tensor(f"{px}xpc", [128, 8 * 4096], BF16))
        PS = e(nc.psum_tensor(f"{px}PS", [128, 4096], F32))

        actx = ExitStack()
        ae = actx.enter_context
        lntb = ae(nc.sbuf_tensor(f"{px}lntb", [128, 4096], BF16))
        lnt = [lntb[:, i * 1024:(i + 1) * 1024] for i in range(4)]
        ws = ae(nc.sbuf_tensor(f"{px}ws", [128, 8192], BF16))
        lnring = ae(nc.sbuf_tensor(f"{px}lnring", [128, 8192], BF16))
        WTfull = dict(p=WTp, **WT)

        if env is None:
            block = e(nc.Block())
            b = B(nc, ctx, block)
        else:
            block = env["block"]
            b = env["b"]
        sl = b.ctr("sl"); sg = b.ctr("sg"); sq = b.ctr("sq"); ss = b.ctr("ss"); slc = b.ctr("slc")
        pe = b.ctr("pe"); dv = b.ctr("dv"); sc = b.ctr("sc")

        identh = nc.inline_tensor(np.eye(128, dtype=BF), f"{px}identnp")
        identfh = nc.inline_tensor(np.eye(128, dtype=np.float32), f"{px}identfnp")
        bsumh = nc.inline_tensor(bsum_np, f"{px}bsumnp")
        bbch = nc.inline_tensor(bbc_np, f"{px}bbcnp")

        prot = {"r": 0, "free": [None, None]}

        def pair_take():
            p = prot["r"] % 2
            fr = prot["free"][p]
            prot["r"] += 1
            return p, fr

        def pair_done(p, ctrn, val):
            prot["free"][p] = (ctrn, val)

        def PB(p, a, wdt):
            return PS[:, p * 1024 + a:p * 1024 + a + wdt]

        def wait_free(t, fr):
            if fr is not None:
                t.wait(b.ctr(fr[0]), fr[1])

        def plan(ctr, tags):
            base = ctr.val
            return {tg: base + k + 1 for k, tg in enumerate(tags)}, base + len(tags)

        def d0_sync(s):
            slc.inc(s.dma_start(out=ident[:, :], in_=identh.ap()), 16)
            slc.inc(s.dma_start(out=identf[:, :], in_=identfh.ap()), 16)
            slc.inc(s.dma_start(out=bsum[:, :], in_=bsumh.ap()), 16)
            slc.inc(s.dma_start(out=bbc[:, :], in_=bbch.ap()), 16)
            for w in wnames:
                for m in range(8):
                    slc.inc(s.dma_start(out=bcol[w][:, m:m + 1],
                                       in_=Bd[w][0:1, m * 128:(m + 1) * 128].rearrange("o p -> p o")), 16)
        b.on("sync", d0_sync)
        const_ld = slc.val

        def d0_vec(v):
            v.wait(slc, const_ld)
            dv.inc(v.tensor_scalar_mul(bcol["q"][:, :], bcol["q"][:, :], 0.125))
        b.on("vector", d0_vec)
        bq_dv = dv.val

        def p_tags(c):
            return [("xpev", c, mp) for mp in range(8)]
        act_pre = [("sqrt", 0)] + [("wev", "p", u) for u in range(8)]
        for c in range(8):
            if c < 7:
                act_pre += [("sqrt", c + 1)]
            act_pre += p_tags(c)
        for w in ["q", "k", "v", "o"]:
            act_pre += [("wev", w, u) for u in range(8)]
        tsc, pre_sc_end = plan(sc, act_pre)
        lnev_dv = [0] * 8

        wld = {}
        wTpe = {}
        wTlast = {}

        def emit_wload(w, prev_w):
            def g_body(g, w=w, prev_w=prev_w):
                if prev_w is not None:
                    g.wait(pe, wTlast[prev_w])
                wld[w] = sg.inc(g.dma_start(out=ws[:, :].rearrange("p (t i) -> p t i", t=8),
                                            in_=Wd[w].rearrange("(t p) i -> p t i", p=128)), 16)
            b.on("gpsimd", g_body)

        def emit_wT(w):
            for u in range(8):
                p, fr = pair_take()

                def t_body(t, w=w, u=u, p=p, fr=fr):
                    t.wait(sg, wld[w])
                    t.wait(slc, const_ld)
                    wait_free(t, fr)
                    for tt in range(8):
                        q = t.matmul(PB(p, tt * 128, 128), ws[:, tt * 1024 + u * 128:tt * 1024 + (u + 1) * 128],
                                     ident, start=True, stop=True)
                    pe.inc(q)
                b.on("tensor", t_body)
                wTpe[(w, u)] = (pe.val, p)
                pair_done(p, "sc", tsc[("wev", w, u)])
            wTlast[w] = pe.val

        def emit_wev(w):
            def s_body(s, w=w):
                for u in range(8):
                    tgt, p = wTpe[(w, u)]
                    s.wait(pe, tgt)
                    sc.inc(s.activation(WTfull[w][:, u * 1024:(u + 1) * 1024], PB(p, 0, 1024),
                                        AF.Identity, bias=0.0))
            b.on("scalar", s_body)
            assert sc.val == tsc[("wev", w, 7)], (w, sc.val)

        apply_dv = [0] * 32
        plt_pe = [0] * 32
        ld_t = [0] * 32
        eps_dv = [0] * 8
        p_pe = {}
        xp_pair = {}

        def emit_lnloads_sync(g0):
            def s_body(s, g0=g0):
                for t in (4 * g0, 4 * g0 + 2):
                    if t >= 4:
                        s.wait(dv, apply_dv[t - 4])
                    ld_t[t] = sl.inc(s.dma_start(out=x2t[t % 4][:, :], in_=x2[t * 128:(t + 1) * 128, :]), 16)
            b.on("sync", s_body)

        def emit_lnloads_scalar(g0):
            def s_body(s, g0=g0):
                for t in (4 * g0 + 1, 4 * g0 + 3):
                    if t >= 4:
                        s.wait(dv, apply_dv[t - 4])
                    ld_t[t] = sq.inc(s.dma_start(out=x2t[t % 4][:, :], in_=x2[t * 128:(t + 1) * 128, :]), 16)
            b.on("scalar", s_body)

        def emit_lndve(g):
            def v_body(v, g=g):
                for i in range(4):
                    t = 4 * g + i
                    v.wait(sl if t % 2 == 0 else sq, ld_t[4 * g + 2 if t % 2 == 0 else 4 * g + 3])
                    v.bn_stats(stat[:, i * 12:i * 12 + 6], x2t[t % 4][:, 0:512])
                    dv.inc(v.bn_stats(stat[:, i * 12 + 6:i * 12 + 12], x2t[t % 4][:, 512:1024]))
                for i in range(4):
                    dv.inc(v.bn_aggr(mvg[:, 2 * i:2 * i + 2], stat[:, i * 12:(i + 1) * 12]))
                for i in range(4):
                    eps_dv[g] = dv.inc(v.tensor_scalar_add(rstd4[:, i:i + 1], mvg[:, 2 * i + 1:2 * i + 2], EPS))
            b.on("vector", v_body)

            def v2_body(v, g=g):
                v.wait(sc, tsc[("sqrt", g)])
                dv.inc(v.reciprocal(rstd4[:, 0:4], rstd4[:, 0:4]))
                for i in range(4):
                    t = 4 * g + i
                    if t >= 4:
                        v.wait(pe, plt_pe[t - 4])
                    apply_dv[t] = dv.inc(v.tensor_scalar(lnt[t % 4][:, :], x2t[t % 4][:, :],
                                                         mvg[:, 2 * i:2 * i + 1], rstd4[:, i:i + 1],
                                                         ALU.subtract, ALU.mult))
            b.on("vector", v2_body)

        def emit_sqrt(g):
            def s_body(s, g=g):
                s.wait(dv, eps_dv[g])
                sc.inc(s.activation(rstd4[:, 0:4], rstd4[:, 0:4], AF.Sqrt))
            b.on("scalar", s_body)
            assert sc.val == tsc[("sqrt", g)], (g, sc.val)

        def emit_plt2(g):
            pps = []
            tgts = [dv.val + i + 1 for i in range(4)]
            for i in range(4):
                t = 4 * g + i
                p, fr = pair_take()

                def t_body(t_, t=t, p=p, fr=fr):
                    t_.wait(dv, apply_dv[t])
                    wait_free(t_, fr)
                    for it in range(8):
                        q = t_.matmul(PB(p, it * 128, 128), lnt[t % 4][:, it * 128:(it + 1) * 128],
                                      ident, start=True, stop=True)
                    pe.inc(q)
                b.on("tensor", t_body)
                plt_pe[t] = pe.val
                pair_done(p, "dv", tgts[i])
                pps.append(p)
            return pps, tgts

        def emit_lnev(g, pps, tgts):
            def v_body(v, g=g, pps=pps):
                for i in range(4):
                    t = 4 * g + i
                    v.wait(pe, plt_pe[t])
                    if g >= 2:
                        v.wait(pe, p_pe[(g - 2, 7)])
                    p = pps[i]
                    slot = g % 2
                    dst = bass.AP(lnring, slot * 512 + i * 128, [[8192, 128], [1024, 8], [1, 128]])
                    src = bass.AP(PS, p * 1024, [[4096, 128], [128, 8], [1, 128]])
                    dv.inc(v.tensor_copy(dst, src))
            b.on("vector", v_body)
            assert dv.val == tgts[3], (g, dv.val)
            lnev_dv[g] = tgts[3]

        def emit_P2(c):
            slot = c % 2
            for mp in range(8):
                p, fr = pair_take()

                def t_body(t, c=c, mp=mp, p=p, fr=fr, slot=slot):
                    t.wait(dv, lnev_dv[c])
                    t.wait(sc, tsc[("wev", "p", 7)])
                    wait_free(t, fr)
                    for u in range(8):
                        q = t.matmul(PB(p, 0, 512), WTp[:, u * 1024 + mp * 128:u * 1024 + (mp + 1) * 128],
                                     lnring[:, u * 1024 + slot * 512:u * 1024 + slot * 512 + 512],
                                     start=(u == 0), stop=(u == 7))
                    pe.inc(q)
                b.on("tensor", t_body)
                p_pe[(c, mp)] = pe.val
                xp_pair[(c, mp)] = (pe.val, p)
                pair_done(p, "sc", tsc[("xpev", c, mp)])

        def emit_xpev(c):
            def s_body(s, c=c):
                for mp in range(8):
                    s.wait(pe, p_pe[(c, mp)])
                    _, p = xp_pair[(c, mp)]
                    sc.inc(s.activation(xpc[:, mp * 4096 + c * 512:mp * 4096 + c * 512 + 512],
                                        PB(p, 0, 512), AF.Identity, bias=bcol["p"][:, mp:mp + 1]))
            b.on("scalar", s_body)
            assert sc.val == tsc[("xpev", c, 7)], (c, sc.val)

        emit_wload("p", None)
        emit_lnloads_sync(0)
        emit_lnloads_scalar(0)
        emit_lndve(0)
        emit_sqrt(0)
        emit_lnloads_sync(1)
        emit_lnloads_scalar(1)
        emit_wT("p")
        emit_wev("p")
        emit_wload("q", "p")
        pps, tg = emit_plt2(0)
        emit_lnev(0, pps, tg)
        for c in range(8):
            if c < 7:
                emit_lndve(c + 1)
                emit_sqrt(c + 1)
                if c < 6:
                    emit_lnloads_sync(c + 2)
                    emit_lnloads_scalar(c + 2)
                pps, tg = emit_plt2(c + 1)
                emit_lnev(c + 1, pps, tg)
            emit_P2(c)
            emit_xpev(c)
        prevw = "q"
        emit_wT("q")
        emit_wev("q")
        for w in ["k", "v", "o"]:
            emit_wload(w, prevw)
            emit_wT(w)
            emit_wev(w)
            prevw = w
        assert sc.val == pre_sc_end

        actx.close()
        qc_ = e(nc.sbuf_tensor(f"{px}qc", [128, 8 * 512], BF16))
        kc = e(nc.sbuf_tensor(f"{px}kc", [128, 8 * 514], BF16))
        vc = e(nc.sbuf_tensor(f"{px}vc", [128, 8 * 514], BF16))
        prodbb = e(nc.sbuf_tensor(f"{px}prodbb", [128, 1024], BF16))
        prodb = [prodbb[:, i * 512:(i + 1) * 512] for i in range(2)]
        sm16f = e(nc.sbuf_tensor(f"{px}sm16f", [16, 2048], F32))
        dnm = sm16f[:, 0:512]
        expb = sm16f[:, 512:2048]
        attnb = sm16b[:, 1024:2560]
        actmp = e(nc.sbuf_tensor(f"{px}actmp", [128, 1024], F32))
        accS = actmp[:, 0:512]
        tmpb = actmp[:, 512:1024]
        attnoT = e(nc.sbuf_tensor(f"{px}attnoT", [128, 8 * 512], BF16))
        borow = e(nc.sbuf_tensor(f"{px}borow", [1, 1152], F32))


        onesh = nc.inline_tensor(np.ones((1, 128), np.float32), f"{px}onesnp")

        def bor_sync(s):
            slc.inc(s.dma_start(out=borow[0:1, 0:128], in_=onesh.ap()), 16)
            slc.inc(s.dma_start(out=borow[0:1, 128:1152], in_=Bd["o"][0:1, :]), 16)
        b.on("sync", bor_sync)
        bor_ld = slc.val

        AVCOL = [3584, 2048]
        av_slot_free = [None, None]
        av_kk = {0: 0, 1: 1, 2: 3}
        exp_prev = None
        tdv_prev = None
        tsc_prev = None
        rec_res = {}
        rec_store = {}
        prev_av_mm_last = 0
        res_ld = {}

        for i in range(9):
            ci = i if i < 8 else None
            ai = i - 1 if i >= 1 else None

            act_tags = []
            if ci is not None:
                for m in range(8):
                    act_tags += [("qev", m), ("kev", m, 0), ("kev", m, 1), ("vev", m, 0), ("vev", m, 1)]
            if ci is not None:
                act_tags += [("exp", n) for n in range(3)]
            tsc_i, sc_end = plan(sc, act_tags)

            dve_tags = []
            if ci is not None:
                dve_tags += [("prod", n, t8) for n in range(3) for t8 in range(8)]
                dve_tags += [("dn", 0), ("dn", 1), ("rc",), ("at", 0), ("at", 1), ("at", 2)]
            if ai is not None:
                dve_tags += [("res", st) for st in range(4)]
            if ci is not None:
                dve_tags += [("av", t8, kk) for t8 in range(8) for kk in range(5)]
            tdv_i, dv_end = plan(dv, dve_tags)

            rec_mm = {}

            if ai is not None:
                def s_body(s, ai=ai):
                    for st in range(4):
                        if ai == 0:
                            s.wait(dv, apply_dv[28 + st])
                        else:
                            s.wait(ss, rec_store[(ai - 1, 3)])
                        res_ld[(ai, st)] = sl.inc(
                            s.dma_start(out=x2t[st][:, :], in_=x2[(ai * 4 + st) * 128:(ai * 4 + st + 1) * 128, :]), 16)
                b.on("sync", s_body)

            if ci is not None:
                for m in range(8):
                    p, fr = pair_take()

                    def t_body(t, ci=ci, m=m, p=p, fr=fr):
                        t.wait(sc, tsc[("wev", "q", 7)])
                        wait_free(t, fr)
                        for u in range(8):
                            q = t.matmul(PB(p, 0, 512), WT["q"][:, u * 1024 + m * 128:u * 1024 + (m + 1) * 128],
                                         xpc[:, u * 4096 + ci * 512:u * 4096 + ci * 512 + 512],
                                         start=(u == 0), stop=(u == 7))
                        pe.inc(q)
                    b.on("tensor", t_body)
                    rec_mm[("q", m)] = (pe.val, p)
                    pair_done(p, "sc", tsc_i[("qev", m)])

                    for w, dstname in (("k", "kev"), ("v", "vev")):
                        p, fr = pair_take()

                        def t_body(t, ci=ci, m=m, p=p, fr=fr, w=w):
                            t.wait(sc, tsc[("wev", w, 7)])
                            wait_free(t, fr)
                            base = ci * 512 - 1
                            if ci == 0:
                                for u in range(8):
                                    q = t.matmul(PB(p, 0, 1), WT[w][:, u * 1024 + m * 128:u * 1024 + (m + 1) * 128],
                                                 xpc[:, u * 4096 + 4095:u * 4096 + 4096],
                                                 start=(u == 0), stop=(u == 7))
                                for u in range(8):
                                    q = t.matmul(PB(p, 1, 256), WT[w][:, u * 1024 + m * 128:u * 1024 + (m + 1) * 128],
                                                 xpc[:, u * 4096:u * 4096 + 256],
                                                 start=(u == 0), stop=(u == 7))
                                for u in range(8):
                                    q = t.matmul(PB(p, 512, 257), WT[w][:, u * 1024 + m * 128:u * 1024 + (m + 1) * 128],
                                                 xpc[:, u * 4096 + 256:u * 4096 + 513],
                                                 start=(u == 0), stop=(u == 7))
                            elif ci == 7:
                                for u in range(8):
                                    q = t.matmul(PB(p, 768, 1), WT[w][:, u * 1024 + m * 128:u * 1024 + (m + 1) * 128],
                                                 xpc[:, u * 4096:u * 4096 + 1],
                                                 start=(u == 0), stop=(u == 7))
                                for u in range(8):
                                    q = t.matmul(PB(p, 0, 257), WT[w][:, u * 1024 + m * 128:u * 1024 + (m + 1) * 128],
                                                 xpc[:, u * 4096 + base:u * 4096 + base + 257],
                                                 start=(u == 0), stop=(u == 7))
                                for u in range(8):
                                    q = t.matmul(PB(p, 512, 256), WT[w][:, u * 1024 + m * 128:u * 1024 + (m + 1) * 128],
                                                 xpc[:, u * 4096 + base + 257:u * 4096 + base + 513],
                                                 start=(u == 0), stop=(u == 7))
                            else:
                                for u in range(8):
                                    q = t.matmul(PB(p, 0, 257), WT[w][:, u * 1024 + m * 128:u * 1024 + (m + 1) * 128],
                                                 xpc[:, u * 4096 + base:u * 4096 + base + 257],
                                                 start=(u == 0), stop=(u == 7))
                                for u in range(8):
                                    q = t.matmul(PB(p, 512, 257), WT[w][:, u * 1024 + m * 128:u * 1024 + (m + 1) * 128],
                                                 xpc[:, u * 4096 + base + 257:u * 4096 + base + 514],
                                                 start=(u == 0), stop=(u == 7))
                            pe.inc(q)
                        b.on("tensor", t_body)
                        rec_mm[(w, m)] = (pe.val, p)
                        pair_done(p, "sc", tsc_i[(dstname, m, 1)])

            if ci is not None:
                for n in range(3):
                    for t8 in range(8):
                        def t_body(t, ci=ci, n=n, t8=t8):
                            t.wait(dv, tdv_i[("prod", n, t8)])
                            if t8 == 0 and exp_prev is not None:
                                t.wait(sc, exp_prev[("exp", n)])
                            if t8 == 0 and n == 0 and av_slot_free[1] is not None:
                                t.wait(dv, av_slot_free[1])
                            q = t.matmul(PS[0:16, 2048 + n * 512:2048 + (n + 1) * 512],
                                         bsum[:, t8 * 16:(t8 + 1) * 16], prodb[(n * 8 + t8) % 2][:, :],
                                         start=(t8 == 0), stop=(t8 == 7))
                            pe.inc(q)
                        b.on("tensor", t_body)
                        rec_mm[("sc", n, t8)] = pe.val

            if ci is not None:
                def v_body(v, ci=ci, tsc_i=tsc_i, tdv_i=tdv_i, rec=dict(rec_mm)):
                    for n in range(3):
                        for t8 in range(8):
                            j = n * 8 + t8
                            v.wait(sc, tsc_i[("kev", t8, 1)])
                            if j >= 2:
                                v.wait(pe, rec[("sc", (j - 2) // 8, (j - 2) % 8)])
                            dv.inc(v.tensor_mul(prodb[j % 2][:, :], qc_[:, t8 * 512:(t8 + 1) * 512],
                                                kc[:, t8 * 514 + n:t8 * 514 + n + 512]))
                b.on("vector", v_body)

                def v_body2(v, ci=ci, tsc_i=tsc_i, pal=prev_av_mm_last):
                    v.wait(sc, tsc_i[("exp", 1)])
                    dv.inc(v.tensor_add(dnm[:, :], expb[:, 0:512], expb[:, 512:1024]))
                    v.wait(sc, tsc_i[("exp", 2)])
                    dv.inc(v.tensor_add(dnm[:, :], dnm[:, :], expb[:, 1024:1536]))
                    dv.inc(v.reciprocal(dnm[:, :], dnm[:, :]))
                    v.wait(pe, pal)
                    for n in range(3):
                        dv.inc(v.tensor_mul(attnb[:, n * 512:(n + 1) * 512],
                                            expb[:, n * 512:(n + 1) * 512], dnm[:, :]))
                b.on("vector", v_body2)

            if ai is not None:
                oad = tdv_prev[("av", 7, 4)]
                for st in range(4):
                    p, fr = pair_take()

                    def t_body(t, ai=ai, st=st, p=p, fr=fr, oad=oad):
                        t.wait(dv, oad)
                        t.wait(sc, tsc[("wev", "o", 7)])
                        t.wait(slc, bor_ld)
                        wait_free(t, fr)
                        for half in range(2):
                            for t8 in range(8):
                                t.matmul(PB(p, half * 512, 512),
                                         attnoT[:, t8 * 512 + st * 128:t8 * 512 + (st + 1) * 128],
                                         WT["o"][:, t8 * 1024 + half * 512:t8 * 1024 + half * 512 + 512],
                                         start=(t8 == 0), stop=False)
                            q = t.matmul(PB(p, half * 512, 512), borow[0:1, 0:128],
                                         borow[0:1, 128 + half * 512:128 + half * 512 + 512],
                                         start=False, stop=True)
                        pe.inc(q)
                    b.on("tensor", t_body)
                    rec_mm[("O", st)] = (pe.val, p)
                    pair_done(p, "dv", tdv_i[("res", st)])

            if ci is not None:
                def s_body(s, ci=ci, ai=ai, tsc_i=tsc_i, rec=dict(rec_mm),
                           pcons=(tdv_prev[("av", 7, 4)] if tdv_prev is not None else 0)):
                    first = True
                    s.wait(dv, bq_dv)
                    for m in range(8):
                        if first:
                            s.wait(dv, pcons)
                            first = False
                        tgt, p = rec[("q", m)]
                        s.wait(pe, tgt)
                        sc.inc(s.activation(qc_[:, m * 512:(m + 1) * 512], PB(p, 0, 512),
                                            AF.Identity, bias=bcol["q"][:, m:m + 1], scale=0.125))
                        tgt, p = rec[("k", m)]
                        s.wait(pe, tgt)
                        sc.inc(s.activation(kc[:, m * 514:m * 514 + 257], PB(p, 0, 257),
                                            AF.Identity, bias=bcol["k"][:, m:m + 1]))
                        sc.inc(s.activation(kc[:, m * 514 + 257:m * 514 + 514], PB(p, 512, 257),
                                            AF.Identity, bias=bcol["k"][:, m:m + 1]))
                        tgt, p = rec[("v", m)]
                        s.wait(pe, tgt)
                        sc.inc(s.activation(vc[:, m * 514:m * 514 + 257], PB(p, 0, 257),
                                            AF.Identity, bias=bcol["v"][:, m:m + 1]))
                        sc.inc(s.activation(vc[:, m * 514 + 257:m * 514 + 514], PB(p, 512, 257),
                                            AF.Identity, bias=bcol["v"][:, m:m + 1]))
                b.on("scalar", s_body)

            if ci is not None:
                def s_body(s, ci=ci, rec=dict(rec_mm),
                           efree=(tdv_prev[("at", 2)] if tdv_prev is not None else 0)):
                    s.wait(dv, efree)
                    for n in range(3):
                        s.wait(pe, rec[("sc", n, 7)])
                        sc.inc(s.activation(expb[:, n * 512:(n + 1) * 512],
                                            PS[0:16, 2048 + n * 512:2048 + (n + 1) * 512], AF.Exp))
                b.on("scalar", s_body)

            if ai is not None:
                def v_body(v, ai=ai, tdv_i=tdv_i, rec=dict(rec_mm)):
                    for st in range(4):
                        tgt, p = rec[("O", st)]
                        v.wait(pe, tgt)
                        v.wait(sl, res_ld[(ai, 3)])
                        dv.inc(v.tensor_add(x2t[st][:, :], PB(p, 0, 1024), x2t[st][:, :]))
                b.on("vector", v_body)

            if ci is not None:
                for t8 in range(8):
                    for n in range(3):
                        r = t8 * 3 + n
                        slot = r % 2

                        def t_body(t, ci=ci, t8=t8, n=n, slot=slot, tsc_i=tsc_i, tdv_i=tdv_i,
                                   slf=av_slot_free[r % 2]):
                            t.wait(dv, tdv_i[("at", n)])
                            if slf is not None:
                                t.wait(dv, slf)
                            if slot == 1:
                                t.wait(sc, tsc_i[("exp", 0)])
                            q = t.matmul(PS[:, AVCOL[slot]:AVCOL[slot] + 512],
                                         bbc[:, t8 * 128:(t8 + 1) * 128], attnb[:, n * 512:(n + 1) * 512],
                                         start=True, stop=True)
                            pe.inc(q)
                        b.on("tensor", t_body)
                        rec_mm[("av", t8, n)] = pe.val
                        av_slot_free[slot] = tdv_i[("av", t8, av_kk[n])]
                prev_av_mm_last = pe.val

            if ci is not None:
                def v_body(v, ci=ci, rec=dict(rec_mm),
                           pol=(rec_mm[("O", 3)][0] if ai is not None else 0)):
                    for t8 in range(8):
                        v.wait(pe, rec[("av", t8, 0)])
                        dv.inc(v.tensor_mul(accS[:, :], PS[:, AVCOL[(t8 * 3) % 2]:AVCOL[(t8 * 3) % 2] + 512],
                                            vc[:, t8 * 514:t8 * 514 + 512]))
                        v.wait(pe, rec[("av", t8, 1)])
                        dv.inc(v.tensor_mul(tmpb[:, :], PS[:, AVCOL[(t8 * 3 + 1) % 2]:AVCOL[(t8 * 3 + 1) % 2] + 512],
                                            vc[:, t8 * 514 + 1:t8 * 514 + 513]))
                        dv.inc(v.tensor_add(accS[:, :], accS[:, :], tmpb[:, :]))
                        v.wait(pe, rec[("av", t8, 2)])
                        if t8 == 0:
                            v.wait(pe, pol)
                        dv.inc(v.tensor_mul(tmpb[:, :], PS[:, AVCOL[(t8 * 3 + 2) % 2]:AVCOL[(t8 * 3 + 2) % 2] + 512],
                                            vc[:, t8 * 514 + 2:t8 * 514 + 514]))
                        dv.inc(v.tensor_add(attnoT[:, t8 * 512:(t8 + 1) * 512], accS[:, :], tmpb[:, :]))
                b.on("vector", v_body)

            if ai is not None:
                def g_body(g, ai=ai, tdv_i=tdv_i):
                    for st in range(4):
                        g.wait(dv, tdv_i[("res", st)])
                        gt = ai * 4 + st
                        rec_store[(ai, st)] = ss.inc(
                            g.dma_start(out=out[gt * 128:(gt + 1) * 128, :], in_=x2t[st][:, :]), 16)
                b.on("gpsimd", g_body)

            if ci is not None:
                exp_prev = tsc_i
            tdv_prev = tdv_i
            tsc_prev = tsc_i
            assert sc.val == sc_end, (i, sc.val, sc_end)
            assert dv.val == dv_end, (i, dv.val, dv_end)

        def fin(s):
            s.wait(ss)
            s.wait(sl)
            s.wait(sg)
            s.wait(sq)
        b.on("sync", fin)
    return nc



def build_launch1(nc):
    with ExitStack() as octx:
        block = octx.enter_context(nc.Block())
        b = B(nc, octx, block)
        env = {"block": block, "b": b}
        x1ap = nc.dram_tensor("x1", [S, D], F32, kind="ExternalOutput").ap()
        build_attn0(nc, dict(x="x", wq="wq", wk="wk", wv="wv", wo="wo", out=x1ap), env=env)
        b.barrier()
        build_moe(nc, dict(x=x1ap, gate_w="gate_w", lb_bias="lb_bias", out="part"),
                  "partials", env=env)
        b.barrier()
    return nc


def build_launch2(nc):
    with ExitStack() as octx:
        block = octx.enter_context(nc.Block())
        b = B(nc, octx, block)
        env = {"block": block, "b": b}
        x2ap = nc.dram_tensor("x2i", [S, D], F32, kind="Internal").ap()
        x3ap = nc.dram_tensor("x3i", [S, D], F32, kind="Internal").ap()
        build_moe(nc, dict(x="x1", gate_w="gate_w", lb_bias="lb_bias", egw="egw",
                           euw="euw", edw="edw", sgw="sgw", suw="suw", sdw="sdw",
                           Sstack="Sstack", Cstack="Cstack", out=x2ap), "full", env=env)
        b.barrier()
        namesD = dict(x=x2ap, out=x3ap)
        for w in "pqkvo":
            namesD["W" + w] = "W" + w
            namesD["b" + w] = "b" + w
        build_attn1(nc, namesD, env=env)
        b.barrier()
        build_ffn1(nc, dict(x=x3ap, k0="k0", k1="k1", k2="k2", s0="s0", s1="s1",
                            s2="s2", out="out"), env=env)
        b.barrier()
    return nc



NCORES = 8


def kernel(**inputs):
    ii = {k: np.ascontiguousarray(v) for k, v in inputs.items()}
    x = ii["L0_attn_in_features"]
    cores = list(range(NCORES))
    lbr = ii["L0_ffn_lb_bias"].reshape(1, 8)

    nc1 = bass.Bass("TRN2", target_bir_lowering=False, debug=False)
    build_launch1(nc1)
    maps1 = [dict(x=x[i], wq=ii["L0_attn_q_proj_weight"], wk=ii["L0_attn_k_proj_weight"],
                  wv=ii["L0_attn_v_proj_weight"], wo=ii["L0_attn_o_proj_weight"],
                  gate_w=ii["L0_ffn_gate_w"], lb_bias=lbr) for i in cores]
    res1 = run_bass_kernel_spmd(nc1, maps1, core_ids=cores)
    x1 = [res1.results[i]["x1"] for i in cores]
    Sg = np.zeros((8, 8), np.float32)
    Cg = np.zeros((8, 8), np.float32)
    for i in cores:
        p = res1.results[i]["part"][:, 0]
        Sg[:, i] = p[0:8]
        Cg[:, i] = p[8:16]

    s0r = np.asarray(ii["L1_ffn_s0"], np.float32).reshape(1, 1)
    s1r = np.asarray(ii["L1_ffn_s1"], np.float32).reshape(1, 1)
    s2r = np.asarray(ii["L1_ffn_s2"], np.float32).reshape(1, 1)
    nc2 = bass.Bass("TRN2", target_bir_lowering=False, debug=False)
    build_launch2(nc2)
    maps2 = []
    for i in cores:
        m = dict(x1=x1[i], gate_w=ii["L0_ffn_gate_w"], lb_bias=lbr,
                 egw=ii["L0_ffn_expert_gate_w"], euw=ii["L0_ffn_expert_up_w"],
                 edw=ii["L0_ffn_expert_down_w"], sgw=ii["L0_ffn_shared_gate_w"],
                 suw=ii["L0_ffn_shared_up_w"], sdw=ii["L0_ffn_shared_down_w"],
                 Sstack=Sg, Cstack=Cg,
                 k0=ii["L1_ffn_k0"].view(np.uint8), k1=ii["L1_ffn_k1"].view(np.uint8),
                 k2=ii["L1_ffn_k2"].view(np.uint8), s0=s0r, s1=s1r, s2=s2r)
        for w in "pqkvo":
            m["W" + w] = ii["L1_attn_W" + w]
            m["b" + w] = ii["L1_attn_b" + w].reshape(1, D)
        maps2.append(m)
    res2 = run_bass_kernel_spmd(nc2, maps2, core_ids=cores)
    return np.stack([res2.results[i]["out"] for i in cores]).astype(np.float32)
